# version 05
# baseline (speedup 1.0000x reference)
import numpy as np
import ml_dtypes
from contextlib import ExitStack

import concourse.bass as bass
import concourse.mybir as mybir
from concourse.bass_utils import run_bass_kernel_spmd

F32 = mybir.dt.float32
BF16 = mybir.dt.bfloat16
AF = mybir.ActivationFunctionType
ALU = mybir.AluOpType
BF = ml_dtypes.bfloat16

MAGIC = float(np.float32(1.5 * 2 ** 23))
B_PC = 2
T = 2048
D = 1024
TOK = B_PC * T
NT = TOK // 128
NTB = T // 128
H = 16
DH = 64
I0 = 3072
I1 = 4096
EPS = 1e-5
VW = 65
VEW = H * VW + 63
PCOLS = 100288

ENGINES = ("sync", "tensor", "vector", "scalar", "gpsimd")

N_PHASES = 4


class Op:
    __slots__ = ("eng", "emit", "post", "idx", "ord", "is_dma", "deps", "tag",
                 "dsem", "dval", "dprev", "_demit", "is_coll")

    def __init__(self, eng, emit, idx, tag):
        self.eng = eng
        self.emit = emit
        self.idx = idx
        self.post = None
        self.ord = 0
        self.is_dma = False
        self.deps = set()
        self.tag = tag
        self.dsem = None
        self.dval = 0
        self.dprev = 0
        self._demit = 0
        self.is_coll = False


NDPOOL = 40


class Sched:
    def __init__(self, nc, ctx):
        self.nc = nc
        self.ops = []
        self.sems = {e: ctx.enter_context(nc.semaphore("prog_" + e)) for e in ENGINES}
        self.cnt = {e: 0 for e in ENGINES}
        self._cur = None
        self.segs = {}
        self.dpool = []
        self.dpool_of = {}
        for e in ("sync", "gpsimd", "scalar", "vector", "tensor"):
            npool = {"sync": 20, "gpsimd": 14}.get(e, 2)
            base = len(self.dpool)
            self.dpool += [ctx.enter_context(nc.semaphore(f"dma_{e}{i}")) for i in range(npool)]
            self.dpool_of[e] = (base, npool)
        self.dcnt = [0] * len(self.dpool)
        self.dhist = [[] for _ in self.dpool]
        self.dseq = {e: 0 for e in ENGINES}
        self.ordn = {e: 0 for e in ENGINES}

    def _touch(self, o, ranges, is_write):
        for (sp, lo, hi) in ranges:
            lst = self.segs.setdefault(sp, [])
            if is_write:
                keep = []
                for seg in lst:
                    if seg[0] < hi and lo < seg[1]:
                        if seg[2] is not None:
                            o.deps.add(seg[2])
                        for r in seg[3]:
                            o.deps.add(r)
                        if seg[0] < lo:
                            keep.append([seg[0], min(seg[1], lo), seg[2], list(seg[3])])
                        if seg[1] > hi:
                            keep.append([max(seg[0], hi), seg[1], seg[2], list(seg[3])])
                    else:
                        keep.append(seg)
                keep.append([lo, hi, o, []])
                self.segs[sp] = keep
            else:
                for seg in lst:
                    if seg[0] < hi and lo < seg[1]:
                        if seg[2] is not None:
                            o.deps.add(seg[2])
                        seg[3].append(o)

    def op(self, eng, emit, reads=(), writes=(), tag="", is_dma_op=False, ndma=1,
           is_coll=False):
        o = Op(eng, emit, len(self.ops), tag)
        o.ord = self.ordn[eng]
        self.ordn[eng] += 1
        o.is_coll = is_coll
        if is_dma_op:
            o.is_dma = True
            base, npool = self.dpool_of[eng]
            o.dsem = base + self.dseq[eng] % npool
            self.dseq[eng] += 1
            sidx = o.dsem
            o.dprev = self.dcnt[sidx]
            self.dcnt[sidx] += 16 * ndma
            o.dval = self.dcnt[sidx]
            self.dhist[sidx].append((o.idx, o.dval))
        self._touch(o, reads, False)
        self._touch(o, writes, True)
        o.deps.discard(o)
        for d in o.deps:
            assert d.idx < o.idx, (d.idx, o.idx, tag, d.tag)
        self.ops.append(o)
        return o

    def emit_all(self, block):
        import bisect
        watched_ord = {e: {f: -1 for f in ENGINES} for e in ENGINES}
        dwatched = {e: [0] * len(self.dpool) for e in ENGINES}
        fixups = []

        by_eng = {e: [] for e in ENGINES}
        for o in self.ops:
            by_eng[o.eng].append(o)

        def mk_emit(_run, _eng):
            def emit_run(e):
                for o in _run:
                    for d in sorted(o.deps, key=lambda d: d.idx):
                        if d.is_dma:
                            hist = self.dhist[d.dsem]
                            k = bisect.bisect_left(hist, (o.idx, -1)) - 1
                            val = hist[k][1]
                            if val > dwatched[_eng][d.dsem]:
                                e.wait_ge(self.dpool[d.dsem], val)
                                dwatched[_eng][d.dsem] = val
                        elif d.eng != _eng or d.is_coll or _eng != "tensor":
                            if d.ord > watched_ord[_eng][d.eng]:
                                if d.post is not None:
                                    e.wait_ge(self.sems[d.eng], d.post)
                                else:
                                    fixups.append(
                                        (e.wait_ge(self.sems[d.eng], 1), d))
                                watched_ord[_eng][d.eng] = d.ord
                    if o.is_dma and o.dprev > 0 and o.dprev > dwatched[_eng][o.dsem]:
                        e.wait_ge(self.dpool[o.dsem], o.dprev)
                        dwatched[_eng][o.dsem] = o.dprev
                    self._cur = o
                    o.emit(e)
                    self._cur = None
                    o.post = self.cnt[_eng]
            return emit_run

        for eng in ENGINES:
            if by_eng[eng]:
                getattr(block, eng)(mk_emit(by_eng[eng], eng))

        for inst, d in fixups:
            assert d.post is not None, d.tag
            ws = inst.ins.sync_info.on_wait
            assert len(ws) == 1, (d.tag, ws)
            ws[0].wait_value = d.post

    def tick(self, inst, n=1):
        o = self._cur
        self.cnt[o.eng] += n
        inst.then_inc(self.sems[o.eng], n)
        return inst

    def dma(self, inst):
        o = self._cur
        assert o.is_dma, f"op {o.tag} must be created with is_dma_op=True"
        o._demit += 16
        assert o.dprev + o._demit <= o.dval, f"too many dma_starts in {o.tag}"
        inst.then_inc(self.dpool[o.dsem], 16)
        return inst


class NS:
    def __init__(self, d):
        self.__dict__.update(d)


def build_nc(n_phases=4):
    nc = bass.Bass("TRN2", target_bir_lowering=False)
    ap = {}

    def DI(name, shape, dt=F32):
        ap[name] = nc.dram_tensor(name, shape, dt, kind="ExternalInput").ap()

    DI("x_in", [B_PC, T, D]); DI("wq", [D, H * DH]); DI("wkv", [D, 2 * H * DH])
    DI("wout", [H * DH, D]); DI("bout", [1, D])
    DI("wg0", [I0, D]); DI("wu0", [I0, D]); DI("wd0", [D, I0])
    DI("w1", [D, D]); DI("b1", [1, D]); DI("w2", [1, D]); DI("b2", [1, 1])
    DI("wg1", [I1, D]); DI("wu1", [I1, D]); DI("wd1", [D, I1])
    DI("id_bf", [128, 128], BF16); DI("id_f", [128, 128], F32)
    DI("ones_bf", [128, 128], BF16); DI("ut128", [128, 128], BF16)
    DI("t16c", [16, 16], BF16); DI("ind16", [128, 256], BF16)
    DI("ones_f", [1, 128], F32); DI("selm16", [16, 2048], BF16)
    ap["out"] = nc.dram_tensor("out", [B_PC, T, D], F32, kind="ExternalOutput").ap()

    def DS(name, shape, dt):
        ap[name] = nc.dram_tensor(name, shape, dt).ap()

    DS("xres", [TOK, D], F32)
    DS("wq_b", [D, H * DH], BF16); DS("wkv_b", [D, 2 * H * DH], BF16)
    DS("wout_b", [H * DH, D], BF16); DS("bout_b", [1, D], BF16)
    DS("w1_b", [D, D], BF16)
    DS("wg0_b", [I0, D], BF16); DS("wu0_b", [I0, D], BF16); DS("wd0_b", [D, I0], BF16)
    DS("wg1_b", [I1, D], BF16); DS("wu1_b", [I1, D], BF16); DS("wd1_b", [D, I1], BF16)
    DS("up_sc", [I0 // 128, 128, TOK], BF16)
    DS("gu_sc", [NT, 128, I1], BF16)
    DS("den_hb", [64, 512], BF16)
    DS("rden_hb", [256, 128], BF16)
    DS("gate_sc", [I0 // 128, 128, TOK], BF16)
    for i in range(1, 6):
        DS(f"cci{i}", [1, 16], F32)
        DS(f"cco{i}", [1, 16], F32)

    with ExitStack() as ctx0:
        sch = Sched(nc, ctx0)
        S = lambda n, sh, dt: ctx0.enter_context(nc.sbuf_tensor(n, sh, dt))
        P = lambda n, sh, dt: ctx0.enter_context(nc.psum_tensor(n, sh, dt))

        sb = {}
        for n, shdt in [("id", ([128, 128], BF16)), ("idf", ([128, 128], F32)),
                        ("ones", ([128, 128], BF16)), ("ut", ([128, 128], BF16)),
                        ("t16", ([16, 16], BF16)), ("ind", ([128, 256], BF16)),
                        ("colm", ([128, NT], F32)), ("colr", ([128, NT], F32)),
                        ("cols", ([128, 64], F32)), ("colx", ([128, 192], F32)),
                        ("glob", ([128, 16], F32)), ("cbias", ([128, 8], F32)),
                        ("dcol", ([128, 256], F32)), ("dcolb", ([128, 256], BF16)),
                        ("drows", ([128, 256], BF16)), ("dstage", ([128, 1024], BF16)),
                        ("tiny", ([128, 32], F32)), ("tcc", ([1, 16], F32)),
                        ("tcc2", ([128, 16], F32)), ("onesf", ([1, 128], F32)),
                        ("selm", ([16, 2048], BF16)), ("b2sb", ([1, 2], F32))]:
            sb[n] = S("sb_" + n, *shdt)

        pool = S("pool", [128, PCOLS], BF16)
        poolf = pool[:, :].bitcast(F32)

        def PB(c0, n):
            return pool[:, c0:c0 + n]

        def PBr(r0, r1, c0, n):
            return pool[r0:r1, c0:c0 + n]

        def PF(c0, n):
            return poolf[:, c0 // 2: c0 // 2 + n]

        ps_sc = P("ps_sc", [128, 2048], F32)
        ps_cx = P("ps_cx", [128, 512], F32)
        ps_dm = P("ps_dm", [128, 512], F32)
        ps_op = P("ps_op", [128, 1024], F32)
        pt_op = ps_op[:, :].bitcast(BF16)

        block = ctx0.enter_context(nc.Block())
        env = NS(dict(nc=nc, ap=ap, sb=sb, pool=pool, poolf=poolf, PB=PB,
                      PBr=PBr, PF=PF, ps_sc=ps_sc, ps_cx=ps_cx, ps_dm=ps_dm,
                      ps_op=ps_op, pt_op=pt_op, sch=sch))
        build_program(env, n_phases)
        sch.emit_all(block)
    return nc


def build_program(g, n_phases):
    sch = g.sch
    ap, sb = g.ap, g.sb
    PB, PBr, PF = g.PB, g.PBr, g.PF
    ps_sc, ps_cx, ps_dm, ps_op, pt_op = g.ps_sc, g.ps_cx, g.ps_dm, g.ps_op, g.pt_op
    op, tick, dma = sch.op, sch.tick, sch.dma

    def pr(c0, n):
        return ("pool", c0, c0 + n)

    def prf(c0, n):
        return ("pool", c0, c0 + 2 * n)

    def rr(name, lo=0, hi=1 << 30):
        return (name, lo, hi)

    PSC = lambda lo, n: ("ps_sc", lo, lo + n)
    PCX = lambda: ("ps_cx", 0, 512)
    PDM = lambda: ("ps_dm", 0, 512)
    POP = lambda lo, n: ("ps_op", lo, lo + n)

    _nheat = [0]

    def heat(n, after=(), reads=()):
        def eh(e):
            mm = None
            for _ in range(n):
                mm = e.matmul(out=g.ps_cx[:, 0:512], lhsT=sb["idf"][:, :],
                              rhs=sb["idf"][:, 0:1].broadcast_to([128, 512]),
                              start=True, stop=True)
            tick(mm)
        o = op("tensor", eh, reads=(rr("c_idf"),) + tuple(reads),
               writes=(PCX(),), tag=f"heat{_nheat[0]}")
        _nheat[0] += 1
        for a in (after if isinstance(after, (list, tuple)) else (after,)):
            if a is not None and a.idx < o.idx:
                o.deps.add(a)
        return o

    for dst, src in [("id", "id_bf"), ("idf", "id_f"), ("ones", "ones_bf"),
                     ("ut", "ut128"), ("t16", "t16c"), ("ind", "ind16"),
                     ("onesf", "ones_f"), ("selm", "selm16")]:
        op("sync", is_dma_op=True, emit=lambda e, d=dst, s=src: dma(e.dma_start(out=sb[d][:, :], in_=ap[s])),
           reads=(rr(src),), writes=(rr("c_" + dst),), tag="const")
    for dst, src in [("wq_b", "wq"), ("wkv_b", "wkv"), ("wout_b", "wout"),
                     ("bout_b", "bout"), ("w1_b", "w1")]:
        op("gpsimd", is_dma_op=True, emit=lambda e, d=dst, s=src: dma(e.dma_start(out=ap[d], in_=ap[s])),
           reads=(rr(src),), writes=(rr(dst),), tag="cast")
    op("vector", lambda e: (e.memset(sb["cbias"][:, 0:1], -12.0),
                            e.memset(sb["cbias"][:, 1:2], 0.0),
                            e.memset(sb["cbias"][:, 2:3], MAGIC),
                            tick(e.memset(sb["cbias"][:, 3:4], -MAGIC))),
       writes=(rr("cbias", 0, 4),), tag="cbias")
    op("vector", lambda e: tick(e.memset(sb["tcc"][0:1, :], 0.0)),
       writes=(rr("tcc", 0, 16),), tag="tccz")

    XA = 0
    LN = 4096
    QP = 20480
    KF = 53248
    VE = 69632
    EB = 87296
    WS = 89344
    LB = 93440
    CX = 0
    WO = 53248
    XD = 69632
    BO = 87296

    def l0_attn():
        for b in range(B_PC):
            op("vector", lambda e: tick(e.memset(PB(QP, 32768), 0.0)),
               writes=(pr(QP, 32768),), tag=f"zq{b}")

            def vext_fill(e):
                e.tensor_copy(
                    out=PB(VE, NTB * VEW).rearrange("p (kt w) -> p kt w", w=VEW)[:, :, H * VW - 1:],
                    in_=sb["ones"][:, 0:1].unsqueeze(2).broadcast_to([128, NTB, 64]))
                e.drain()
                vv = PB(VE, NTB * VEW).rearrange("p (kt w) -> p kt w", w=VEW)[:, :, 0:H * VW]
                vv = vv.rearrange("p kt (hh w) -> p kt hh w", w=VW)
                tick(e.tensor_copy(out=vv[:, :, :, 64:65],
                                   in_=sb["ones"][:, 0:1].unsqueeze(2).unsqueeze(3).broadcast_to([128, NTB, H, 1])))
            op("vector", vext_fill, reads=(rr("c_ones"),),
               writes=(pr(VE, NTB * VEW),), tag=f"vones{b}")

            for t in range(NTB):
                ti = b * NTB + t
                sl2 = (t % 2) * 2048
                op("sync", is_dma_op=True, emit=lambda e, b=b, t=t, sl2=sl2: dma(e.dma_start(
                    out=PF(XA + sl2, 1024),
                    in_=ap["x_in"][b, t * 128:(t + 1) * 128, :])),
                   reads=(rr("x_in"),), writes=(prf(XA + sl2, 1024),), tag=f"xa{ti}")

                def stats(e, ti=ti, t=t, sl2=sl2):
                    e.bn_stats(out=sb["tiny"][:, 0:6], in_=PF(XA + sl2, 512))
                    e.bn_stats(out=sb["tiny"][:, 6:12], in_=PF(XA + sl2 + 1024, 512))
                    e.drain()
                    e.bn_aggr(out=sb["tiny"][:, 12:14], in_=sb["tiny"][:, 0:12])
                    e.drain()
                    e.tensor_copy(out=sb["colm"][:, ti:ti + 1], in_=sb["tiny"][:, 12:13])
                    tick(e.tensor_scalar(out=sb["colx"][:, 160 + t:161 + t],
                                         in0=sb["tiny"][:, 13:14],
                                         scalar1=EPS, scalar2=None, op0=ALU.add))
                op("vector", stats, reads=(prf(XA + sl2, 1024),),
                   writes=(rr("tiny", 0, 15), rr("colm", ti, ti + 1),
                           rr("colx", 160 + t, 161 + t)), tag=f"st{ti}")
            op("scalar", lambda e, b=b: (
                e.activation(out=sb["colx"][:, 160:160 + NTB], in_=sb["colx"][:, 160:160 + NTB], func=AF.Ln),
                e.drain(),
                tick(e.activation(out=sb["colr"][:, b * NTB:(b + 1) * NTB],
                                  in_=sb["colx"][:, 160:160 + NTB], func=AF.Exp, scale=-0.5))),
               reads=(rr("colx", 160, 160 + NTB),),
               writes=(rr("colx", 160, 160 + NTB), rr("colr", b * NTB, (b + 1) * NTB)),
               tag=f"rsb{b}")
            for t in range(NTB):
                ti = b * NTB + t
                sl2 = (t % 2) * 2048
                op("sync", is_dma_op=True, emit=lambda e, b=b, t=t, sl2=sl2: dma(e.dma_start(
                    out=PF(XA + sl2, 1024),
                    in_=ap["x_in"][b, t * 128:(t + 1) * 128, :])),
                   reads=(rr("x_in"),), writes=(prf(XA + sl2, 1024),), tag=f"xb{ti}")

                def lnb(e, ti=ti, sl2=sl2):
                    tick(e.tensor_scalar(out=PB(LB + (ti % 2) * 1024, 1024),
                                         in0=PF(XA + sl2, 1024),
                                         scalar1=sb["colm"][:, ti:ti + 1],
                                         scalar2=sb["colr"][:, ti:ti + 1],
                                         op0=ALU.subtract, op1=ALU.mult))
                op("vector", lnb,
                   reads=(prf(XA + sl2, 1024), rr("colm", ti, ti + 1), rr("colr", ti, ti + 1)),
                   writes=(pr(LB + (ti % 2) * 1024, 1024),), tag=f"lnb{ti}")

                def lnT(e, ti=ti):
                    sl = (ti % 2) * 1024
                    for dc in range(8):
                        i = e.transpose(out=pt_op[:, sl + dc * 128: sl + (dc + 1) * 128],
                                        in_=PB(LB + (ti % 2) * 1024 + dc * 128, 128),
                                        identity=sb["id"][:, :])
                    tick(i)
                op("tensor", lnT,
                   reads=(pr(LB + (ti % 2) * 1024, 1024), rr("c_id")),
                   writes=(POP((ti % 2) * 512, 512),), tag=f"lnT{ti}")

                def lnE(e, ti=ti, t=t):
                    sl = (ti % 2) * 1024
                    tick(e.tensor_copy(
                        out=PB(LN, 16384).rearrange("p (dc n) -> p dc n", n=2048)[:, :, t * 128:(t + 1) * 128],
                        in_=pt_op[:, sl:sl + 1024].rearrange("p (dc n) -> p dc n", n=128)))
                op("vector", lnE, reads=(POP((ti % 2) * 512, 512),),
                   writes=(POP((ti % 2) * 512, 512),) + tuple(
                       pr(LN + dc * 2048 + t * 128, 128) for dc in range(8)),
                   tag=f"lnE{ti}")

            NQG, NKG = H, H // 2
            ev = 0
            for li in range(NQG + NKG + 2):
                wsl = WS + (li % 2) * 2048
                if li < NQG:
                    h = li
                    op("sync", is_dma_op=True, emit=lambda e, h=h, wsl=wsl: dma(e.dma_start(
                        out=PB(wsl, 512).rearrange("p (dc n) -> p dc n", n=64),
                        in_=ap["wq_b"][:, h * 64:(h + 1) * 64].rearrange("(dc p) n -> p dc n", p=128))),
                       reads=(rr("wq_b"),), writes=(pr(wsl, 512),), tag=f"wq{b}_{h}")
                    for tc in range(4):
                        sl = ev % 4
                        def mmq(e, h=h, tc=tc, sl=sl, wsl=wsl):
                            for dc in range(8):
                                i = e.matmul(
                                    out=ps_sc[(h % 2) * 64:(h % 2) * 64 + 64, sl * 512:(sl + 1) * 512],
                                    lhsT=PB(wsl + dc * 64, 64),
                                    rhs=PB(LN + dc * 2048 + tc * 512, 512),
                                    start=(dc == 0), stop=(dc == 7),
                                    tile_position=(0, (h % 2) * 64))
                            tick(i)
                        op("tensor", mmq, reads=(pr(wsl, 512), pr(LN, 16384)),
                           writes=(PSC(sl * 512, 512),), tag=f"mmq{b}_{h}_{tc}")
                        def evq(e, h=h, tc=tc, sl=sl):
                            tick(e.tensor_copy(
                                out=PBr((h % 2) * 64, (h % 2) * 64 + 64,
                                        QP + h * 2048 + tc * 512, 512),
                                in_=ps_sc[(h % 2) * 64:(h % 2) * 64 + 64, sl * 512:(sl + 1) * 512]))
                        op("vector", evq,
                           reads=(PSC(sl * 512, 512), pr(QP + h * 2048 + tc * 512, 512)),
                           writes=(PSC(sl * 512, 512), pr(QP + h * 2048 + tc * 512, 512)),
                           tag=f"evq{b}_{h}_{tc}")
                        ev += 1
                elif li < NQG + NKG:
                    hp = li - NQG
                    op("sync", is_dma_op=True, emit=lambda e, hp=hp, wsl=wsl: dma(e.dma_start(
                        out=PB(wsl, 1024).rearrange("p (dc n) -> p dc n", n=128),
                        in_=ap["wkv_b"][:, hp * 128:(hp + 1) * 128].rearrange("(dc p) n -> p dc n", p=128))),
                       reads=(rr("wkv_b"),), writes=(pr(wsl, 1024),), tag=f"wk{b}_{hp}")
                    for tc in range(4):
                        sl = ev % 4
                        def mmk(e, tc=tc, sl=sl, wsl=wsl):
                            for dc in range(8):
                                i = e.matmul(
                                    out=ps_sc[:, sl * 512:(sl + 1) * 512],
                                    lhsT=PB(wsl + dc * 128, 128),
                                    rhs=PB(LN + dc * 2048 + tc * 512, 512),
                                    start=(dc == 0), stop=(dc == 7))
                            tick(i)
                        op("tensor", mmk, reads=(pr(wsl, 1024), pr(LN, 16384)),
                           writes=(PSC(sl * 512, 512),), tag=f"mmk{b}_{hp}_{tc}")
                        def evk(e, hp=hp, tc=tc, sl=sl):
                            tick(e.tensor_copy(
                                out=PB(KF + hp * 2048 + tc * 512, 512),
                                in_=ps_sc[:, sl * 512:(sl + 1) * 512]))
                        op("vector", evk, reads=(PSC(sl * 512, 512),),
                           writes=(PSC(sl * 512, 512), pr(KF + hp * 2048 + tc * 512, 512)),
                           tag=f"evk{b}_{hp}_{tc}")
                        ev += 1
                else:
                    vc = li - NQG - NKG
                    op("sync", is_dma_op=True, emit=lambda e, vc=vc: dma(e.dma_start(
                        out=PB(WS, 4096).rearrange("p (dc n) -> p dc n", n=512),
                        in_=ap["wkv_b"][:, 1024 + vc * 512:1024 + (vc + 1) * 512].rearrange("(dc p) n -> p dc n", p=128))),
                       reads=(rr("wkv_b"),), writes=(pr(WS, 4096),), tag=f"wv{b}_{vc}")
                    for kt in range(NTB):
                        sl = ev % 4
                        def mmv(e, kt=kt, sl=sl):
                            for dc in range(8):
                                i = e.matmul(
                                    out=ps_sc[:, sl * 512:(sl + 1) * 512],
                                    lhsT=PB(LN + dc * 2048 + kt * 128, 128),
                                    rhs=PB(WS + dc * 512, 512),
                                    start=(dc == 0), stop=(dc == 7))
                            tick(i)
                        op("tensor", mmv, reads=(pr(WS, 4096), pr(LN, 16384)),
                           writes=(PSC(sl * 512, 512),), tag=f"mmv{b}_{vc}_{kt}")
                        def evv(e, kt=kt, vc=vc, sl=sl):
                            vv = PB(VE + kt * VEW + vc * 8 * VW, 8 * VW)
                            vv = vv.rearrange("p (hh w) -> p hh w", w=VW)[:, :, 0:64]
                            tick(e.tensor_copy(
                                out=vv,
                                in_=ps_sc[:, sl * 512:(sl + 1) * 512].rearrange("p (hh n) -> p hh n", n=64)))
                        op("vector", evv, reads=(PSC(sl * 512, 512),),
                           writes=(PSC(sl * 512, 512),
                                   pr(VE + kt * VEW + vc * 8 * VW, 8 * VW)),
                           tag=f"evv{b}_{vc}_{kt}")
                        ev += 1

            for h in range(H):
                hp = h // 2
                for qc in range(4):
                    qi = (b * H + h) * 4 + qc
                    cxb = ps_cx if qi % 2 == 0 else ps_dm
                    cxr = PCX() if qi % 2 == 0 else PDM()
                    def mk_qk(p):
                        gs = p % 2
                        def qk(e, hp=hp, h=h, qc=qc, p=p, gs=gs):
                            for j in range(2):
                                kt = 2 * p + j
                                i = e.matmul(
                                    out=ps_sc[:, gs * 1024 + j * 512: gs * 1024 + (j + 1) * 512],
                                    lhsT=PB(KF + hp * 2048 + kt * 128, 128),
                                    rhs=PB(QP + h * 2048 + qc * 512, 512),
                                    start=True, stop=True)
                            tick(i)
                        op("tensor", qk,
                           reads=(pr(KF + hp * 2048, 2048), pr(QP + h * 2048 + qc * 512, 512)),
                           writes=(PSC(gs * 1024, 1024),), tag=f"qk{qi}_{p}")
                        def ex(e, gs=gs):
                            tick(e.activation(out=PB(EB + gs * 1024, 1024),
                                              in_=ps_sc[:, gs * 1024:(gs + 1) * 1024],
                                              func=AF.Exp, scale=0.125,
                                              bias=sb["cbias"][:, 0:1]))
                        op("scalar", ex, reads=(PSC(gs * 1024, 1024), rr("cbias", 0, 1)),
                           writes=(PSC(gs * 1024, 1024), pr(EB + gs * 1024, 1024)),
                           tag=f"ex{qi}_{p}")

                    def mk_av(p):
                        gs = p % 2
                        def av(e, h=h, p=p, gs=gs, cxb=cxb):
                            for j in range(2):
                                kt = 2 * p + j
                                i = e.matmul(
                                    out=cxb[:, 0:512],
                                    lhsT=PB(VE + kt * VEW + h * VW, 128),
                                    rhs=PB(EB + gs * 1024 + j * 512, 512),
                                    start=(p == 0 and j == 0), stop=(p == 7 and j == 1))
                            tick(i)
                        op("tensor", av,
                           reads=(pr(EB + gs * 1024, 1024), pr(VE, NTB * VEW)),
                           writes=(cxr,), tag=f"av{qi}_{p}")

                    mk_qk(0)
                    for p in range(1, 8):
                        mk_qk(p)
                        mk_av(p - 1)
                    mk_av(7)
                    def cxe(e, h=h, qc=qc, qi=qi, cxb=cxb):
                        e.tensor_copy(
                            out=PBr(0, 64, CX + h * 2048 + qc * 512, 512),
                            in_=cxb[0:64, 0:512])
                        e.drain()
                        tick(e.tensor_copy(
                            out=sb["dstage"][64:65, (qi % 2) * 512:(qi % 2) * 512 + 512],
                            in_=cxb[64:65, 0:512]))
                    op("vector", cxe, reads=(cxr,),
                       writes=(cxr, pr(CX + h * 2048 + qc * 512, 512),
                               rr("dstage", (qi % 2) * 512, (qi % 2) * 512 + 512)),
                       tag=f"cxe{qi}")
                    op("gpsimd", is_dma_op=True, emit=lambda e, h=h, qc=qc, qi=qi: dma(e.dma_start(
                        out=ap["den_hb"][h * 4 + qc: h * 4 + qc + 1, :],
                        in_=sb["dstage"][64:65, (qi % 2) * 512:(qi % 2) * 512 + 512])),
                       reads=(rr("dstage", (qi % 2) * 512, (qi % 2) * 512 + 512),),
                       writes=(rr("den_hb", h * 4 + qc, h * 4 + qc + 1),), tag=f"dh{qi}")

            for s in range(4):
                op("sync", is_dma_op=True, emit=lambda e, s=s: dma(e.dma_start_transpose(
                    out=sb["dcolb"][:, s * 64:(s + 1) * 64],
                    in_=ap["den_hb"][0:64, s * 128:(s + 1) * 128])),
                   reads=(rr("den_hb"),), writes=(rr("dcolb", s * 64, (s + 1) * 64),),
                   tag=f"dt{b}_{s}")

            def recip(e):
                e.reciprocal(out=sb["dcol"][:, :], in_=sb["dcolb"][:, :])
                e.drain()
                tick(e.tensor_copy(out=sb["dcolb"][:, :], in_=sb["dcol"][:, :]))
            op("vector", recip, reads=(rr("dcolb"),),
               writes=(rr("dcolb"), rr("dcol")), tag=f"recip{b}")

            for Bb in range(2):
                op("sync", is_dma_op=True, emit=lambda e, Bb=Bb: dma(e.dma_start_transpose(
                    out=sb["drows"][:, Bb * 128:(Bb + 1) * 128],
                    in_=sb["dcolb"][:, Bb * 128:(Bb + 1) * 128])),
                   reads=(rr("dcolb", Bb * 128, (Bb + 1) * 128),),
                   writes=(rr("drows", Bb * 128, (Bb + 1) * 128),), tag=f"dtr{b}_{Bb}")
            op("gpsimd", is_dma_op=True, emit=lambda e: dma(e.dma_start(
                out=ap["rden_hb"].rearrange("(bb r) q -> r bb q", bb=2),
                in_=sb["drows"][:, :].rearrange("r (bb q) -> r bb q", q=128))),
               reads=(rr("drows"),), writes=(rr("rden_hb"),), tag=f"rdw{b}")

            for h in range(H):
                for qc in range(4):
                    i4 = (b * H + h) * 4 + qc
                    dmb = ps_cx if i4 % 2 == 0 else ps_dm
                    dmr = PCX() if i4 % 2 == 0 else PDM()
                    hq = h * 4 + qc
                    rsl = (i4 % 2) * 512
                    op("sync", is_dma_op=True, emit=lambda e, hq=hq, rsl=rsl: dma(e.dma_start(
                        out=sb["dstage"][0:1, rsl:rsl + 512].rearrange("p (s n) -> p s n", n=128),
                        in_=ap["rden_hb"][hq::64, :].unsqueeze(0))),
                       reads=(rr("rden_hb"),),
                       writes=(rr("dstage", 1024 + rsl, 1024 + rsl + 512),), tag=f"rdg{i4}")
                    def dmm(e, rsl=rsl, dmb=dmb):
                        tick(e.matmul(out=dmb[0:64, 0:512],
                                      lhsT=sb["ones"][0:1, 0:64],
                                      rhs=sb["dstage"][0:1, rsl:rsl + 512],
                                      start=True, stop=True))
                    op("tensor", dmm,
                       reads=(rr("dstage", 1024 + rsl, 1024 + rsl + 512), rr("c_ones")),
                       writes=(dmr,), tag=f"dmm{i4}")
                    def nrm(e, h=h, qc=qc, dmb=dmb):
                        tick(e.tensor_tensor(
                            out=PBr(0, 64, CX + h * 2048 + qc * 512, 512),
                            in0=PBr(0, 64, CX + h * 2048 + qc * 512, 512),
                            in1=dmb[0:64, 0:512], op=ALU.mult))
                    op("vector", nrm,
                       reads=(dmr, pr(CX + h * 2048 + qc * 512, 512)),
                       writes=(dmr, pr(CX + h * 2048 + qc * 512, 512)), tag=f"nrm{i4}")

            for h in range(H):
                op("sync", is_dma_op=True, emit=lambda e, h=h: dma(e.dma_start(
                    out=PBr(0, 64, WO + h * 1024, 1024),
                    in_=ap["wout_b"][h * 64:(h + 1) * 64, :])),
                   reads=(rr("wout_b"),), writes=(pr(WO + h * 1024, 1024),),
                   tag=f"wo{b}_{h}")
            op("sync", is_dma_op=True, emit=lambda e: dma(e.dma_start(out=g.pool[0:1, BO:BO + 1024],
                                                 in_=ap["bout_b"])),
               reads=(rr("bout_b"),), writes=(pr(BO, 1024),), tag=f"bo{b}")
            for t in range(NTB):
                ti = b * NTB + t
                sl2 = (t % 2) * 2048
                op("sync", is_dma_op=True, emit=lambda e, b=b, t=t, sl2=sl2: dma(e.dma_start(
                    out=PF(XD + sl2, 1024),
                    in_=ap["x_in"][b, t * 128:(t + 1) * 128, :])),
                   reads=(rr("x_in"),), writes=(prf(XD + sl2, 1024),), tag=f"xd{ti}")
                def opj(e, t=t):
                    for dd in range(2):
                        for h in range(H):
                            i = e.matmul(out=ps_op[:, dd * 512:(dd + 1) * 512],
                                         lhsT=PBr(0, 64, CX + h * 2048 + t * 128, 128),
                                         rhs=PBr(0, 64, WO + h * 1024 + dd * 512, 512),
                                         start=(h == 0), stop=False)
                        i = e.matmul(out=ps_op[:, dd * 512:(dd + 1) * 512],
                                     lhsT=sb["ones"][0:1, 0:128],
                                     rhs=g.pool[0:1, BO + dd * 512: BO + (dd + 1) * 512],
                                     start=False, stop=True)
                    tick(i)
                op("tensor", opj,
                   reads=(pr(CX, 32768), pr(WO, 16384), pr(BO, 1024), rr("c_ones")),
                   writes=(POP(0, 1024),), tag=f"opj{ti}")
                def resid(e, sl2=sl2):
                    tick(e.scalar_tensor_tensor(out=PF(XD + sl2, 1024),
                                                in0=ps_op[:, 0:1024], scalar=1.0,
                                                in1=PF(XD + sl2, 1024),
                                                op0=ALU.mult, op1=ALU.add))
                op("vector", resid, reads=(POP(0, 1024), prf(XD + sl2, 1024)),
                   writes=(POP(0, 1024), prf(XD + sl2, 1024)), tag=f"res{ti}")
                op("gpsimd", is_dma_op=True, emit=lambda e, b=b, t=t, sl2=sl2: dma(e.dma_start(
                    out=ap["xres"][(b * T + t * 128):(b * T + (t + 1) * 128), :],
                    in_=PF(XD + sl2, 1024))),
                   reads=(prf(XD + sl2, 1024),),
                   writes=(rr("xres", b * T + t * 128, b * T + (t + 1) * 128),),
                   tag=f"xw{ti}")

    l0_attn()

    XF2 = 0
    YJ = 4096
    XQ = 12288
    WL = 45056
    WB2 = 49152
    GU = 53248
    EL = 61440
    WT = 69632
    EL2 = 12288
    OI = 49152
    DS8 = 53248
    NIC = I0 // 128

    def lpbq_prep(wname, bname, ntile, ncol):
        nct = ncol // 1024
        tiles = [(it, ct) for it in range(ntile) for ct in range(nct)]

        def lp_load(ti):
            it, ct = tiles[ti]
            sl = ti % 2
            src = ap[wname][it * 128:(it + 1) * 128, ct * 1024:(ct + 1) * 1024]
            op("sync", is_dma_op=True,
               emit=lambda e, src=src, sl=sl: dma(e.dma_start(out=PF(WL + sl * 2048, 1024), in_=src)),
               reads=(rr(wname),), writes=(prf(WL + sl * 2048, 1024),),
               tag=f"lq{wname}{ti}")

        def lp_step(ti, k):
            it, ct = tiles[ti]
            sl = ti % 2
            wv = PF(WL + sl * 2048, 1024)
            wvr = prf(WL + sl * 2048, 1024)
            R = sb["cols"] if sl == 0 else sb["dcol"]
            Rn = "cols" if sl == 0 else "dcol"
            ev = PF(EL + sl * 2048, 1024)
            evr = prf(EL + sl * 2048, 1024)
            if k == 0:
                return op("vector", lambda e, wv=wv, R=R: tick(e.tensor_reduce(
                    out=R[:, 0:32], in_=wv.rearrange("p (b k) -> p b k", k=32),
                    axis=mybir.AxisListType.X, op=ALU.max,
                    apply_absolute_value=True)),
                   reads=(wvr,), writes=(rr(Rn, 0, 32),), tag=f"dqa{wname}{ti}")
            elif k == 1:
                return op("vector", lambda e, R=R: tick(e.tensor_scalar(
                    out=R[:, 32:64], in0=R[:, 0:32], scalar1=1.0 / 7.0,
                    scalar2=1e-12, op0=ALU.mult, op1=ALU.max)),
                   reads=(rr(Rn, 0, 32),), writes=(rr(Rn, 32, 64),), tag=f"dqb{wname}{ti}")
            elif k == 2:
                return op("vector", lambda e, R=R: tick(e.reciprocal(
                    out=R[:, 0:32], in_=R[:, 32:64])),
                   reads=(rr(Rn, 32, 64),), writes=(rr(Rn, 0, 32),), tag=f"dqc{wname}{ti}")
            elif k == 3:
                return op("vector", lambda e, wv=wv, ev=ev, R=R: tick(e.tensor_tensor(
                    out=ev.rearrange("p (b k) -> p b k", k=32),
                    in0=wv.rearrange("p (b k) -> p b k", k=32),
                    in1=R[:, 0:32].unsqueeze(2).broadcast_to([128, 32, 32]),
                    op=ALU.mult)),
                   reads=(wvr, rr(Rn, 0, 32)), writes=(evr,), tag=f"dqd{wname}{ti}")
            elif k == 4:
                return op("vector", lambda e, ev=ev: tick(e.tensor_scalar(
                    out=ev, in0=ev, scalar1=MAGIC, scalar2=MAGIC,
                    op0=ALU.add, op1=ALU.subtract)),
                   reads=(evr,), writes=(evr,), tag=f"dqe{wname}{ti}")
            elif k == 5:
                return op("vector", lambda e, ev=ev: tick(e.tensor_scalar(
                    out=ev, in0=ev, scalar1=7.0, scalar2=-8.0,
                    op0=ALU.min, op1=ALU.max)),
                   reads=(evr,), writes=(evr,), tag=f"dqf{wname}{ti}")
            else:
                return op("vector", lambda e, ev=ev, R=R, sl=sl: tick(e.tensor_tensor(
                    out=PB(WB2 + sl * 1024, 1024).rearrange("p (b k) -> p b k", k=32),
                    in0=ev.rearrange("p (b k) -> p b k", k=32),
                    in1=R[:, 32:64].unsqueeze(2).broadcast_to([128, 32, 32]),
                    op=ALU.mult)),
                   reads=(evr, rr(Rn, 32, 64)),
                   writes=(pr(WB2 + sl * 1024, 1024),), tag=f"dqg{wname}{ti}")

        def lp_store(ti):
            it, ct = tiles[ti]
            sl = ti % 2
            dst = ap[bname][it * 128:(it + 1) * 128, ct * 1024:(ct + 1) * 1024]
            op("gpsimd", is_dma_op=True,
               emit=lambda e, dst=dst, sl=sl: dma(e.dma_start(out=dst, in_=PB(WB2 + sl * 1024, 1024))),
               reads=(pr(WB2 + sl * 1024, 1024),),
               writes=(rr(bname, ti, ti + 1),), tag=f"wqw{wname}{ti}")

        n = len(tiles)
        for p in range(0, n, 2):
            pts = [p] + ([p + 1] if p + 1 < n else [])
            for ti in pts:
                lp_load(ti)
            pace = []
            for k in range(7):
                for ti in pts:
                    o_ = lp_step(ti, k)
                if k in (3, 6):
                    pace.append(o_)
            for pa in pace:
                heat(6, after=pa)
            for ti in pts:
                lp_store(ti)

    def ground(gi, nsrc, cci, cco):
        def comb(e, nsrc=nsrc, gi=gi):
            e.scalar_tensor_tensor(out=sb["colx"][:, 128:128 + nsrc],
                                   in0=sb["colx"][:, 64:64 + nsrc], scalar=-1.0,
                                   in1=sb["colx"][:, 0:nsrc], op0=ALU.mult, op1=ALU.max)
            e.drain()
            tick(e.tensor_reduce(out=sb["tcc2"][:, gi:gi + 1],
                                 in_=sb["colx"][:, 128:128 + nsrc],
                                 axis=mybir.AxisListType.X, op=ALU.max))
        op("vector", comb, reads=(rr("colx", 0, 64 + nsrc),),
           writes=(rr("colx", 128, 128 + nsrc), rr("tcc2", gi, gi + 1)), tag=f"gc{gi}")
        op("gpsimd", lambda e, gi=gi: tick(e.tensor_reduce(
            out=sb["tcc"][0:1, gi:gi + 1], in_=sb["tcc2"][:, gi:gi + 1],
            axis=mybir.AxisListType.C, op=ALU.max)),
           reads=(rr("tcc2", gi, gi + 1),), writes=(rr("tcc", gi, gi + 1),), tag=f"gp{gi}")
        op("gpsimd", is_dma_op=True,
           emit=lambda e, cci=cci: dma(e.dma_start(out=ap[cci], in_=sb["tcc"][0:1, :])),
           reads=(rr("tcc"),), writes=(rr(cci),), tag=f"gw{gi}")
        op("gpsimd", lambda e, cci=cci, cco=cco: tick(e.collective_compute(
            "AllReduce", ALU.max, replica_groups=[list(range(8))],
            ins=[ap[cci].opt()], outs=[ap[cco].opt()])),
           reads=(rr(cci),), writes=(rr(cco),), tag=f"cc{gi}", is_coll=True)
        op("gpsimd", is_dma_op=True,
           emit=lambda e, cco=cco, gi=gi: dma(e.dma_start(
               out=sb["tcc"][0:1, 8 + gi:8 + gi + 1], in_=ap[cco][0:1, gi:gi + 1])),
           reads=(rr(cco),), writes=(rr("tcc", 8 + gi, 8 + gi + 1),), tag=f"gr{gi}")
        op("tensor", lambda e, gi=gi: tick(e.matmul(
            out=ps_dm[:, 0:1], lhsT=sb["onesf"][0:1, :],
            rhs=sb["tcc"][0:1, 8 + gi:8 + gi + 1], start=True, stop=True)),
           reads=(rr("tcc", 8 + gi, 8 + gi + 1), rr("c_onesf")), writes=(PDM(),),
           tag=f"gb{gi}")
        op("vector", lambda e, gi=gi: tick(e.tensor_copy(
            out=sb["glob"][:, gi:gi + 1], in_=ps_dm[:, 0:1])),
           reads=(PDM(),), writes=(PDM(), rr("glob", gi, gi + 1)), tag=f"ge{gi}")

    def l0_ffn():
        lpbq_prep("wg0", "wg0_b", NIC, D)
        lpbq_prep("wu0", "wu0_b", NIC, D)
        lpbq_prep("wd0", "wd0_b", D // 128, I0)

        for t in range(NT):
            sl = t % 2
            op("sync", is_dma_op=True, emit=lambda e, t=t, sl=sl: dma(e.dma_start(
                out=PF(XF2 + sl * 2048, 1024), in_=ap["xres"][t * 128:(t + 1) * 128, :])),
               reads=(rr("xres", t * 128, (t + 1) * 128),),
               writes=(prf(XF2 + sl * 2048, 1024),), tag=f"f2x{t}")

            def stats(e, t=t, sl=sl):
                e.bn_stats(out=sb["tiny"][:, 0:6], in_=PF(XF2 + sl * 2048, 512))
                e.bn_stats(out=sb["tiny"][:, 6:12], in_=PF(XF2 + sl * 2048 + 1024, 512))
                e.drain()
                e.bn_aggr(out=sb["tiny"][:, 12:14], in_=sb["tiny"][:, 0:12])
                e.drain()
                e.tensor_copy(out=sb["colm"][:, t:t + 1], in_=sb["tiny"][:, 12:13])
                tick(e.tensor_scalar(out=sb["colx"][:, 160 + t:161 + t], in0=sb["tiny"][:, 13:14],
                                     scalar1=EPS, scalar2=None, op0=ALU.add))
            op("vector", stats, reads=(prf(XF2 + sl * 2048, 1024),),
               writes=(rr("tiny", 0, 15), rr("colm", t, t + 1),
                       rr("colx", 160 + t, 161 + t)),
               tag=f"f2s{t}")
            jx = PF(YJ, 1024) if sl == 0 else PF(XQ, 1024)
            jn = PF(YJ + 2048, 1024) if sl == 0 else PF(XQ + 2048, 1024)
            jxr = prf(YJ, 1024) if sl == 0 else prf(XQ, 1024)
            jnr = prf(YJ + 2048, 1024) if sl == 0 else prf(XQ + 2048, 1024)
            op("vector", lambda e, t=t, sl=sl, jx=jx: tick(e.tensor_scalar(
                out=jx, in0=PF(XF2 + sl * 2048, 1024),
                scalar1=1.0, scalar2=-1e30, op0=ALU.mult, op1=ALU.max,
                accum_out=sb["dcol"][:, t:t + 1])),
               reads=(prf(XF2 + sl * 2048, 1024),),
               writes=(jxr, rr("dcol", t, t + 1)), tag=f"f2gx{t}")
            og = op("vector", lambda e, t=t, sl=sl, jn=jn: tick(e.tensor_scalar(
                out=jn, in0=PF(XF2 + sl * 2048, 1024),
                scalar1=1.0, scalar2=1e30, op0=ALU.mult, op1=ALU.min,
                accum_out=sb["dcol"][:, 32 + t:33 + t])),
               reads=(prf(XF2 + sl * 2048, 1024),),
               writes=(jnr, rr("dcol", 32 + t, 33 + t)), tag=f"f2gn{t}")
            heat(3, after=og)
        op("scalar", lambda e: (
            e.activation(out=sb["colx"][:, 160:160 + NT], in_=sb["colx"][:, 160:160 + NT], func=AF.Ln),
            e.drain(),
            tick(e.activation(out=sb["colr"][:, 0:NT], in_=sb["colx"][:, 160:160 + NT],
                              func=AF.Exp, scale=-0.5))),
           reads=(rr("colx", 160, 160 + NT),),
           writes=(rr("colx", 160, 160 + NT), rr("colr", 0, NT)), tag="f2rsb")
        def scale_cols(e):
            e.tensor_tensor(out=sb["colx"][:, 0:NT], in0=sb["dcol"][:, 0:NT],
                            in1=sb["colm"][:, 0:NT], op=ALU.subtract)
            e.tensor_tensor(out=sb["colx"][:, 64:64 + NT], in0=sb["dcol"][:, 32:32 + NT],
                            in1=sb["colm"][:, 0:NT], op=ALU.subtract)
            e.drain()
            e.tensor_tensor(out=sb["colx"][:, 0:NT], in0=sb["colx"][:, 0:NT],
                            in1=sb["colr"][:, 0:NT], op=ALU.mult)
            tick(e.tensor_tensor(out=sb["colx"][:, 64:64 + NT], in0=sb["colx"][:, 64:64 + NT],
                                 in1=sb["colr"][:, 0:NT], op=ALU.mult))
        op("vector", scale_cols, reads=(rr("colr", 0, NT), rr("dcol", 0, 64),
                                        rr("colm", 0, NT)),
           writes=(rr("colx", 0, NT), rr("colx", 64, 64 + NT)), tag="f2sc")
        ground(0, NT, "cci1", "cco1")
        def mkscale0(e):
            e.reciprocal(out=sb["cols"][:, 0:1], in_=sb["glob"][:, 0:1])
            e.drain()
            e.tensor_scalar(out=sb["cols"][:, 0:1], in0=sb["cols"][:, 0:1],
                            scalar1=32767.0, scalar2=None, op0=ALU.mult)
            e.tensor_scalar(out=sb["cols"][:, 1:2], in0=sb["glob"][:, 0:1],
                            scalar1=1.0 / 32767.0, scalar2=None, op0=ALU.mult)
            e.drain()
            tick(e.tensor_tensor(out=sb["cols"][:, 8:8 + NT], in0=sb["colr"][:, 0:NT],
                                 in1=sb["cols"][:, 0:1].broadcast_to([128, NT]),
                                 op=ALU.mult))
        op("vector", mkscale0, reads=(rr("glob", 0, 1), rr("colr", 0, NT)),
           writes=(rr("cols", 0, 2), rr("cols", 8, 8 + NT)), tag="f2k0")

        def f2_ldx(t):
            sl = t % 2
            op("sync", is_dma_op=True, emit=lambda e, t=t, sl=sl: dma(e.dma_start(
                out=PF(XF2 + sl * 2048, 1024), in_=ap["xres"][t * 128:(t + 1) * 128, :])),
               reads=(rr("xres", t * 128, (t + 1) * 128),),
               writes=(prf(XF2 + sl * 2048, 1024),), tag=f"f2x2_{t}")

        def f2_tev(t):
            def xqT(e, t=t):
                for dc in range(8):
                    i = e.transpose(out=pt_op[:, (t % 2) * 1024 + dc * 128:(t % 2) * 1024 + (dc + 1) * 128],
                                    in_=PB(WB2 + (t % 2) * 1024 + dc * 128, 128),
                                    identity=sb["id"][:, :])
                tick(i)
            op("tensor", xqT, reads=(pr(WB2 + (t % 2) * 1024, 1024), rr("c_id")),
               writes=(POP((t % 2) * 512, 512),), tag=f"f2t{t}")
            op("vector", lambda e, t=t: tick(e.tensor_copy(
                out=PB(XQ, 32768).rearrange("p (dc n) -> p dc n", n=4096)[:, :, t * 128:(t + 1) * 128],
                in_=pt_op[:, (t % 2) * 1024:(t % 2) * 1024 + 1024].rearrange("p (dc n) -> p dc n", n=128))),
               reads=(POP((t % 2) * 512, 512),),
               writes=(POP((t % 2) * 512, 512),) + tuple(
                   pr(XQ + dc * 4096 + t * 128, 128) for dc in range(8)),
               tag=f"f2e{t}")

        for tp in range(NT // 2):
            t0, t1 = 2 * tp, 2 * tp + 1
            f2_ldx(t0)
            f2_ldx(t1)
            _last = []
            def one(t, k, _last=_last):
                sl = t % 2
                xt = PF(XF2 + sl * 2048, 1024)
                yv = PF(YJ + sl * 2048, 1024)
                if k == 0:
                    return op("vector", lambda e, t=t, xt=xt, yv=yv: tick(e.tensor_scalar(
                        out=yv, in0=xt, scalar1=sb["colm"][:, t:t + 1],
                        scalar2=sb["cols"][:, 8 + t:8 + t + 1],
                        op0=ALU.subtract, op1=ALU.mult)),
                       reads=(prf(XF2 + sl * 2048, 1024), rr("colm", t, t + 1),
                              rr("cols", 8 + t, 9 + t)),
                       writes=(prf(YJ + sl * 2048, 1024),), tag=f"f2qa{t}")
                elif k == 1:
                    return op("vector", lambda e, yv=yv: tick(e.tensor_scalar(
                        out=yv, in0=yv, scalar1=MAGIC, scalar2=MAGIC,
                        op0=ALU.add, op1=ALU.subtract)),
                       reads=(prf(YJ + sl * 2048, 1024),),
                       writes=(prf(YJ + sl * 2048, 1024),), tag=f"f2qb{t}")
                elif k == 2:
                    return op("vector", lambda e, yv=yv: tick(e.tensor_scalar(
                        out=yv, in0=yv, scalar1=32767.0, scalar2=-32768.0,
                        op0=ALU.min, op1=ALU.max)),
                       reads=(prf(YJ + sl * 2048, 1024),),
                       writes=(prf(YJ + sl * 2048, 1024),), tag=f"f2qc{t}")
                else:
                    return op("vector", lambda e, t=t, yv=yv: tick(e.tensor_scalar(
                        out=PB(WB2 + (t % 2) * 1024, 1024), in0=yv,
                        scalar1=sb["cols"][:, 1:2], scalar2=None, op0=ALU.mult)),
                       reads=(prf(YJ + sl * 2048, 1024), rr("cols", 1, 2)),
                       writes=(pr(WB2 + (t % 2) * 1024, 1024),), tag=f"f2qd{t}")
            for k in range(4):
                ok_ = one(t0, k)
                one(t1, k)
                if k == 2:
                    heat(4, after=ok_)
            f2_tev(t0)
            f2_tev(t1)

        def wT_loads(bname, tagp):
            for dc in range(8):
                op("sync", is_dma_op=True, emit=lambda e, dc=dc, bname=bname: dma(
                    e.dma_start_transpose(out=PB(WT + dc * 3072, 3072),
                                          in_=ap[bname][:, dc * 128:(dc + 1) * 128])),
                   reads=(rr(bname),), writes=(pr(WT + dc * 3072, 3072),),
                   tag=f"{tagp}{dc}")

        def mm_ug(scname, colbase, tagp):
            for ic in range(NIC):
                gsl = ic % 2
                for tc in range(8):
                    psl = tc % 4
                    def mm(e, ic=ic, tc=tc, psl=psl):
                        for dc in range(8):
                            i = e.matmul(out=ps_sc[:, psl * 512:(psl + 1) * 512],
                                         lhsT=PB(WT + dc * 3072 + ic * 128, 128),
                                         rhs=PB(XQ + dc * 4096 + tc * 512, 512),
                                         start=(dc == 0), stop=(dc == 7))
                        tick(i)
                    op("tensor", mm, reads=(pr(WT, 24576), pr(XQ, 32768)),
                       writes=(PSC(psl * 512, 512),), tag=f"{tagp}m{ic}_{tc}")

                    def ev(e, ic=ic, tc=tc, psl=psl, gsl=gsl):
                        e.tensor_scalar(out=PB(GU + gsl * 4096 + tc * 512, 512),
                                        in0=ps_sc[:, psl * 512:(psl + 1) * 512],
                                        scalar1=1.0, scalar2=-1e30,
                                        op0=ALU.mult, op1=ALU.max,
                                        accum_out=sb["tiny"][:, 16 + tc:17 + tc])
                        tick(e.tensor_scalar(out=PB(YJ + (tc % 2) * 2048, 512),
                                             in0=ps_sc[:, psl * 512:(psl + 1) * 512],
                                             scalar1=1.0, scalar2=1e30,
                                             op0=ALU.mult, op1=ALU.min,
                                             accum_out=sb["tiny"][:, 24 + tc:25 + tc]))
                    op("vector", ev,
                       reads=(PSC(psl * 512, 512),),
                       writes=(PSC(psl * 512, 512), pr(GU + gsl * 4096 + tc * 512, 512),
                               pr(YJ + (tc % 2) * 2048, 512),
                               rr("tiny", 16 + tc, 17 + tc),
                               rr("tiny", 24 + tc, 25 + tc)),
                       tag=f"{tagp}e{ic}_{tc}")
                def fold(e, ic=ic, colbase=colbase):
                    e.tensor_reduce(out=sb["colx"][:, (0 if colbase == 0 else 32) + ic:(0 if colbase == 0 else 32) + ic + 1],
                                    in_=sb["tiny"][:, 16:24], axis=mybir.AxisListType.X, op=ALU.max)
                    e.drain()
                    tick(e.tensor_reduce(out=sb["colx"][:, (64 if colbase == 0 else 96) + ic:(64 if colbase == 0 else 96) + ic + 1],
                                         in_=sb["tiny"][:, 24:32], axis=mybir.AxisListType.X, op=ALU.min))
                op("vector", fold, reads=(rr("tiny", 16, 32),),
                   writes=(rr("colx", (0 if colbase == 0 else 32) + ic, (0 if colbase == 0 else 32) + ic + 1),
                           rr("colx", (64 if colbase == 0 else 96) + ic, (64 if colbase == 0 else 96) + ic + 1)),
                   tag=f"{tagp}f{ic}")
                op("gpsimd", is_dma_op=True, emit=lambda e, ic=ic, gsl=gsl, scname=scname: dma(
                    e.dma_start(out=ap[scname][ic], in_=PB(GU + gsl * 4096, 4096))),
                   reads=(pr(GU + gsl * 4096, 4096),),
                   writes=(rr(scname, ic, ic + 1),), tag=f"{tagp}s{ic}")

        wT_loads("wg0_b", "wgT")
        mm_ug("gate_sc", 0, "g")
        ground(1, NIC, "cci2", "cco2")
        wT_loads("wu0_b", "wuT")
        mm_ug("up_sc", 32, "u")
        def upmix(e):
            e.scalar_tensor_tensor(out=sb["colx"][:, 128:128 + NIC],
                                   in0=sb["colx"][:, 96:96 + NIC], scalar=-1.0,
                                   in1=sb["colx"][:, 32:32 + NIC], op0=ALU.mult, op1=ALU.max)
            e.drain()
            tick(e.tensor_reduce(out=sb["tcc2"][:, 2:3], in_=sb["colx"][:, 128:128 + NIC],
                                 axis=mybir.AxisListType.X, op=ALU.max))
        op("vector", upmix, reads=(rr("colx", 32, 32 + NIC), rr("colx", 96, 96 + NIC)),
           writes=(rr("colx", 128, 128 + NIC), rr("tcc2", 2, 3)), tag="gcup")
        op("gpsimd", lambda e: tick(e.tensor_reduce(
            out=sb["tcc"][0:1, 2:3], in_=sb["tcc2"][:, 2:3],
            axis=mybir.AxisListType.C, op=ALU.max)),
           reads=(rr("tcc2", 2, 3),), writes=(rr("tcc", 2, 3),), tag="gpup")
        op("gpsimd", is_dma_op=True,
           emit=lambda e: dma(e.dma_start(out=ap["cci3"], in_=sb["tcc"][0:1, :])),
           reads=(rr("tcc"),), writes=(rr("cci3"),), tag="gwup")
        op("gpsimd", lambda e: tick(e.collective_compute(
            "AllReduce", ALU.max, replica_groups=[list(range(8))],
            ins=[ap["cci3"].opt()], outs=[ap["cco3"].opt()])),
           reads=(rr("cci3"),), writes=(rr("cco3"),), tag="ccup", is_coll=True)
        op("gpsimd", is_dma_op=True,
           emit=lambda e: dma(e.dma_start(out=sb["tcc"][0:1, 10:11], in_=ap["cco3"][0:1, 2:3])),
           reads=(rr("cco3"),), writes=(rr("tcc", 10, 11),), tag="grup")
        op("tensor", lambda e: tick(e.matmul(
            out=ps_dm[:, 0:1], lhsT=sb["onesf"][0:1, :],
            rhs=sb["tcc"][0:1, 10:11], start=True, stop=True)),
           reads=(rr("tcc", 10, 11), rr("c_onesf")), writes=(PDM(),), tag="gbup")
        op("vector", lambda e: tick(e.tensor_copy(
            out=sb["glob"][:, 2:3], in_=ps_dm[:, 0:1])),
           reads=(PDM(),), writes=(PDM(), rr("glob", 2, 3)), tag="geup")

        def mkscale12(e):
            e.reciprocal(out=sb["cols"][:, 2:3], in_=sb["glob"][:, 1:2])
            e.reciprocal(out=sb["cols"][:, 4:5], in_=sb["glob"][:, 2:3])
            e.drain()
            e.tensor_scalar(out=sb["cols"][:, 2:3], in0=sb["cols"][:, 2:3],
                            scalar1=32767.0, scalar2=None, op0=ALU.mult)
            e.tensor_scalar(out=sb["cols"][:, 3:4], in0=sb["glob"][:, 1:2],
                            scalar1=1.0 / 32767.0, scalar2=None, op0=ALU.mult)
            e.tensor_scalar(out=sb["cols"][:, 4:5], in0=sb["cols"][:, 4:5],
                            scalar1=32767.0, scalar2=None, op0=ALU.mult)
            e.tensor_scalar(out=sb["cols"][:, 5:6], in0=sb["glob"][:, 2:3],
                            scalar1=1.0 / 32767.0, scalar2=None, op0=ALU.mult)
            e.drain()
            tick(e.tensor_scalar(out=sb["cols"][:, 44:45], in0=sb["cols"][:, 3:4],
                                 scalar1=1.0 / 65536.0, scalar2=None, op0=ALU.mult))
        op("vector", mkscale12, reads=(rr("glob", 1, 3),),
           writes=(rr("cols", 2, 6), rr("cols", 44, 45)), tag="f2k12")

        P2Q = lambda r: XQ + r * 8192
        for icp in range(NIC // 2):
            for ic in (2 * icp, 2 * icp + 1):
                rho = ic % 2
                op("sync", is_dma_op=True, emit=lambda e, ic=ic, rho=rho: dma(e.dma_start(
                    out=PB(GU + rho * 4096, 4096), in_=ap["gate_sc"][ic])),
                   reads=(rr("gate_sc", ic, ic + 1),),
                   writes=(pr(GU + rho * 4096, 4096),), tag=f"tld{ic}")

            def p2s(ic, k):
                rho = ic % 2
                q, qr = PF(P2Q(rho), 4096), prf(P2Q(rho), 4096)
                sg, sgr = PF(P2Q(2 + rho), 4096), prf(P2Q(2 + rho), 4096)
                tv, tvr = PF(WT + rho * 8192, 4096), prf(WT + rho * 8192, 4096)
                ob, obr = PB(WL + rho * 4096, 4096), pr(WL + rho * 4096, 4096)
                if k == 0:
                    return op("scalar", lambda e, rho=rho, q=q: tick(e.activation(
                        out=q, in_=PB(GU + rho * 4096, 4096), func=AF.Copy,
                        scale=sb["cols"][:, 2:3], bias=MAGIC)),
                       reads=(pr(GU + rho * 4096, 4096), rr("cols", 2, 3)),
                       writes=(qr,), tag=f"tqa{ic}")
                elif k == 1:
                    return op("scalar", lambda e, q=q: tick(e.activation(
                        out=q, in_=q, func=AF.Copy, bias=-MAGIC)),
                       reads=(qr,), writes=(qr,), tag=f"tqa2{ic}")
                elif k == 2:
                    return op("vector", lambda e, q=q: tick(e.tensor_scalar(
                        out=q, in0=q, scalar1=32767.0, scalar2=-32768.0,
                        op0=ALU.min, op1=ALU.max)),
                       reads=(qr,), writes=(qr,), tag=f"tqb{ic}")
                elif k == 3:
                    return op("scalar", lambda e, q=q, sg=sg: tick(e.activation(
                        out=sg, in_=q, func=AF.Sigmoid, scale=sb["cols"][:, 3:4])),
                       reads=(qr, rr("cols", 3, 4)), writes=(sgr,), tag=f"tsg{ic}")
                elif k == 4:
                    return op("scalar", lambda e, sg=sg: tick(e.activation(
                        out=sg, in_=sg, func=AF.Copy, scale=65536.0,
                        bias=MAGIC)),
                       reads=(sgr,), writes=(sgr,), tag=f"tfa{ic}")
                elif k == 5:
                    return op("scalar", lambda e, sg=sg: tick(e.activation(
                        out=sg, in_=sg, func=AF.Copy, bias=-MAGIC)),
                       reads=(sgr,), writes=(sgr,), tag=f"tfa2{ic}")
                elif k == 6:
                    return op("vector", lambda e, sg=sg: tick(e.tensor_scalar(
                        out=sg, in0=sg, scalar1=65535.0, scalar2=0.0,
                        op0=ALU.min, op1=ALU.max)),
                       reads=(sgr,), writes=(sgr,), tag=f"tfb{ic}")
                elif k == 9:
                    return op("vector", lambda e, q=q, sg=sg, tv=tv: tick(e.tensor_tensor(
                        out=tv, in0=sg, in1=q, op=ALU.mult)),
                       reads=(sgr, qr), writes=(tvr,), tag=f"tfc{ic}")
                elif k == 7:
                    return op("vector", lambda e, ic=ic, tv=tv, ob=ob: tick(e.tensor_scalar(
                        out=ob, in0=tv, scalar1=sb["cols"][:, 44:45], scalar2=-1e30,
                        op0=ALU.mult, op1=ALU.max,
                        accum_out=sb["colx"][:, ic:ic + 1])),
                       reads=(tvr, rr("cols", 44, 45)),
                       writes=(obr, rr("colx", ic, ic + 1)), tag=f"tfd{ic}")
                else:
                    return op("vector", lambda e, ic=ic, tv=tv, q=q: tick(e.tensor_scalar(
                        out=q, in0=tv, scalar1=sb["cols"][:, 44:45], scalar2=1e30,
                        op0=ALU.mult, op1=ALU.min,
                        accum_out=sb["colx"][:, 64 + ic:64 + ic + 1])),
                       reads=(tvr, rr("cols", 44, 45)),
                       writes=(qr, rr("colx", 64 + ic, 64 + ic + 1)), tag=f"tfe{ic}")

            pace = []
            for k in (0, 1, 2, 3, 4, 5, 6, 9, 7, 8):
                oa = p2s(2 * icp, k)
                p2s(2 * icp + 1, k)
                if k in (2, 6, 9, 7, 8):
                    pace.append(oa)
            for pa in pace:
                heat(10, after=pa)
            for ic in (2 * icp, 2 * icp + 1):
                rho = ic % 2
                op("gpsimd", is_dma_op=True, emit=lambda e, ic=ic, rho=rho: dma(e.dma_start(
                    out=ap["gate_sc"][ic], in_=PB(WL + rho * 4096, 4096))),
                   reads=(pr(WL + rho * 4096, 4096),),
                   writes=(rr("gate_sc", ic, ic + 1),), tag=f"tst{ic}")
        ground(3, NIC, "cci4", "cco4")
        def mkscale3(e):
            e.reciprocal(out=sb["cols"][:, 6:7], in_=sb["glob"][:, 3:4])
            e.drain()
            e.tensor_scalar(out=sb["cols"][:, 6:7], in0=sb["cols"][:, 6:7],
                            scalar1=32767.0, scalar2=None, op0=ALU.mult)
            tick(e.tensor_scalar(out=sb["cols"][:, 7:8], in0=sb["glob"][:, 3:4],
                                 scalar1=1.0 / 32767.0, scalar2=None, op0=ALU.mult))
        op("vector", mkscale3, reads=(rr("glob", 3, 4),), writes=(rr("cols", 6, 8),), tag="f2k3")

        for icp in range(NIC // 2):
            for ic in (2 * icp, 2 * icp + 1):
                rho = ic % 2
                op("sync", is_dma_op=True, emit=lambda e, ic=ic, rho=rho: dma(e.dma_start(
                    out=PB(GU + rho * 4096, 4096), in_=ap["gate_sc"][ic])),
                   reads=(rr("gate_sc", ic, ic + 1),),
                   writes=(pr(GU + rho * 4096, 4096),), tag=f"old{ic}")
                op("scalar", is_dma_op=True, emit=lambda e, ic=ic, rho=rho: dma(e.dma_start(
                    out=PB(EL + rho * 4096, 4096), in_=ap["up_sc"][ic])),
                   reads=(rr("up_sc", ic, ic + 1),),
                   writes=(pr(EL + rho * 4096, 4096),), tag=f"old2{ic}")

            def p3s(ic, k):
                rho = ic % 2
                g2, g2r = PF(P2Q(rho), 4096), prf(P2Q(rho), 4096)
                uq, uqr = PF(P2Q(2 + rho), 4096), prf(P2Q(2 + rho), 4096)
                tv, tvr = PF(WT + rho * 8192, 4096), prf(WT + rho * 8192, 4096)
                ob, obr = PB(WL + rho * 4096, 4096), pr(WL + rho * 4096, 4096)
                if k == 0:
                    return op("scalar", lambda e, rho=rho, g2=g2: tick(e.activation(
                        out=g2, in_=PB(GU + rho * 4096, 4096), func=AF.Copy,
                        scale=sb["cols"][:, 6:7], bias=MAGIC)),
                       reads=(pr(GU + rho * 4096, 4096), rr("cols", 6, 7)),
                       writes=(g2r,), tag=f"opa{ic}")
                elif k == 1:
                    return op("scalar", lambda e, g2=g2: tick(e.activation(
                        out=g2, in_=g2, func=AF.Copy, bias=-MAGIC)),
                       reads=(g2r,), writes=(g2r,), tag=f"opa2{ic}")
                elif k == 2:
                    return op("vector", lambda e, g2=g2: tick(e.tensor_scalar(
                        out=g2, in0=g2, scalar1=32767.0, scalar2=-32768.0,
                        op0=ALU.min, op1=ALU.max)),
                       reads=(g2r,), writes=(g2r,), tag=f"opb{ic}")
                elif k == 3:
                    return op("scalar", lambda e, rho=rho, uq=uq: tick(e.activation(
                        out=uq, in_=PB(EL + rho * 4096, 4096), func=AF.Copy,
                        scale=sb["cols"][:, 4:5], bias=MAGIC)),
                       reads=(pr(EL + rho * 4096, 4096), rr("cols", 4, 5)),
                       writes=(uqr,), tag=f"opd{ic}")
                elif k == 4:
                    return op("scalar", lambda e, uq=uq: tick(e.activation(
                        out=uq, in_=uq, func=AF.Copy, bias=-MAGIC)),
                       reads=(uqr,), writes=(uqr,), tag=f"opd2{ic}")
                elif k == 5:
                    return op("vector", lambda e, uq=uq: tick(e.tensor_scalar(
                        out=uq, in0=uq, scalar1=32767.0, scalar2=-32768.0,
                        op0=ALU.min, op1=ALU.max)),
                       reads=(uqr,), writes=(uqr,), tag=f"ope{ic}")
                elif k == 6:
                    return op("vector", lambda e, g2=g2, uq=uq, tv=tv: tick(e.scalar_tensor_tensor(
                        out=tv, in0=g2, scalar=sb["cols"][:, 7:8], in1=uq,
                        op0=ALU.mult, op1=ALU.mult)),
                       reads=(g2r, uqr, rr("cols", 7, 8)), writes=(tvr,), tag=f"opg{ic}")
                elif k == 7:
                    return op("vector", lambda e, ic=ic, tv=tv, ob=ob: tick(e.tensor_scalar(
                        out=ob, in0=tv, scalar1=sb["cols"][:, 5:6], scalar2=-1e30,
                        op0=ALU.mult, op1=ALU.max,
                        accum_out=sb["colx"][:, 32 + ic:32 + ic + 1])),
                       reads=(tvr, rr("cols", 5, 6)),
                       writes=(obr, rr("colx", 32 + ic, 32 + ic + 1)), tag=f"oph{ic}")
                else:
                    return op("vector", lambda e, ic=ic, tv=tv, g2=g2: tick(e.tensor_scalar(
                        out=g2, in0=tv, scalar1=sb["cols"][:, 5:6], scalar2=1e30,
                        op0=ALU.mult, op1=ALU.min,
                        accum_out=sb["colx"][:, 96 + ic:96 + ic + 1])),
                       reads=(tvr, rr("cols", 5, 6)),
                       writes=(g2r, rr("colx", 96 + ic, 96 + ic + 1)), tag=f"opi{ic}")

            pace = []
            for k in range(9):
                oa = p3s(2 * icp, k)
                p3s(2 * icp + 1, k)
                if k in (2, 5, 6, 7, 8):
                    pace.append(oa)
            for pa in pace:
                heat(10, after=pa)
            for ic in (2 * icp, 2 * icp + 1):
                rho = ic % 2
                op("gpsimd", is_dma_op=True, emit=lambda e, ic=ic, rho=rho: dma(e.dma_start(
                    out=ap["up_sc"][ic], in_=PB(WL + rho * 4096, 4096))),
                   reads=(pr(WL + rho * 4096, 4096),),
                   writes=(rr("up_sc", ic, ic + 1),), tag=f"ost{ic}")
        def omix(e):
            e.scalar_tensor_tensor(out=sb["colx"][:, 128:128 + NIC],
                                   in0=sb["colx"][:, 96:96 + NIC], scalar=-1.0,
                                   in1=sb["colx"][:, 32:32 + NIC], op0=ALU.mult, op1=ALU.max)
            e.drain()
            tick(e.tensor_reduce(out=sb["tcc2"][:, 4:5], in_=sb["colx"][:, 128:128 + NIC],
                                 axis=mybir.AxisListType.X, op=ALU.max))
        op("vector", omix, reads=(rr("colx", 32, 32 + NIC), rr("colx", 96, 96 + NIC)),
           writes=(rr("colx", 128, 128 + NIC), rr("tcc2", 4, 5)), tag="gco")
        op("gpsimd", lambda e: tick(e.tensor_reduce(
            out=sb["tcc"][0:1, 4:5], in_=sb["tcc2"][:, 4:5],
            axis=mybir.AxisListType.C, op=ALU.max)),
           reads=(rr("tcc2", 4, 5),), writes=(rr("tcc", 4, 5),), tag="gpo")
        op("gpsimd", is_dma_op=True,
           emit=lambda e: dma(e.dma_start(out=ap["cci5"], in_=sb["tcc"][0:1, :])),
           reads=(rr("tcc"),), writes=(rr("cci5"),), tag="gwo")
        op("gpsimd", lambda e: tick(e.collective_compute(
            "AllReduce", ALU.max, replica_groups=[list(range(8))],
            ins=[ap["cci5"].opt()], outs=[ap["cco5"].opt()])),
           reads=(rr("cci5"),), writes=(rr("cco5"),), tag="cco", is_coll=True)
        op("gpsimd", is_dma_op=True,
           emit=lambda e: dma(e.dma_start(out=sb["tcc"][0:1, 12:13], in_=ap["cco5"][0:1, 4:5])),
           reads=(rr("cco5"),), writes=(rr("tcc", 12, 13),), tag="gro")
        op("tensor", lambda e: tick(e.matmul(
            out=ps_dm[:, 0:1], lhsT=sb["onesf"][0:1, :],
            rhs=sb["tcc"][0:1, 12:13], start=True, stop=True)),
           reads=(rr("tcc", 12, 13), rr("c_onesf")), writes=(PDM(),), tag="gbo")
        op("vector", lambda e: (
            e.tensor_copy(out=sb["glob"][:, 4:5], in_=ps_dm[:, 0:1]),
            e.drain(),
            e.reciprocal(out=sb["cols"][:, 12:13], in_=sb["glob"][:, 4:5]),
            e.drain(),
            e.tensor_scalar(out=sb["cols"][:, 12:13], in0=sb["cols"][:, 12:13],
                            scalar1=32767.0, scalar2=None, op0=ALU.mult),
            tick(e.tensor_scalar(out=sb["cols"][:, 13:14], in0=sb["glob"][:, 4:5],
                                 scalar1=1.0 / 32767.0, scalar2=None, op0=ALU.mult))),
           reads=(PDM(),), writes=(PDM(), rr("glob", 4, 5), rr("cols", 12, 14)), tag="geo")

        for ic in range(NIC):
            op("sync", is_dma_op=True, emit=lambda e, ic=ic: dma(
                e.dma_start_transpose(out=PB(WT + ic * 1024, 1024),
                                      in_=ap["wd0_b"][:, ic * 128:(ic + 1) * 128])),
               reads=(rr("wd0_b"),), writes=(pr(WT + ic * 1024, 1024),), tag=f"wdT{ic}")
        for tg in range(8):
            for ic in range(NIC):
                osl = ic % 4
                op("sync", is_dma_op=True, emit=lambda e, tg=tg, ic=ic: dma(e.dma_start(
                    out=PB(EL + (ic % 2) * 1024, 512),
                    in_=ap["up_sc"][ic, :, tg * 512:(tg + 1) * 512])),
                   reads=(rr("up_sc", ic, ic + 1),),
                   writes=(pr(EL + (ic % 2) * 1024, 512),), tag=f"dol{tg}_{ic}")

                def oq(e, tg=tg, ic=ic, osl=osl):
                    ov = PF(EL2, 512)
                    e.tensor_scalar(out=ov, in0=PB(EL + (ic % 2) * 1024, 512),
                                    scalar1=sb["cols"][:, 12:13], scalar2=MAGIC,
                                    op0=ALU.mult, op1=ALU.add)
                    e.drain()
                    e.tensor_scalar(out=ov, in0=ov, scalar1=MAGIC, scalar2=32767.0,
                                    op0=ALU.subtract, op1=ALU.min)
                    e.drain()
                    tick(e.tensor_scalar(out=PB(OI + osl * 512, 512), in0=ov,
                                         scalar1=-32768.0, scalar2=sb["cols"][:, 13:14],
                                         op0=ALU.max, op1=ALU.mult))
                op("vector", oq, reads=(pr(EL + (ic % 2) * 1024, 512), rr("cols", 12, 14)),
                   writes=(prf(EL2, 512), pr(OI + osl * 512, 512)), tag=f"doq{tg}_{ic}")

                def dmm(e, tg=tg, ic=ic, osl=osl):
                    for hc in range(8):
                        if hc < 4:
                            out = ps_sc[:, hc * 512:(hc + 1) * 512]
                        elif hc == 4:
                            out = ps_cx[:, 0:512]
                        elif hc == 5:
                            out = ps_dm[:, 0:512]
                        else:
                            out = ps_op[:, (hc - 6) * 512:(hc - 5) * 512]
                        i = e.matmul(out=out,
                                     lhsT=PB(WT + ic * 1024 + hc * 128, 128),
                                     rhs=PB(OI + osl * 512, 512),
                                     start=(ic == 0), stop=(ic == NIC - 1))
                    tick(i)
                op("tensor", dmm, reads=(pr(WT, 24576), pr(OI + osl * 512, 512)),
                   writes=(PSC(0, 2048), PCX(), PDM(), POP(0, 1024)), tag=f"dmm{tg}_{ic}")
            def dev(e, tg=tg):
                for hc in range(8):
                    if hc < 4:
                        src = ps_sc[:, hc * 512:(hc + 1) * 512]
                    elif hc == 4:
                        src = ps_cx[:, 0:512]
                    elif hc == 5:
                        src = ps_dm[:, 0:512]
                    else:
                        src = ps_op[:, (hc - 6) * 512:(hc - 5) * 512]
                    i = e.tensor_copy(out=PF(DS8 + hc * 1024, 512), in_=src)
                tick(i)
            op("vector", dev,
               reads=(PSC(0, 2048), PCX(), PDM(), POP(0, 1024)),
               writes=(PSC(0, 2048), PCX(), PDM(), POP(0, 1024), prf(DS8, 4096)),
               tag=f"dev{tg}")

            def dtr2(e, tg=tg):
                for hc in range(8):
                    for tc2 in range(4):
                        if tc2 < 2:
                            out = ps_sc[:, tc2 * 1024 + hc * 128: tc2 * 1024 + (hc + 1) * 128]
                        elif tc2 == 2:
                            out = (ps_cx if hc < 4 else ps_dm)[:, (hc % 4) * 128:(hc % 4 + 1) * 128]
                        else:
                            out = ps_op[:, hc * 128:(hc + 1) * 128]
                        i = e.transpose(out=out,
                                        in_=PF(DS8 + hc * 1024 + tc2 * 256, 128),
                                        identity=sb["idf"][:, :])
                tick(i)
            op("tensor", dtr2, reads=(prf(DS8, 4096), rr("c_idf")),
               writes=(PSC(0, 2048), PCX(), PDM(), POP(0, 1024)), tag=f"dtr{tg}")
            for tc2 in range(4):
                t = tg * 4 + tc2
                op("sync", is_dma_op=True, emit=lambda e, t=t, tc2=tc2: dma(e.dma_start(
                    out=PF(XF2 + (tc2 % 2) * 2048, 1024),
                    in_=ap["xres"][t * 128:(t + 1) * 128, :])),
                   reads=(rr("xres", t * 128, (t + 1) * 128),),
                   writes=(prf(XF2 + (tc2 % 2) * 2048, 1024),), tag=f"dx{t}")

                def dres(e, tc2=tc2):
                    xv = PF(XF2 + (tc2 % 2) * 2048, 1024)
                    if tc2 < 2:
                        i = e.scalar_tensor_tensor(out=xv, in0=ps_sc[:, tc2 * 1024:(tc2 + 1) * 1024],
                                                   scalar=1.0, in1=xv, op0=ALU.mult, op1=ALU.add)
                    elif tc2 == 2:
                        e.scalar_tensor_tensor(out=PF(XF2 + (tc2 % 2) * 2048, 512), in0=ps_cx[:, 0:512],
                                               scalar=1.0, in1=PF(XF2 + (tc2 % 2) * 2048, 512),
                                               op0=ALU.mult, op1=ALU.add)
                        e.drain()
                        i = e.scalar_tensor_tensor(out=PF(XF2 + (tc2 % 2) * 2048 + 1024, 512),
                                                   in0=ps_dm[:, 0:512], scalar=1.0,
                                                   in1=PF(XF2 + (tc2 % 2) * 2048 + 1024, 512),
                                                   op0=ALU.mult, op1=ALU.add)
                    else:
                        i = e.scalar_tensor_tensor(out=xv, in0=ps_op[:, 0:1024],
                                                   scalar=1.0, in1=xv, op0=ALU.mult, op1=ALU.add)
                    tick(i)
                rdsets = {0: (PSC(0, 1024),), 1: (PSC(1024, 1024),),
                          2: (PCX(), PDM()), 3: (POP(0, 1024),)}[tc2]
                op("vector", dres,
                   reads=rdsets + (prf(XF2 + (tc2 % 2) * 2048, 1024),),
                   writes=rdsets + (prf(XF2 + (tc2 % 2) * 2048, 1024),), tag=f"dr{t}")
                op("gpsimd", is_dma_op=True, emit=lambda e, t=t, tc2=tc2: dma(e.dma_start(
                    out=ap["xres"][t * 128:(t + 1) * 128, :],
                    in_=PF(XF2 + (tc2 % 2) * 2048, 1024))),
                   reads=(prf(XF2 + (tc2 % 2) * 2048, 1024),),
                   writes=(rr("xres", t * 128, (t + 1) * 128),), tag=f"dw{t}")

    if n_phases >= 2:
        l0_ffn()

    XF3 = 0
    LNT = 4096
    LNF = 20480
    W1T = 36864
    HF = 45056
    YB = 61440
    CSB = 65536
    SR = 69632
    ER = 73728
    WC = 77824

    def l1_attn():
        op("sync", is_dma_op=True, emit=lambda e: dma(e.dma_start(
            out=sb["cols"][:, 16:24],
            in_=ap["b1"][0:1, :].rearrange("o (c p) -> (o p) c", p=128),
            allow_slow_non_contiguous=True)),
           reads=(rr("b1"),), writes=(rr("cols", 16, 24),), tag="b1l")
        op("sync", is_dma_op=True, emit=lambda e: dma(e.dma_start(
            out=sb["cols"][:, 24:32],
            in_=ap["w2"][0:1, :].rearrange("o (c p) -> (o p) c", p=128),
            allow_slow_non_contiguous=True)),
           reads=(rr("w2"),), writes=(rr("cols", 24, 32),), tag="w2l")
        op("vector", lambda e: tick(e.tensor_copy(out=PB(WC, 8), in_=sb["cols"][:, 24:32])),
           reads=(rr("cols", 24, 32),), writes=(pr(WC, 8),), tag="w2c")
        op("sync", is_dma_op=True, emit=lambda e: dma(e.dma_start(
            out=sb["b2sb"][0:1, 0:1], in_=ap["b2"])),
           reads=(rr("b2"),), writes=(rr("b2sb", 0, 1),), tag="b2l")
        for dc in range(8):
            op("sync", is_dma_op=True, emit=lambda e, dc=dc: dma(
                e.dma_start_transpose(out=PB(W1T + dc * 1024, 1024),
                                      in_=ap["w1_b"][:, dc * 128:(dc + 1) * 128])),
               reads=(rr("w1_b"),), writes=(pr(W1T + dc * 1024, 1024),), tag=f"w1T{dc}")

        for b in range(B_PC):
            for t in range(NTB):
                ti = b * NTB + t
                sl2 = (t % 2) * 2048
                op("sync", is_dma_op=True, emit=lambda e, b=b, t=t, sl2=sl2: dma(e.dma_start(
                    out=PF(XF3 + sl2, 1024),
                    in_=ap["xres"][(b * T + t * 128):(b * T + (t + 1) * 128), :])),
                   reads=(rr("xres", b * T + t * 128, b * T + (t + 1) * 128),),
                   writes=(prf(XF3 + sl2, 1024),), tag=f"a3x{ti}")

                def stats(e, t=t, ti=ti, sl2=sl2):
                    e.bn_stats(out=sb["tiny"][:, 0:6], in_=PF(XF3 + sl2, 512))
                    e.bn_stats(out=sb["tiny"][:, 6:12], in_=PF(XF3 + sl2 + 1024, 512))
                    e.drain()
                    e.bn_aggr(out=sb["tiny"][:, 12:14], in_=sb["tiny"][:, 0:12])
                    e.drain()
                    e.tensor_copy(out=sb["colm"][:, ti:ti + 1], in_=sb["tiny"][:, 12:13])
                    tick(e.tensor_scalar(out=sb["colx"][:, 160 + t:161 + t],
                                         in0=sb["tiny"][:, 13:14],
                                         scalar1=EPS, scalar2=None, op0=ALU.add))
                os_ = op("vector", stats, reads=(prf(XF3 + sl2, 1024),),
                   writes=(rr("tiny", 0, 15), rr("colm", ti, ti + 1),
                           rr("colx", 160 + t, 161 + t)), tag=f"a3s{ti}")
                heat(2, after=os_)
            op("scalar", lambda e, b=b: (
                e.activation(out=sb["colx"][:, 160:160 + NTB],
                             in_=sb["colx"][:, 160:160 + NTB], func=AF.Ln),
                e.drain(),
                tick(e.activation(out=sb["colr"][:, b * NTB:(b + 1) * NTB],
                                  in_=sb["colx"][:, 160:160 + NTB],
                                  func=AF.Exp, scale=-0.5))),
               reads=(rr("colx", 160, 160 + NTB),),
               writes=(rr("colx", 160, 160 + NTB),
                       rr("colr", b * NTB, (b + 1) * NTB)), tag=f"a3rb{b}")
            for t in range(NTB):
                ti = b * NTB + t
                sl2 = (t % 2) * 2048
                op("sync", is_dma_op=True, emit=lambda e, b=b, t=t, sl2=sl2: dma(e.dma_start(
                    out=PF(XF3 + sl2, 1024),
                    in_=ap["xres"][(b * T + t * 128):(b * T + (t + 1) * 128), :])),
                   reads=(rr("xres", b * T + t * 128, b * T + (t + 1) * 128),),
                   writes=(prf(XF3 + sl2, 1024),), tag=f"a3xb{ti}")
                op("vector", lambda e, t=t, ti=ti, sl2=sl2: tick(e.tensor_scalar(
                    out=PB(LNT + t * 1024, 1024), in0=PF(XF3 + sl2, 1024),
                    scalar1=sb["colm"][:, ti:ti + 1], scalar2=sb["colr"][:, ti:ti + 1],
                    op0=ALU.subtract, op1=ALU.mult)),
                   reads=(prf(XF3 + sl2, 1024), rr("colm", ti, ti + 1), rr("colr", ti, ti + 1)),
                   writes=(pr(LNT + t * 1024, 1024),), tag=f"a3l{ti}")
                def lnT(e, t=t):
                    for dc in range(8):
                        i = e.transpose(out=pt_op[:, (t % 2) * 1024 + dc * 128:(t % 2) * 1024 + (dc + 1) * 128],
                                        in_=PB(LNT + t * 1024 + dc * 128, 128),
                                        identity=sb["id"][:, :])
                    tick(i)
                op("tensor", lnT, reads=(pr(LNT + t * 1024, 1024), rr("c_id")),
                   writes=(POP((t % 2) * 512, 512),), tag=f"a3T{ti}")
                op("vector", lambda e, t=t: tick(e.tensor_copy(
                    out=PB(LNF, 16384).rearrange("p (dc n) -> p dc n", n=2048)[:, :, t * 128:(t + 1) * 128],
                    in_=pt_op[:, (t % 2) * 1024:(t % 2) * 1024 + 1024].rearrange("p (dc n) -> p dc n", n=128))),
                   reads=(POP((t % 2) * 512, 512),),
                   writes=(POP((t % 2) * 512, 512),) + tuple(
                       pr(LNF + dc * 2048 + t * 128, 128) for dc in range(8)),
                   tag=f"a3e{ti}")

            for ec in range(8):
                for tc in range(4):
                    psl = (ec * 4 + tc) % 4
                    def hmm(e, ec=ec, tc=tc, psl=psl):
                        for dc in range(8):
                            i = e.matmul(out=ps_sc[:, psl * 512:(psl + 1) * 512],
                                         lhsT=PB(W1T + dc * 1024 + ec * 128, 128),
                                         rhs=PB(LNF + dc * 2048 + tc * 512, 512),
                                         start=(dc == 0), stop=(dc == 7))
                        tick(i)
                    op("tensor", hmm, reads=(pr(W1T, 8192), pr(LNF, 16384)),
                       writes=(PSC(psl * 512, 512),), tag=f"hm{b}_{ec}_{tc}")
                    op("scalar", lambda e, ec=ec, tc=tc, psl=psl: tick(e.activation(
                        out=PB(HF + ec * 2048 + tc * 512, 512),
                        in_=ps_sc[:, psl * 512:(psl + 1) * 512], func=AF.Tanh,
                        bias=sb["cols"][:, 16 + ec:17 + ec])),
                       reads=(PSC(psl * 512, 512), rr("cols", 16, 24)),
                       writes=(PSC(psl * 512, 512), pr(HF + ec * 2048 + tc * 512, 512)),
                       tag=f"he{b}_{ec}_{tc}")

            for tc in range(4):
                def smm(e, tc=tc):
                    for ec in range(8):
                        i = e.matmul(out=ps_dm[0:1, 0:512],
                                     lhsT=g.pool[:, WC + ec:WC + ec + 1],
                                     rhs=PB(HF + ec * 2048 + tc * 512, 512),
                                     start=(ec == 0), stop=(ec == 7))
                    tick(i)
                op("tensor", smm, reads=(pr(WC, 8), pr(HF, 16384)),
                   writes=(PDM(),), tag=f"sm{b}_{tc}")
                op("vector", lambda e, tc=tc: tick(e.tensor_scalar(
                    out=PF(SR + tc * 1024, 512)[0:1, :], in0=ps_dm[0:1, 0:512],
                    scalar1=sb["b2sb"][0:1, 0:1], scalar2=None, op0=ALU.add)),
                   reads=(PDM(), rr("b2sb", 0, 1)),
                   writes=(PDM(), prf(SR + tc * 1024, 512)), tag=f"se{b}_{tc}")
            def mrow(e):
                tick(e.tensor_reduce(out=sb["b2sb"][0:1, 1:2], in_=PF(SR, 2048)[0:1, :],
                                     axis=mybir.AxisListType.X, op=ALU.max, negate=True))
            op("vector", mrow, reads=(prf(SR, 2048),),
               writes=(rr("b2sb", 1, 2), rr("tcc", 15, 16)), tag=f"mr{b}")
            op("scalar", lambda e: tick(e.activation(
                out=PF(ER, 2048)[0:1, :], in_=PF(SR, 2048)[0:1, :], func=AF.Exp,
                bias=sb["b2sb"][0:1, 1:2])),
               reads=(prf(SR, 2048), rr("b2sb", 1, 2)),
               writes=(prf(ER, 2048),), tag=f"ex3{b}")
            def ecT(e):
                for t in range(NTB):
                    i = e.transpose(out=ps_cx[:, t:t + 1], in_=PF(ER + t * 256, 128)[0:1, :],
                                    identity=sb["idf"][0:1, 0:1])
                tick(i)
            op("tensor", ecT, reads=(prf(ER, 2048), rr("c_idf")), writes=(PCX(),),
               tag=f"ecT{b}")
            op("vector", lambda e: tick(e.tensor_copy(out=sb["glob"][:, 0:16],
                                                      in_=ps_cx[:, 0:16])),
               reads=(PCX(),), writes=(PCX(), rr("glob", 0, 16)), tag=f"ecE{b}")

            for m in range(NTB):
                ysl = (m % 2) * 2048
                def ymk(e, m=m, ysl=ysl):
                    e.tensor_scalar(out=PB(YB + ysl, 1024), in0=PB(LNT + m * 1024, 1024),
                                    scalar1=sb["glob"][:, m:m + 1], scalar2=None,
                                    op0=ALU.mult)
                    e.drain()
                    tick(e.tensor_copy(out=PB(YB + ysl + 1024, 1), in_=sb["glob"][:, m:m + 1]))
                op("vector", ymk, reads=(pr(LNT + m * 1024, 1024), rr("glob", 0, 16)),
                   writes=(pr(YB + ysl, 1025),), tag=f"y{b}_{m}")
                def csm(e, m=m, ysl=ysl):
                    e.matmul(out=ps_sc[0:16, 0:512], lhsT=sb["ind"][:, m * 16:(m + 1) * 16],
                             rhs=PB(YB + ysl, 512), start=(m == 0), stop=(m == NTB - 1))
                    e.matmul(out=ps_sc[0:16, 512:1024], lhsT=sb["ind"][:, m * 16:(m + 1) * 16],
                             rhs=PB(YB + ysl + 512, 512), start=(m == 0), stop=(m == NTB - 1))
                    tick(e.matmul(out=ps_sc[0:16, 1024:1025], lhsT=sb["ind"][:, m * 16:(m + 1) * 16],
                                  rhs=PB(YB + ysl + 1024, 1), start=(m == 0), stop=(m == NTB - 1)))
                op("tensor", csm, reads=(pr(YB + ysl, 1025), rr("c_ind")),
                   writes=(PSC(0, 1536),), tag=f"cs{b}_{m}")
            op("vector", lambda e: tick(e.tensor_copy(
                out=g.pool[0:16, CSB:CSB + 1024], in_=ps_sc[0:16, 0:1024])),
               reads=(PSC(0, 1536),), writes=(PSC(0, 1536), pr(CSB, 1024)), tag=f"cse{b}")
            op("vector", lambda e: tick(e.tensor_copy(
                out=g.pool[0:16, CSB + 1024:CSB + 1025], in_=ps_sc[0:16, 1024:1025])),
               reads=(PSC(0, 1536),), writes=(pr(CSB + 1024, 1),), tag=f"cs2{b}")
            def spre(e):
                e.matmul(out=ps_cx[0:16, 0:512], lhsT=sb["t16"][:, :],
                         rhs=g.pool[0:16, CSB:CSB + 512], start=True, stop=True)
                e.matmul(out=ps_dm[0:16, 0:512], lhsT=sb["t16"][:, :],
                         rhs=g.pool[0:16, CSB + 512:CSB + 1024], start=True, stop=True)
                tick(e.matmul(out=ps_op[0:16, 0:1], lhsT=sb["t16"][:, :],
                              rhs=g.pool[0:16, CSB + 1024:CSB + 1025], start=True, stop=True))
            op("tensor", spre, reads=(pr(CSB, 1025), rr("c_t16")),
               writes=(PCX(), PDM(), POP(0, 1)), tag=f"sp{b}")
            def sev(e):
                e.tensor_copy(out=g.pool[0:16, CSB + 2048:CSB + 2048 + 512], in_=ps_cx[0:16, 0:512])
                e.tensor_copy(out=g.pool[0:16, CSB + 2560:CSB + 2560 + 512], in_=ps_dm[0:16, 0:512])
                tick(e.tensor_copy(out=g.pool[0:16, CSB + 3072:CSB + 3072 + 1], in_=ps_op[0:16, 0:1]))
            op("vector", sev, reads=(PCX(), PDM(), POP(0, 1)),
               writes=(PCX(), PDM(), POP(0, 1), pr(CSB + 2048, 1025)), tag=f"sev{b}")

            for m in range(NTB):
                ysl = (m % 2) * 2048
                def ymk2(e, m=m, ysl=ysl):
                    e.tensor_scalar(out=PB(YB + ysl, 1024), in0=PB(LNT + m * 1024, 1024),
                                    scalar1=sb["glob"][:, m:m + 1], scalar2=None,
                                    op0=ALU.mult)
                    e.drain()
                    tick(e.tensor_copy(out=PB(YB + ysl + 1024, 1), in_=sb["glob"][:, m:m + 1]))
                op("vector", ymk2, reads=(pr(LNT + m * 1024, 1024), rr("glob", 0, 16)),
                   writes=(pr(YB + ysl, 1025),), tag=f"y2{b}_{m}")
                def numm(e, m=m, ysl=ysl):
                    e.matmul(out=ps_sc[:, 0:512], lhsT=sb["ut"][:, :],
                             rhs=PB(YB + ysl, 512), start=True, stop=False)
                    e.matmul(out=ps_sc[:, 512:1024], lhsT=sb["ut"][:, :],
                             rhs=PB(YB + ysl + 512, 512), start=True, stop=False)
                    e.matmul(out=ps_sc[:, 1024:1025], lhsT=sb["ut"][:, :],
                             rhs=PB(YB + ysl + 1024, 1), start=True, stop=False)
                    e.matmul(out=ps_sc[:, 0:512], lhsT=sb["selm"][:, m * 128:(m + 1) * 128],
                             rhs=g.pool[0:16, CSB + 2048:CSB + 2048 + 512],
                             start=False, stop=True)
                    e.matmul(out=ps_sc[:, 512:1024], lhsT=sb["selm"][:, m * 128:(m + 1) * 128],
                             rhs=g.pool[0:16, CSB + 2560:CSB + 2560 + 512],
                             start=False, stop=True)
                    tick(e.matmul(out=ps_sc[:, 1024:1025], lhsT=sb["selm"][:, m * 128:(m + 1) * 128],
                                  rhs=g.pool[0:16, CSB + 3072:CSB + 3072 + 1],
                                  start=False, stop=True))
                op("tensor", numm,
                   reads=(pr(YB + ysl, 1025), pr(CSB + 2048, 1025), rr("c_ut"), rr("c_selm")),
                   writes=(PSC(0, 1536),), tag=f"nm{b}_{m}")
                t = m
                sl2 = (t % 2) * 2048
                op("sync", is_dma_op=True, emit=lambda e, b=b, t=t, sl2=sl2: dma(e.dma_start(
                    out=PF(XF3 + sl2, 1024),
                    in_=ap["xres"][(b * T + t * 128):(b * T + (t + 1) * 128), :])),
                   reads=(rr("xres", b * T + t * 128, b * T + (t + 1) * 128),),
                   writes=(prf(XF3 + sl2, 1024),), tag=f"a3x2_{b}_{t}")
                def resid(e, sl2=sl2):
                    e.reciprocal(out=sb["tiny"][:, 16:17], in_=ps_sc[:, 1024:1025])
                    e.drain()
                    tick(e.scalar_tensor_tensor(out=PF(XF3 + sl2, 1024),
                                                in0=ps_sc[:, 0:1024],
                                                scalar=sb["tiny"][:, 16:17],
                                                in1=PF(XF3 + sl2, 1024),
                                                op0=ALU.mult, op1=ALU.add))
                op("vector", resid,
                   reads=(PSC(0, 1536), prf(XF3 + sl2, 1024)),
                   writes=(PSC(0, 1536), prf(XF3 + sl2, 1024), rr("tiny", 16, 17)),
                   tag=f"a3d{b}_{m}")
                op("gpsimd", is_dma_op=True, emit=lambda e, b=b, t=t, sl2=sl2: dma(e.dma_start(
                    out=ap["xres"][(b * T + t * 128):(b * T + (t + 1) * 128), :],
                    in_=PF(XF3 + sl2, 1024))),
                   reads=(prf(XF3 + sl2, 1024),),
                   writes=(rr("xres", b * T + t * 128, b * T + (t + 1) * 128),),
                   tag=f"a3w{b}_{t}")

    if n_phases >= 3:
        l1_attn()

    XF4 = 0
    YS4 = 4096
    XQ8 = 6144
    W4A = 38912
    W4B = 55296
    GT4 = 71680
    UP4 = 73728
    GUF = 75776
    GUB = 79872
    WL4 = 83968
    WB4 = 88064
    W4D = 38912
    OB4 = 71680
    Q8S = 79872
    Q8T = 88064

    def l1_ffn():
        for wi, (wname, bname) in enumerate([("wg1", "wg1_b"), ("wu1", "wu1_b"),
                                             ("wd1", "wd1_b")]):
            nrow = ap[wname].shape[0]
            ncol = ap[wname].shape[1]
            ntile = nrow // 128
            nct = ncol // 1024
            ci = 0
            for it in range(ntile):
                for ct in range(nct):
                    sl = ci % 2
                    op("sync", is_dma_op=True, emit=lambda e, wname=wname, it=it, ct=ct, sl=sl: dma(
                        e.dma_start(out=PF(WL4 + sl * 2048, 1024),
                                    in_=ap[wname][it * 128:(it + 1) * 128, ct * 1024:(ct + 1) * 1024])),
                       reads=(rr(wname),), writes=(prf(WL4 + sl * 2048, 1024),),
                       tag=f"m4l{wi}_{ci}")
                    om_ = op("scalar", lambda e, sl=sl, ci=ci, wi=wi: tick(e.activation(
                        out=PF(YS4, 1024), in_=PF(WL4 + sl * 2048, 1024), func=AF.Abs,
                        accum_out=sb["colx"][:, 160 + (ci % 32):161 + (ci % 32)])),
                       reads=(prf(WL4 + sl * 2048, 1024),),
                       writes=(prf(YS4, 1024), rr("colx", 160 + (ci % 32), 161 + (ci % 32))),
                       tag=f"m4a{wi}_{ci}")
                    if ci % 2 == 1:
                        heat(3, after=om_)
                    if ci % 32 == 31:
                        op("vector", lambda e, wi=wi, first=(ci == 31): (
                            e.tensor_reduce(out=sb["tiny"][:, 18:19], in_=sb["colx"][:, 160:192],
                                            axis=mybir.AxisListType.X, op=ALU.add),
                            e.drain(),
                            tick(e.tensor_tensor(out=sb["tiny"][:, 19:20],
                                                 in0=sb["tiny"][:, 18:19],
                                                 in1=sb["tiny"][:, 19:20], op=ALU.add)
                                 if not first else
                                 e.tensor_copy(out=sb["tiny"][:, 19:20], in_=sb["tiny"][:, 18:19]))),
                           reads=(rr("colx", 160, 192), rr("tiny", 19, 20)) if ci != 31 else (rr("colx", 160, 192),),
                           writes=(rr("tiny", 18, 20),), tag=f"m4c{wi}_{ci}")
                    ci += 1
            op("gpsimd", lambda e, wi=wi: tick(e.tensor_reduce(
                out=sb["tcc"][0:1, 5 + wi:6 + wi], in_=sb["tiny"][:, 19:20],
                axis=mybir.AxisListType.C, op=ALU.add)),
               reads=(rr("tiny", 19, 20),), writes=(rr("tcc", 5 + wi, 6 + wi),),
               tag=f"m4g{wi}")
            nelem = float(nrow * ncol)
            op("tensor", lambda e, wi=wi: tick(e.matmul(
                out=ps_dm[:, 0:1], lhsT=sb["onesf"][0:1, :],
                rhs=sb["tcc"][0:1, 5 + wi:6 + wi], start=True, stop=True)),
               reads=(rr("tcc", 5 + wi, 6 + wi), rr("c_onesf")), writes=(PDM(),),
               tag=f"m4b{wi}")
            def mkw(e, wi=wi, nelem=nelem):
                e.tensor_scalar(out=sb["glob"][:, 5 + wi:6 + wi], in0=ps_dm[:, 0:1],
                                scalar1=1.0 / nelem, scalar2=1e-5,
                                op0=ALU.mult, op1=ALU.max)
                e.drain()
                tick(e.reciprocal(out=sb["cols"][:, 40 + wi:41 + wi],
                                  in_=sb["glob"][:, 5 + wi:6 + wi]))
            op("vector", mkw, reads=(PDM(),),
               writes=(PDM(), rr("glob", 5 + wi, 6 + wi), rr("cols", 40 + wi, 41 + wi)),
               tag=f"m4k{wi}")
            ci = 0
            for it in range(ntile):
                for ct in range(nct):
                    sl = ci % 2
                    op("sync", is_dma_op=True, emit=lambda e, wname=wname, it=it, ct=ct, sl=sl: dma(
                        e.dma_start(out=PF(WL4 + sl * 2048, 1024),
                                    in_=ap[wname][it * 128:(it + 1) * 128, ct * 1024:(ct + 1) * 1024])),
                       reads=(rr(wname),), writes=(prf(WL4 + sl * 2048, 1024),),
                       tag=f"t4l{wi}_{ci}")
                    wv = PF(WL4 + sl * 2048, 1024)
                    wvr = prf(WL4 + sl * 2048, 1024)
                    op("vector", lambda e, sl=sl, wi=wi, wv=wv: tick(e.tensor_scalar(
                        out=wv, in0=wv, scalar1=sb["cols"][:, 40 + wi:41 + wi],
                        scalar2=MAGIC, op0=ALU.mult, op1=ALU.add)),
                       reads=(wvr, rr("cols", 40 + wi, 41 + wi)),
                       writes=(wvr,), tag=f"t4ta{wi}_{ci}")
                    op("vector", lambda e, wv=wv: tick(e.tensor_scalar(
                        out=wv, in0=wv, scalar1=MAGIC, scalar2=1.0,
                        op0=ALU.subtract, op1=ALU.min)),
                       reads=(wvr,), writes=(wvr,), tag=f"t4tb{wi}_{ci}")
                    ot_ = op("vector", lambda e, sl=sl, wv=wv: tick(e.tensor_scalar(
                        out=PB(WB4 + sl * 1024, 1024), in0=wv,
                        scalar1=-1.0, scalar2=None, op0=ALU.max)),
                       reads=(wvr,), writes=(pr(WB4 + sl * 1024, 1024),),
                       tag=f"t4tc{wi}_{ci}")
                    heat(3, after=ot_)
                    op("gpsimd", is_dma_op=True, emit=lambda e, bname=bname, it=it, ct=ct, sl=sl: dma(
                        e.dma_start(out=ap[bname][it * 128:(it + 1) * 128, ct * 1024:(ct + 1) * 1024],
                                    in_=PB(WB4 + sl * 1024, 1024))),
                       reads=(pr(WB4 + sl * 1024, 1024),),
                       writes=(rr(bname, it * nct + ct, it * nct + ct + 1),),
                       tag=f"t4w{wi}_{ci}")
                    ci += 1

        for t in range(NT):
            sl2 = (t % 2) * 2048
            op("sync", is_dma_op=True, emit=lambda e, t=t, sl2=sl2: dma(e.dma_start(
                out=PF(XF4 + sl2, 1024), in_=ap["xres"][t * 128:(t + 1) * 128, :])),
               reads=(rr("xres", t * 128, (t + 1) * 128),),
               writes=(prf(XF4 + sl2, 1024),), tag=f"q8x{t}")

            def stats(e, t=t, sl2=sl2):
                e.bn_stats(out=sb["tiny"][:, 0:6], in_=PF(XF4 + sl2, 512))
                e.bn_stats(out=sb["tiny"][:, 6:12], in_=PF(XF4 + sl2 + 1024, 512))
                e.drain()
                e.bn_aggr(out=sb["tiny"][:, 12:14], in_=sb["tiny"][:, 0:12])
                e.drain()
                e.tensor_copy(out=sb["colm"][:, t:t + 1], in_=sb["tiny"][:, 12:13])
                tick(e.tensor_scalar(out=sb["colx"][:, 160 + t:161 + t],
                                     in0=sb["tiny"][:, 13:14],
                                     scalar1=EPS, scalar2=None, op0=ALU.add))
            op("vector", stats, reads=(prf(XF4 + sl2, 1024),),
               writes=(rr("tiny", 0, 15), rr("colm", t, t + 1),
                       rr("colx", 160 + t, 161 + t)), tag=f"q8s{t}")
            jx = PF(YS4, 1024) if t % 2 == 0 else PF(UP4, 1024)
            jn = PF(GT4, 1024) if t % 2 == 0 else PF(GUF, 1024)
            jxr = prf(YS4, 1024) if t % 2 == 0 else prf(UP4, 1024)
            jnr = prf(GT4, 1024) if t % 2 == 0 else prf(GUF, 1024)
            op("vector", lambda e, t=t, sl2=sl2, jx=jx: tick(e.tensor_scalar(
                out=jx, in0=PF(XF4 + sl2, 1024),
                scalar1=1.0, scalar2=-1e30, op0=ALU.mult, op1=ALU.max,
                accum_out=sb["dcol"][:, t:t + 1])),
               reads=(prf(XF4 + sl2, 1024),),
               writes=(jxr, rr("dcol", t, t + 1)), tag=f"q8gx{t}")
            og = op("vector", lambda e, t=t, sl2=sl2, jn=jn: tick(e.tensor_scalar(
                out=jn, in0=PF(XF4 + sl2, 1024),
                scalar1=1.0, scalar2=1e30, op0=ALU.mult, op1=ALU.min,
                accum_out=sb["dcol"][:, 32 + t:33 + t])),
               reads=(prf(XF4 + sl2, 1024),),
               writes=(jnr, rr("dcol", 32 + t, 33 + t)), tag=f"q8gn{t}")
            heat(3, after=og)
        op("scalar", lambda e: (
            e.activation(out=sb["colx"][:, 160:160 + NT],
                         in_=sb["colx"][:, 160:160 + NT], func=AF.Ln),
            e.drain(),
            tick(e.activation(out=sb["colr"][:, 0:NT], in_=sb["colx"][:, 160:160 + NT],
                              func=AF.Exp, scale=-0.5))),
           reads=(rr("colx", 160, 160 + NT),),
           writes=(rr("colx", 160, 160 + NT), rr("colr", 0, NT)), tag="q8rb")
        def q8cols(e):
            e.tensor_tensor(out=sb["dcol"][:, 64:96], in0=sb["dcol"][:, 0:32],
                            in1=sb["colm"][:, 0:NT], op=ALU.subtract)
            e.scalar_tensor_tensor(out=sb["dcol"][:, 96:128], in0=sb["dcol"][:, 32:64],
                                   scalar=-1.0, in1=sb["colm"][:, 0:NT],
                                   op0=ALU.mult, op1=ALU.add)
            e.drain()
            e.tensor_tensor(out=sb["dcol"][:, 64:96], in0=sb["dcol"][:, 64:96],
                            in1=sb["dcol"][:, 96:128], op=ALU.max)
            e.drain()
            e.tensor_tensor(out=sb["dcol"][:, 64:96], in0=sb["dcol"][:, 64:96],
                            in1=sb["colr"][:, 0:NT], op=ALU.mult)
            e.drain()
            e.tensor_scalar(out=sb["colx"][:, 0:32], in0=sb["dcol"][:, 64:96],
                            scalar1=1e-5, scalar2=None, op0=ALU.max)
            e.drain()
            e.reciprocal(out=sb["dcol"][:, 96:128], in_=sb["colx"][:, 0:32])
            e.drain()
            tick(e.scalar_tensor_tensor(out=sb["dcol"][:, 96:128],
                                        in0=sb["dcol"][:, 96:128], scalar=127.0,
                                        in1=sb["colr"][:, 0:NT],
                                        op0=ALU.mult, op1=ALU.mult))
        op("vector", q8cols, reads=(rr("dcol", 0, 64), rr("colm", 0, NT),
                                    rr("colr", 0, NT)),
           writes=(rr("dcol", 64, 128), rr("colx", 0, 32)), tag="q8cols")

        def q8_ldx(t):
            sl2 = (t % 2) * 2048
            op("sync", is_dma_op=True, emit=lambda e, t=t, sl2=sl2: dma(e.dma_start(
                out=PF(XF4 + sl2, 1024), in_=ap["xres"][t * 128:(t + 1) * 128, :])),
               reads=(rr("xres", t * 128, (t + 1) * 128),),
               writes=(prf(XF4 + sl2, 1024),), tag=f"q8xb{t}")

        def q8_step(t, k):
            sl2 = (t % 2) * 2048
            xt = PF(XF4 + sl2, 1024)
            yv = PF(YS4, 1024) if t % 2 == 0 else PF(GT4, 1024)
            ybase = ("pool", YS4, YS4 + 2048) if t % 2 == 0 else ("pool", GT4, GT4 + 2048)
            if k == 0:
                return op("vector", lambda e, t=t, xt=xt, yv=yv: tick(e.scalar_tensor_tensor(
                    out=yv, in0=xt, scalar=sb["colm"][:, t:t + 1],
                    in1=sb["dcol"][:, 96 + t:97 + t].broadcast_to([128, 1024]),
                    op0=ALU.subtract, op1=ALU.mult)),
                   reads=(prf(XF4 + sl2, 1024), rr("colm", t, t + 1),
                          rr("dcol", 96 + t, 97 + t)),
                   writes=(ybase,), tag=f"q8qa{t}")
            elif k == 1:
                return op("vector", lambda e, yv=yv: tick(e.tensor_scalar(
                    out=yv, in0=yv, scalar1=MAGIC, scalar2=MAGIC,
                    op0=ALU.add, op1=ALU.subtract)),
                   reads=(ybase,), writes=(ybase,), tag=f"q8qb{t}")
            else:
                return op("vector", lambda e, t=t, yv=yv: tick(e.tensor_scalar(
                    out=PB(WB4 + (t % 2) * 1024, 1024), in0=yv,
                    scalar1=127.0, scalar2=-128.0, op0=ALU.min, op1=ALU.max)),
                   reads=(ybase,), writes=(pr(WB4 + (t % 2) * 1024, 1024),),
                   tag=f"q8qc{t}")

        def q8_tev(t):
            def xqT(e, t=t):
                for dc in range(8):
                    i = e.transpose(out=pt_op[:, (t % 2) * 1024 + dc * 128:(t % 2) * 1024 + (dc + 1) * 128],
                                    in_=PB(WB4 + (t % 2) * 1024 + dc * 128, 128),
                                    identity=sb["id"][:, :])
                tick(i)
            op("tensor", xqT, reads=(pr(WB4 + (t % 2) * 1024, 1024), rr("c_id")),
               writes=(POP((t % 2) * 512, 512),), tag=f"q8T{t}")
            op("vector", lambda e, t=t: tick(e.tensor_copy(
                out=PB(XQ8, 32768).rearrange("p (dc n) -> p dc n", n=4096)[:, :, t * 128:(t + 1) * 128],
                in_=pt_op[:, (t % 2) * 1024:(t % 2) * 1024 + 1024].rearrange("p (dc n) -> p dc n", n=128))),
               reads=(POP((t % 2) * 512, 512),),
               writes=(POP((t % 2) * 512, 512),) + tuple(
                   pr(XQ8 + dc * 4096 + t * 128, 128) for dc in range(8)),
               tag=f"q8e{t}")

        for tp in range(NT // 2):
            t0, t1 = 2 * tp, 2 * tp + 1
            q8_ldx(t0)
            q8_ldx(t1)
            for k in range(3):
                ok_ = q8_step(t0, k)
                q8_step(t1, k)
                if k == 1:
                    heat(4, after=ok_)
            q8_tev(t0)
            q8_tev(t1)

        def mkoscale(e):
            e.scalar_tensor_tensor(out=sb["colx"][:, 32:64], in0=sb["colx"][:, 0:32],
                                   scalar=1.0 / 127.0,
                                   in1=sb["glob"][:, 5:6].broadcast_to([128, 32]),
                                   op0=ALU.mult, op1=ALU.mult)
            tick(e.scalar_tensor_tensor(out=sb["colx"][:, 64:96], in0=sb["colx"][:, 0:32],
                                        scalar=1.0 / 127.0,
                                        in1=sb["glob"][:, 6:7].broadcast_to([128, 32]),
                                        op0=ALU.mult, op1=ALU.mult))
        op("vector", mkoscale, reads=(rr("colx", 0, 32), rr("glob", 5, 7)),
           writes=(rr("colx", 32, 96),), tag="q8os")

        for hf in range(2):
            for dc in range(8):
                op("sync", is_dma_op=True, emit=lambda e, dc=dc, hf=hf: dma(
                    e.dma_start_transpose(out=PB(W4A + dc * 2048, 2048),
                                          in_=ap["wg1_b"][hf * 2048:(hf + 1) * 2048, dc * 128:(dc + 1) * 128])),
                   reads=(rr("wg1_b"),), writes=(pr(W4A + dc * 2048, 2048),),
                   tag=f"wgT4{hf}_{dc}")
                op("sync", is_dma_op=True, emit=lambda e, dc=dc, hf=hf: dma(
                    e.dma_start_transpose(out=PB(W4B + dc * 2048, 2048),
                                          in_=ap["wu1_b"][hf * 2048:(hf + 1) * 2048, dc * 128:(dc + 1) * 128])),
                   reads=(rr("wu1_b"),), writes=(pr(W4B + dc * 2048, 2048),),
                   tag=f"wuT4{hf}_{dc}")
            for t in range(NT):
                for ic8 in range(4):
                    psl = ic8 % 2
                    def gmm(e, t=t, ic8=ic8, psl=psl):
                        for dc in range(8):
                            i = e.matmul(out=ps_sc[:, psl * 512:(psl + 1) * 512],
                                         lhsT=PB(XQ8 + dc * 4096 + t * 128, 128),
                                         rhs=PB(W4A + dc * 2048 + ic8 * 512, 512),
                                         start=(dc == 0), stop=(dc == 7))
                        tick(i)
                    op("tensor", gmm, reads=(pr(XQ8, 32768), pr(W4A, 16384)),
                       writes=(PSC(psl * 512, 512),), tag=f"gm4{hf}_{t}_{ic8}")
                    op("scalar", lambda e, t=t, ic8=ic8, psl=psl: tick(e.activation(
                        out=PB(GT4 + ic8 * 512, 512),
                        in_=ps_sc[:, psl * 512:(psl + 1) * 512], func=AF.Silu,
                        scale=sb["colx"][:, 32 + t:33 + t])),
                       reads=(PSC(psl * 512, 512), rr("colx", 32, 64)),
                       writes=(PSC(psl * 512, 512), pr(GT4 + ic8 * 512, 512)),
                       tag=f"ge4{hf}_{t}_{ic8}")
                    def umm(e, t=t, ic8=ic8, psl=psl):
                        for dc in range(8):
                            i = e.matmul(out=ps_sc[:, 1024 + psl * 512:1024 + (psl + 1) * 512],
                                         lhsT=PB(XQ8 + dc * 4096 + t * 128, 128),
                                         rhs=PB(W4B + dc * 2048 + ic8 * 512, 512),
                                         start=(dc == 0), stop=(dc == 7))
                        tick(i)
                    op("tensor", umm, reads=(pr(XQ8, 32768), pr(W4B, 16384)),
                       writes=(PSC(1024 + psl * 512, 512),), tag=f"um4{hf}_{t}_{ic8}")
                    op("vector", lambda e, t=t, ic8=ic8, psl=psl: tick(e.tensor_scalar(
                        out=PB(UP4 + ic8 * 512, 512),
                        in0=ps_sc[:, 1024 + psl * 512:1024 + (psl + 1) * 512],
                        scalar1=sb["colx"][:, 64 + t:65 + t], scalar2=None,
                        op0=ALU.mult)),
                       reads=(PSC(1024 + psl * 512, 512), rr("colx", 64, 96)),
                       writes=(PSC(1024 + psl * 512, 512), pr(UP4 + ic8 * 512, 512)),
                       tag=f"ue4{hf}_{t}_{ic8}")
                gsl = t % 2
                op("vector", lambda e, t=t: tick(e.tensor_tensor(
                    out=PF(GUF, 2048), in0=PB(GT4, 2048),
                    in1=PB(UP4, 2048), op=ALU.mult)),
                   reads=(pr(GT4, 2048), pr(UP4, 2048)),
                   writes=(prf(GUF, 2048),), tag=f"gu4{hf}_{t}")
                mxdst = sb["colx"][:, 96 + t:97 + t] if hf == 0 else sb["dcol"][:, t:t + 1]
                mndst = sb["colx"][:, 128 + t:129 + t] if hf == 0 else sb["dcol"][:, 32 + t:33 + t]
                mxr = rr("colx", 96 + t, 97 + t) if hf == 0 else rr("dcol", t, t + 1)
                mnr = rr("colx", 128 + t, 129 + t) if hf == 0 else rr("dcol", 32 + t, 33 + t)
                op("vector", lambda e, t=t, gsl=gsl, mxdst=mxdst: tick(e.tensor_scalar(
                    out=PB(GUB + gsl * 2048, 2048), in0=PF(GUF, 2048),
                    scalar1=1.0, scalar2=-1e30, op0=ALU.mult, op1=ALU.max,
                    accum_out=mxdst)),
                   reads=(prf(GUF, 2048),),
                   writes=(pr(GUB + gsl * 2048, 2048), mxr), tag=f"gux{hf}_{t}")
                op("vector", lambda e, t=t, mndst=mndst: tick(e.tensor_scalar(
                    out=PB(YS4, 2048), in0=PF(GUF, 2048),
                    scalar1=1.0, scalar2=1e30, op0=ALU.mult, op1=ALU.min,
                    accum_out=mndst)),
                   reads=(prf(GUF, 2048),),
                   writes=(pr(YS4, 2048), mnr), tag=f"gun{hf}_{t}")
                op("gpsimd", is_dma_op=True, emit=lambda e, t=t, hf=hf, gsl=gsl: dma(
                    e.dma_start(out=ap["gu_sc"][t, :, hf * 2048:(hf + 1) * 2048],
                                in_=PB(GUB + gsl * 2048, 2048))),
                   reads=(pr(GUB + gsl * 2048, 2048),),
                   writes=(rr("gu_sc", t * 2 + hf, t * 2 + hf + 1),), tag=f"gs4{hf}_{t}")

        def bfold(e):
            e.tensor_tensor(out=sb["colx"][:, 96:128], in0=sb["colx"][:, 96:128],
                            in1=sb["dcol"][:, 0:32], op=ALU.max)
            e.tensor_tensor(out=sb["colx"][:, 128:160], in0=sb["colx"][:, 128:160],
                            in1=sb["dcol"][:, 32:64], op=ALU.min)
            e.drain()
            e.scalar_tensor_tensor(out=sb["colx"][:, 96:128], in0=sb["colx"][:, 128:160],
                                   scalar=-1.0, in1=sb["colx"][:, 96:128],
                                   op0=ALU.mult, op1=ALU.max)
            e.drain()
            e.tensor_scalar(out=sb["colx"][:, 96:128], in0=sb["colx"][:, 96:128],
                            scalar1=1e-5, scalar2=None, op0=ALU.max)
            e.drain()
            e.reciprocal(out=sb["dcol"][:, 64:96], in_=sb["colx"][:, 96:128])
            e.scalar_tensor_tensor(out=sb["dcol"][:, 96:128], in0=sb["colx"][:, 96:128],
                                   scalar=1.0 / 127.0,
                                   in1=sb["glob"][:, 7:8].broadcast_to([128, 32]),
                                   op0=ALU.mult, op1=ALU.mult)
            e.drain()
            tick(e.tensor_scalar(out=sb["dcol"][:, 64:96], in0=sb["dcol"][:, 64:96],
                                 scalar1=127.0, scalar2=None, op0=ALU.mult))
        op("vector", bfold, reads=(rr("colx", 96, 160), rr("dcol", 0, 64),
                                   rr("glob", 7, 8)),
           writes=(rr("colx", 96, 160), rr("dcol", 64, 128)), tag="bfold")

        for ic in range(I1 // 128):
            op("sync", is_dma_op=True, emit=lambda e, ic=ic: dma(
                e.dma_start_transpose(out=PB(W4D + ic * 1024, 1024),
                                      in_=ap["wd1_b"][:, ic * 128:(ic + 1) * 128])),
               reads=(rr("wd1_b"),), writes=(pr(W4D + ic * 1024, 1024),), tag=f"wdT4{ic}")
        for t in range(NT):
            gsl = t % 2
            op("sync", is_dma_op=True, emit=lambda e, t=t, gsl=gsl: dma(e.dma_start(
                out=PB(OB4 + gsl * 4096, 4096), in_=ap["gu_sc"][t])),
               reads=(rr("gu_sc", t * 2, t * 2 + 2),),
               writes=(pr(OB4 + gsl * 4096, 4096),), tag=f"gl4{t}")

            for k in range(3):
                for hh in range(2):
                    qv = PF(XQ8 + hh * 4096, 2048)
                    qvr = prf(XQ8 + hh * 4096, 2048)
                    if k == 0:
                        op("vector", lambda e, t=t, gsl=gsl, hh=hh, qv=qv: tick(
                            e.tensor_scalar(
                                out=qv, in0=PB(OB4 + gsl * 4096 + hh * 2048, 2048),
                                scalar1=sb["dcol"][:, 64 + t:65 + t], scalar2=MAGIC,
                                op0=ALU.mult, op1=ALU.add)),
                           reads=(pr(OB4 + gsl * 4096 + hh * 2048, 2048),
                                  rr("dcol", 64 + t, 65 + t)),
                           writes=(qvr,), tag=f"q24a{t}_{hh}")
                    elif k == 1:
                        op("vector", lambda e, qv=qv: tick(e.tensor_scalar(
                            out=qv, in0=qv, scalar1=MAGIC, scalar2=127.0,
                            op0=ALU.subtract, op1=ALU.min)),
                           reads=(qvr,), writes=(qvr,), tag=f"q24b{t}_{hh}")
                    else:
                        op("vector", lambda e, hh=hh, qv=qv: tick(e.tensor_scalar(
                            out=PB(Q8T + hh * 2048, 2048), in0=qv,
                            scalar1=-128.0, scalar2=None, op0=ALU.max)),
                           reads=(qvr,), writes=(pr(Q8T + hh * 2048, 2048),),
                           tag=f"q24c{t}_{hh}")
            for half in range(2):
                def trh(e, t=t, half=half):
                    for c in range(16):
                        i = e.transpose(out=pt_op[:, c * 128:(c + 1) * 128],
                                        in_=PB(Q8T + (half * 16 + c) * 128, 128),
                                        identity=sb["id"][:, :])
                    tick(i)
                op("tensor", trh, reads=(pr(Q8T, 4096), rr("c_id")),
                   writes=(POP(0, 1024),), tag=f"q2T{t}_{half}")
                op("vector", lambda e, t=t, half=half, gsl=gsl: tick(e.tensor_copy(
                    out=PB(Q8S + gsl * 4096 + half * 2048, 2048),
                    in_=pt_op[:, 0:2048])),
                   reads=(POP(0, 1024),),
                   writes=(POP(0, 1024), pr(Q8S + gsl * 4096 + half * 2048, 2048)),
                   tag=f"q2E{t}_{half}")

            def dmm4(e, t=t, gsl=gsl):
                for dd in range(2):
                    for ic in range(32):
                        i = e.matmul(out=ps_sc[:, dd * 512:(dd + 1) * 512],
                                     lhsT=PB(Q8S + gsl * 4096 + ic * 128, 128),
                                     rhs=PB(W4D + ic * 1024 + dd * 512, 512),
                                     start=(ic == 0), stop=(ic == 31))
                    tick(i)
            op("tensor", dmm4, reads=(pr(Q8S + gsl * 4096, 4096), pr(W4D, 32768)),
               writes=(PSC(0, 1024),), tag=f"dm4{t}")
            sl2 = (t % 2) * 2048
            op("sync", is_dma_op=True, emit=lambda e, t=t, sl2=sl2: dma(e.dma_start(
                out=PF(XF4 + sl2, 1024), in_=ap["xres"][t * 128:(t + 1) * 128, :])),
               reads=(rr("xres", t * 128, (t + 1) * 128), prf(XF4, 4096)),
               writes=(prf(XF4 + sl2, 1024),), tag=f"q8x2_{t}")
            op("vector", lambda e, t=t, sl2=sl2: tick(e.scalar_tensor_tensor(
                out=PF(XF4 + sl2, 1024), in0=ps_sc[:, 0:1024],
                scalar=sb["dcol"][:, 96 + t:97 + t], in1=PF(XF4 + sl2, 1024),
                op0=ALU.mult, op1=ALU.add)),
               reads=(PSC(0, 1024), prf(XF4 + sl2, 1024), rr("dcol", 96 + t, 97 + t)),
               writes=(PSC(0, 1024), prf(XF4 + sl2, 1024)),
               tag=f"dr4{t}")
            op("gpsimd", is_dma_op=True, emit=lambda e, t=t, sl2=sl2: dma(e.dma_start(
                out=ap["out"].rearrange("b tt d -> (b tt) d")[t * 128:(t + 1) * 128, :],
                in_=PF(XF4 + sl2, 1024))),
               reads=(prf(XF4 + sl2, 1024),),
               writes=(rr("OUT", t * 128, (t + 1) * 128),), tag=f"ow4{t}")

    if n_phases >= 4:
        l1_ffn()

    if n_phases < 4:
        op("sync", is_dma_op=True, emit=lambda e: dma(e.dma_start(
            out=ap["out"].rearrange("b t d -> (b t) d"), in_=ap["xres"])),
           reads=(rr("xres"),), writes=(rr("OUT"),), tag="out")
    op("sync", emit=lambda e: None, reads=(rr("OUT"),), writes=(), tag="fin")


def _consts():
    idb = np.eye(128).astype(BF)
    idf = np.eye(128).astype(np.float32)
    ones = np.ones((128, 128), BF)
    ut = np.triu(np.ones((128, 128))).astype(BF)
    t16 = (np.arange(16)[:, None] < np.arange(16)[None, :]).astype(BF)
    ind = np.zeros((128, 256), BF)
    for m in range(16):
        ind[:, m * 16 + m] = 1
    selm = np.zeros((16, 2048), BF)
    for m in range(16):
        selm[m, m * 128:(m + 1) * 128] = 1
    return dict(id_bf=idb, id_f=idf, ones_bf=ones, ut128=ut, t16c=t16, ind16=ind,
                ones_f=np.ones((1, 128), np.float32), selm16=selm)


def kernel(L0_attn_x, L0_attn_w_q, L0_attn_w_kv, L0_attn_w_out, L0_attn_b_out,
           L0_ffn_w_gate, L0_ffn_w_up, L0_ffn_w_down, L1_attn_W1, L1_attn_b1_,
           L1_attn_W2, L1_attn_b2_, L1_ffn_w_gate, L1_ffn_w_up, L1_ffn_w_down):
    x = np.ascontiguousarray(L0_attn_x)
    shared = dict(
        wq=np.ascontiguousarray(L0_attn_w_q), wkv=np.ascontiguousarray(L0_attn_w_kv),
        wout=np.ascontiguousarray(L0_attn_w_out),
        bout=np.ascontiguousarray(L0_attn_b_out).reshape(1, D),
        wg0=np.ascontiguousarray(L0_ffn_w_gate), wu0=np.ascontiguousarray(L0_ffn_w_up),
        wd0=np.ascontiguousarray(L0_ffn_w_down), w1=np.ascontiguousarray(L1_attn_W1),
        b1=np.ascontiguousarray(L1_attn_b1_).reshape(1, D),
        w2=np.ascontiguousarray(L1_attn_W2).reshape(1, D),
        b2=np.ascontiguousarray(L1_attn_b2_).reshape(1, 1),
        wg1=np.ascontiguousarray(L1_ffn_w_gate), wu1=np.ascontiguousarray(L1_ffn_w_up),
        wd1=np.ascontiguousarray(L1_ffn_w_down), **_consts())
    nc = build_nc(N_PHASES)
    in_maps = []
    for ci in range(8):
        m = dict(shared)
        m["x_in"] = x[2 * ci:2 * ci + 2]
        in_maps.append(m)
    res = run_bass_kernel_spmd(nc, in_maps, core_ids=list(range(8)))
    return np.concatenate([r["out"] for r in res.results], axis=0)


# version 06
# speedup vs baseline: 1.0063x; 1.0063x over previous
import numpy as np
import ml_dtypes
from contextlib import ExitStack

import concourse.bass as bass
import concourse.mybir as mybir
from concourse.bass_utils import run_bass_kernel_spmd

F32 = mybir.dt.float32
BF16 = mybir.dt.bfloat16
AF = mybir.ActivationFunctionType
ALU = mybir.AluOpType
BF = ml_dtypes.bfloat16

MAGIC = float(np.float32(1.5 * 2 ** 23))
B_PC = 2
T = 2048
D = 1024
TOK = B_PC * T
NT = TOK // 128
NTB = T // 128
H = 16
DH = 64
I0 = 3072
I1 = 4096
EPS = 1e-5
VW = 65
VEW = H * VW + 63
PCOLS = 100288

ENGINES = ("sync", "tensor", "vector", "scalar", "gpsimd")

N_PHASES = 4


class Op:
    __slots__ = ("eng", "emit", "post", "idx", "ord", "is_dma", "deps", "tag",
                 "dsem", "dval", "dprev", "_demit", "is_coll")

    def __init__(self, eng, emit, idx, tag):
        self.eng = eng
        self.emit = emit
        self.idx = idx
        self.post = None
        self.ord = 0
        self.is_dma = False
        self.deps = set()
        self.tag = tag
        self.dsem = None
        self.dval = 0
        self.dprev = 0
        self._demit = 0
        self.is_coll = False


NDPOOL = 40


class Sched:
    def __init__(self, nc, ctx):
        self.nc = nc
        self.ops = []
        self.sems = {e: ctx.enter_context(nc.semaphore("prog_" + e)) for e in ENGINES}
        self.cnt = {e: 0 for e in ENGINES}
        self._cur = None
        self.segs = {}
        self.dpool = []
        self.dpool_of = {}
        for e in ("sync", "gpsimd", "scalar", "vector", "tensor"):
            npool = {"sync": 20, "gpsimd": 14}.get(e, 2)
            base = len(self.dpool)
            self.dpool += [ctx.enter_context(nc.semaphore(f"dma_{e}{i}")) for i in range(npool)]
            self.dpool_of[e] = (base, npool)
        self.dcnt = [0] * len(self.dpool)
        self.dhist = [[] for _ in self.dpool]
        self.dseq = {e: 0 for e in ENGINES}
        self.ordn = {e: 0 for e in ENGINES}

    def _touch(self, o, ranges, is_write):
        for (sp, lo, hi) in ranges:
            lst = self.segs.setdefault(sp, [])
            if is_write:
                keep = []
                for seg in lst:
                    if seg[0] < hi and lo < seg[1]:
                        if seg[2] is not None:
                            o.deps.add(seg[2])
                        for r in seg[3]:
                            o.deps.add(r)
                        if seg[0] < lo:
                            keep.append([seg[0], min(seg[1], lo), seg[2], list(seg[3])])
                        if seg[1] > hi:
                            keep.append([max(seg[0], hi), seg[1], seg[2], list(seg[3])])
                    else:
                        keep.append(seg)
                keep.append([lo, hi, o, []])
                self.segs[sp] = keep
            else:
                for seg in lst:
                    if seg[0] < hi and lo < seg[1]:
                        if seg[2] is not None:
                            o.deps.add(seg[2])
                        seg[3].append(o)

    def op(self, eng, emit, reads=(), writes=(), tag="", is_dma_op=False, ndma=1,
           is_coll=False):
        o = Op(eng, emit, len(self.ops), tag)
        o.ord = self.ordn[eng]
        self.ordn[eng] += 1
        o.is_coll = is_coll
        if is_dma_op:
            o.is_dma = True
            base, npool = self.dpool_of[eng]
            o.dsem = base + self.dseq[eng] % npool
            self.dseq[eng] += 1
            sidx = o.dsem
            o.dprev = self.dcnt[sidx]
            self.dcnt[sidx] += 16 * ndma
            o.dval = self.dcnt[sidx]
            self.dhist[sidx].append((o.idx, o.dval))
        self._touch(o, reads, False)
        self._touch(o, writes, True)
        o.deps.discard(o)
        for d in o.deps:
            assert d.idx < o.idx, (d.idx, o.idx, tag, d.tag)
        self.ops.append(o)
        return o

    def emit_all(self, block):
        import bisect
        watched_ord = {e: {f: -1 for f in ENGINES} for e in ENGINES}
        dwatched = {e: [0] * len(self.dpool) for e in ENGINES}
        fixups = []

        by_eng = {e: [] for e in ENGINES}
        for o in self.ops:
            by_eng[o.eng].append(o)

        def mk_emit(_run, _eng):
            def emit_run(e):
                for o in _run:
                    for d in sorted(o.deps, key=lambda d: d.idx):
                        if d.is_dma:
                            hist = self.dhist[d.dsem]
                            k = bisect.bisect_left(hist, (o.idx, -1)) - 1
                            val = hist[k][1]
                            if val > dwatched[_eng][d.dsem]:
                                e.wait_ge(self.dpool[d.dsem], val)
                                dwatched[_eng][d.dsem] = val
                        elif d.eng != _eng or d.is_coll or _eng != "tensor":
                            if d.ord > watched_ord[_eng][d.eng]:
                                if d.post is not None:
                                    e.wait_ge(self.sems[d.eng], d.post)
                                else:
                                    fixups.append(
                                        (e.wait_ge(self.sems[d.eng], 1), d))
                                watched_ord[_eng][d.eng] = d.ord
                    if o.is_dma and o.dprev > 0 and o.dprev > dwatched[_eng][o.dsem]:
                        e.wait_ge(self.dpool[o.dsem], o.dprev)
                        dwatched[_eng][o.dsem] = o.dprev
                    self._cur = o
                    o.emit(e)
                    self._cur = None
                    o.post = self.cnt[_eng]
            return emit_run

        for eng in ENGINES:
            if by_eng[eng]:
                getattr(block, eng)(mk_emit(by_eng[eng], eng))

        for inst, d in fixups:
            assert d.post is not None, d.tag
            ws = inst.ins.sync_info.on_wait
            assert len(ws) == 1, (d.tag, ws)
            ws[0].wait_value = d.post

    def tick(self, inst, n=1):
        o = self._cur
        self.cnt[o.eng] += n
        inst.then_inc(self.sems[o.eng], n)
        return inst

    def dma(self, inst):
        o = self._cur
        assert o.is_dma, f"op {o.tag} must be created with is_dma_op=True"
        o._demit += 16
        assert o.dprev + o._demit <= o.dval, f"too many dma_starts in {o.tag}"
        inst.then_inc(self.dpool[o.dsem], 16)
        return inst


class NS:
    def __init__(self, d):
        self.__dict__.update(d)


def build_nc(n_phases=4):
    nc = bass.Bass("TRN2", target_bir_lowering=False)
    ap = {}

    def DI(name, shape, dt=F32):
        ap[name] = nc.dram_tensor(name, shape, dt, kind="ExternalInput").ap()

    DI("x_in", [B_PC, T, D]); DI("wq", [D, H * DH]); DI("wkv", [D, 2 * H * DH])
    DI("wout", [H * DH, D]); DI("bout", [1, D])
    DI("wg0", [I0, D]); DI("wu0", [I0, D]); DI("wd0", [D, I0])
    DI("w1", [D, D]); DI("b1", [1, D]); DI("w2", [1, D]); DI("b2", [1, 1])
    DI("wg1", [I1, D]); DI("wu1", [I1, D]); DI("wd1", [D, I1])
    DI("id_bf", [128, 128], BF16); DI("id_f", [128, 128], F32)
    DI("ones_bf", [128, 128], BF16); DI("ut128", [128, 128], BF16)
    DI("t16c", [16, 16], BF16); DI("ind16", [128, 256], BF16)
    DI("ones_f", [1, 128], F32); DI("selm16", [16, 2048], BF16)
    ap["out"] = nc.dram_tensor("out", [B_PC, T, D], F32, kind="ExternalOutput").ap()

    def DS(name, shape, dt):
        ap[name] = nc.dram_tensor(name, shape, dt).ap()

    DS("xres", [TOK, D], F32)
    DS("wq_b", [D, H * DH], BF16); DS("wkv_b", [D, 2 * H * DH], BF16)
    DS("wout_b", [H * DH, D], BF16); DS("bout_b", [1, D], BF16)
    DS("w1_b", [D, D], BF16)
    DS("wg0_b", [I0, D], BF16); DS("wu0_b", [I0, D], BF16); DS("wd0_b", [D, I0], BF16)
    DS("wg1_b", [I1, D], BF16); DS("wu1_b", [I1, D], BF16); DS("wd1_b", [D, I1], BF16)
    DS("up_sc", [I0 // 128, 128, TOK], BF16)
    DS("gu_sc", [NT, 128, I1], BF16)
    DS("den_hb", [64, 512], BF16)
    DS("rden_hb", [256, 128], BF16)
    DS("gate_sc", [I0 // 128, 128, TOK], BF16)
    for i in range(1, 6):
        DS(f"cci{i}", [1, 16], F32)
        DS(f"cco{i}", [1, 16], F32)

    with ExitStack() as ctx0:
        sch = Sched(nc, ctx0)
        S = lambda n, sh, dt: ctx0.enter_context(nc.sbuf_tensor(n, sh, dt))
        P = lambda n, sh, dt: ctx0.enter_context(nc.psum_tensor(n, sh, dt))

        sb = {}
        for n, shdt in [("id", ([128, 128], BF16)), ("idf", ([128, 128], F32)),
                        ("ones", ([128, 128], BF16)), ("ut", ([128, 128], BF16)),
                        ("t16", ([16, 16], BF16)), ("ind", ([128, 256], BF16)),
                        ("colm", ([128, NT], F32)), ("colr", ([128, NT], F32)),
                        ("cols", ([128, 64], F32)), ("colx", ([128, 192], F32)),
                        ("glob", ([128, 16], F32)), ("cbias", ([128, 8], F32)),
                        ("dcol", ([128, 256], F32)), ("dcolb", ([128, 256], BF16)),
                        ("drows", ([128, 256], BF16)), ("dstage", ([128, 1024], BF16)),
                        ("tiny", ([128, 32], F32)), ("tcc", ([1, 16], F32)),
                        ("tcc2", ([128, 16], F32)), ("onesf", ([1, 128], F32)),
                        ("selm", ([16, 2048], BF16)), ("b2sb", ([1, 2], F32))]:
            sb[n] = S("sb_" + n, *shdt)

        pool = S("pool", [128, PCOLS], BF16)
        poolf = pool[:, :].bitcast(F32)

        def PB(c0, n):
            return pool[:, c0:c0 + n]

        def PBr(r0, r1, c0, n):
            return pool[r0:r1, c0:c0 + n]

        def PF(c0, n):
            return poolf[:, c0 // 2: c0 // 2 + n]

        ps_sc = P("ps_sc", [128, 2048], F32)
        ps_cx = P("ps_cx", [128, 512], F32)
        ps_dm = P("ps_dm", [128, 512], F32)
        ps_op = P("ps_op", [128, 1024], F32)
        pt_op = ps_op[:, :].bitcast(BF16)

        block = ctx0.enter_context(nc.Block())
        env = NS(dict(nc=nc, ap=ap, sb=sb, pool=pool, poolf=poolf, PB=PB,
                      PBr=PBr, PF=PF, ps_sc=ps_sc, ps_cx=ps_cx, ps_dm=ps_dm,
                      ps_op=ps_op, pt_op=pt_op, sch=sch))
        build_program(env, n_phases)
        sch.emit_all(block)
    return nc


def build_program(g, n_phases):
    sch = g.sch
    ap, sb = g.ap, g.sb
    PB, PBr, PF = g.PB, g.PBr, g.PF
    ps_sc, ps_cx, ps_dm, ps_op, pt_op = g.ps_sc, g.ps_cx, g.ps_dm, g.ps_op, g.pt_op
    op, tick, dma = sch.op, sch.tick, sch.dma

    def pr(c0, n):
        return ("pool", c0, c0 + n)

    def prf(c0, n):
        return ("pool", c0, c0 + 2 * n)

    def rr(name, lo=0, hi=1 << 30):
        return (name, lo, hi)

    PSC = lambda lo, n: ("ps_sc", lo, lo + n)
    PCX = lambda: ("ps_cx", 0, 512)
    PDM = lambda: ("ps_dm", 0, 512)
    POP = lambda lo, n: ("ps_op", lo, lo + n)

    _nheat = [0]

    def heat(n, after=(), reads=()):
        def eh(e):
            mm = None
            for _ in range(n):
                mm = e.matmul(out=g.ps_cx[:, 0:512], lhsT=sb["idf"][:, :],
                              rhs=sb["idf"][:, 0:1].broadcast_to([128, 512]),
                              start=True, stop=True)
            tick(mm)
        o = op("tensor", eh, reads=(rr("c_idf"),) + tuple(reads),
               writes=(PCX(),), tag=f"heat{_nheat[0]}")
        _nheat[0] += 1
        for a in (after if isinstance(after, (list, tuple)) else (after,)):
            if a is not None and a.idx < o.idx:
                o.deps.add(a)
        return o

    for dst, src in [("id", "id_bf"), ("idf", "id_f"), ("ones", "ones_bf"),
                     ("ut", "ut128"), ("t16", "t16c"), ("ind", "ind16"),
                     ("onesf", "ones_f"), ("selm", "selm16")]:
        op("sync", is_dma_op=True, emit=lambda e, d=dst, s=src: dma(e.dma_start(out=sb[d][:, :], in_=ap[s])),
           reads=(rr(src),), writes=(rr("c_" + dst),), tag="const")
    for dst, src in [("wq_b", "wq"), ("wkv_b", "wkv"), ("wout_b", "wout"),
                     ("bout_b", "bout"), ("w1_b", "w1")]:
        op("gpsimd", is_dma_op=True, emit=lambda e, d=dst, s=src: dma(e.dma_start(out=ap[d], in_=ap[s])),
           reads=(rr(src),), writes=(rr(dst),), tag="cast")
    op("vector", lambda e: (e.memset(sb["cbias"][:, 0:1], -12.0),
                            e.memset(sb["cbias"][:, 1:2], 0.0),
                            e.memset(sb["cbias"][:, 2:3], MAGIC),
                            tick(e.memset(sb["cbias"][:, 3:4], -MAGIC))),
       writes=(rr("cbias", 0, 4),), tag="cbias")
    op("vector", lambda e: tick(e.memset(sb["tcc"][0:1, :], 0.0)),
       writes=(rr("tcc", 0, 16),), tag="tccz")

    XA = 0
    LN = 4096
    QP = 20480
    KF = 53248
    VE = 69632
    EB = 87296
    WS = 89344
    LB = 93440
    CX = 0
    WO = 53248
    XD = 69632
    BO = 87296

    def l0_attn():
        for b in range(B_PC):
            op("vector", lambda e: tick(e.memset(PB(QP, 32768), 0.0)),
               writes=(pr(QP, 32768),), tag=f"zq{b}")

            def vext_fill(e):
                e.tensor_copy(
                    out=PB(VE, NTB * VEW).rearrange("p (kt w) -> p kt w", w=VEW)[:, :, H * VW - 1:],
                    in_=sb["ones"][:, 0:1].unsqueeze(2).broadcast_to([128, NTB, 64]))
                e.drain()
                vv = PB(VE, NTB * VEW).rearrange("p (kt w) -> p kt w", w=VEW)[:, :, 0:H * VW]
                vv = vv.rearrange("p kt (hh w) -> p kt hh w", w=VW)
                tick(e.tensor_copy(out=vv[:, :, :, 64:65],
                                   in_=sb["ones"][:, 0:1].unsqueeze(2).unsqueeze(3).broadcast_to([128, NTB, H, 1])))
            op("vector", vext_fill, reads=(rr("c_ones"),),
               writes=(pr(VE, NTB * VEW),), tag=f"vones{b}")

            for t in range(NTB):
                ti = b * NTB + t
                sl2 = (t % 2) * 2048
                op("sync", is_dma_op=True, emit=lambda e, b=b, t=t, sl2=sl2: dma(e.dma_start(
                    out=PF(XA + sl2, 1024),
                    in_=ap["x_in"][b, t * 128:(t + 1) * 128, :])),
                   reads=(rr("x_in"),), writes=(prf(XA + sl2, 1024),), tag=f"xa{ti}")

                def stats(e, ti=ti, t=t, sl2=sl2):
                    e.bn_stats(out=sb["tiny"][:, 0:6], in_=PF(XA + sl2, 512))
                    e.bn_stats(out=sb["tiny"][:, 6:12], in_=PF(XA + sl2 + 1024, 512))
                    e.drain()
                    e.bn_aggr(out=sb["tiny"][:, 12:14], in_=sb["tiny"][:, 0:12])
                    e.drain()
                    e.tensor_copy(out=sb["colm"][:, ti:ti + 1], in_=sb["tiny"][:, 12:13])
                    tick(e.tensor_scalar(out=sb["colx"][:, 160 + t:161 + t],
                                         in0=sb["tiny"][:, 13:14],
                                         scalar1=EPS, scalar2=None, op0=ALU.add))
                ost = op("vector", stats, reads=(prf(XA + sl2, 1024),),
                   writes=(rr("tiny", 0, 15), rr("colm", ti, ti + 1),
                           rr("colx", 160 + t, 161 + t)), tag=f"st{ti}")
                heat(2, after=ost)
            op("scalar", lambda e, b=b: (
                e.activation(out=sb["colx"][:, 160:160 + NTB], in_=sb["colx"][:, 160:160 + NTB], func=AF.Ln),
                e.drain(),
                tick(e.activation(out=sb["colr"][:, b * NTB:(b + 1) * NTB],
                                  in_=sb["colx"][:, 160:160 + NTB], func=AF.Exp, scale=-0.5))),
               reads=(rr("colx", 160, 160 + NTB),),
               writes=(rr("colx", 160, 160 + NTB), rr("colr", b * NTB, (b + 1) * NTB)),
               tag=f"rsb{b}")
            for t in range(NTB):
                ti = b * NTB + t
                sl2 = (t % 2) * 2048
                op("sync", is_dma_op=True, emit=lambda e, b=b, t=t, sl2=sl2: dma(e.dma_start(
                    out=PF(XA + sl2, 1024),
                    in_=ap["x_in"][b, t * 128:(t + 1) * 128, :])),
                   reads=(rr("x_in"),), writes=(prf(XA + sl2, 1024),), tag=f"xb{ti}")

                def lnb(e, ti=ti, sl2=sl2):
                    tick(e.tensor_scalar(out=PB(LB + (ti % 2) * 1024, 1024),
                                         in0=PF(XA + sl2, 1024),
                                         scalar1=sb["colm"][:, ti:ti + 1],
                                         scalar2=sb["colr"][:, ti:ti + 1],
                                         op0=ALU.subtract, op1=ALU.mult))
                op("vector", lnb,
                   reads=(prf(XA + sl2, 1024), rr("colm", ti, ti + 1), rr("colr", ti, ti + 1)),
                   writes=(pr(LB + (ti % 2) * 1024, 1024),), tag=f"lnb{ti}")

                def lnT(e, ti=ti):
                    sl = (ti % 2) * 1024
                    for dc in range(8):
                        i = e.transpose(out=pt_op[:, sl + dc * 128: sl + (dc + 1) * 128],
                                        in_=PB(LB + (ti % 2) * 1024 + dc * 128, 128),
                                        identity=sb["id"][:, :])
                    tick(i)
                op("tensor", lnT,
                   reads=(pr(LB + (ti % 2) * 1024, 1024), rr("c_id")),
                   writes=(POP((ti % 2) * 512, 512),), tag=f"lnT{ti}")

                def lnE(e, ti=ti, t=t):
                    sl = (ti % 2) * 1024
                    tick(e.tensor_copy(
                        out=PB(LN, 16384).rearrange("p (dc n) -> p dc n", n=2048)[:, :, t * 128:(t + 1) * 128],
                        in_=pt_op[:, sl:sl + 1024].rearrange("p (dc n) -> p dc n", n=128)))
                op("vector", lnE, reads=(POP((ti % 2) * 512, 512),),
                   writes=(POP((ti % 2) * 512, 512),) + tuple(
                       pr(LN + dc * 2048 + t * 128, 128) for dc in range(8)),
                   tag=f"lnE{ti}")

            NQG, NKG = H, H // 2
            ev = 0
            for li in range(NQG + NKG + 2):
                wsl = WS + (li % 2) * 2048
                if li < NQG:
                    h = li
                    op("sync", is_dma_op=True, emit=lambda e, h=h, wsl=wsl: dma(e.dma_start(
                        out=PB(wsl, 512).rearrange("p (dc n) -> p dc n", n=64),
                        in_=ap["wq_b"][:, h * 64:(h + 1) * 64].rearrange("(dc p) n -> p dc n", p=128))),
                       reads=(rr("wq_b"),), writes=(pr(wsl, 512),), tag=f"wq{b}_{h}")
                    for tc in range(4):
                        sl = ev % 4
                        def mmq(e, h=h, tc=tc, sl=sl, wsl=wsl):
                            for dc in range(8):
                                i = e.matmul(
                                    out=ps_sc[(h % 2) * 64:(h % 2) * 64 + 64, sl * 512:(sl + 1) * 512],
                                    lhsT=PB(wsl + dc * 64, 64),
                                    rhs=PB(LN + dc * 2048 + tc * 512, 512),
                                    start=(dc == 0), stop=(dc == 7),
                                    tile_position=(0, (h % 2) * 64))
                            tick(i)
                        op("tensor", mmq, reads=(pr(wsl, 512), pr(LN, 16384)),
                           writes=(PSC(sl * 512, 512),), tag=f"mmq{b}_{h}_{tc}")
                        def evq(e, h=h, tc=tc, sl=sl):
                            tick(e.tensor_copy(
                                out=PBr((h % 2) * 64, (h % 2) * 64 + 64,
                                        QP + h * 2048 + tc * 512, 512),
                                in_=ps_sc[(h % 2) * 64:(h % 2) * 64 + 64, sl * 512:(sl + 1) * 512]))
                        op("vector", evq,
                           reads=(PSC(sl * 512, 512), pr(QP + h * 2048 + tc * 512, 512)),
                           writes=(PSC(sl * 512, 512), pr(QP + h * 2048 + tc * 512, 512)),
                           tag=f"evq{b}_{h}_{tc}")
                        ev += 1
                elif li < NQG + NKG:
                    hp = li - NQG
                    op("sync", is_dma_op=True, emit=lambda e, hp=hp, wsl=wsl: dma(e.dma_start(
                        out=PB(wsl, 1024).rearrange("p (dc n) -> p dc n", n=128),
                        in_=ap["wkv_b"][:, hp * 128:(hp + 1) * 128].rearrange("(dc p) n -> p dc n", p=128))),
                       reads=(rr("wkv_b"),), writes=(pr(wsl, 1024),), tag=f"wk{b}_{hp}")
                    for tc in range(4):
                        sl = ev % 4
                        def mmk(e, tc=tc, sl=sl, wsl=wsl):
                            for dc in range(8):
                                i = e.matmul(
                                    out=ps_sc[:, sl * 512:(sl + 1) * 512],
                                    lhsT=PB(wsl + dc * 128, 128),
                                    rhs=PB(LN + dc * 2048 + tc * 512, 512),
                                    start=(dc == 0), stop=(dc == 7))
                            tick(i)
                        op("tensor", mmk, reads=(pr(wsl, 1024), pr(LN, 16384)),
                           writes=(PSC(sl * 512, 512),), tag=f"mmk{b}_{hp}_{tc}")
                        def evk(e, hp=hp, tc=tc, sl=sl):
                            tick(e.tensor_copy(
                                out=PB(KF + hp * 2048 + tc * 512, 512),
                                in_=ps_sc[:, sl * 512:(sl + 1) * 512]))
                        op("vector", evk, reads=(PSC(sl * 512, 512),),
                           writes=(PSC(sl * 512, 512), pr(KF + hp * 2048 + tc * 512, 512)),
                           tag=f"evk{b}_{hp}_{tc}")
                        ev += 1
                else:
                    vc = li - NQG - NKG
                    op("sync", is_dma_op=True, emit=lambda e, vc=vc: dma(e.dma_start(
                        out=PB(WS, 4096).rearrange("p (dc n) -> p dc n", n=512),
                        in_=ap["wkv_b"][:, 1024 + vc * 512:1024 + (vc + 1) * 512].rearrange("(dc p) n -> p dc n", p=128))),
                       reads=(rr("wkv_b"),), writes=(pr(WS, 4096),), tag=f"wv{b}_{vc}")
                    for kt in range(NTB):
                        sl = ev % 4
                        def mmv(e, kt=kt, sl=sl):
                            for dc in range(8):
                                i = e.matmul(
                                    out=ps_sc[:, sl * 512:(sl + 1) * 512],
                                    lhsT=PB(LN + dc * 2048 + kt * 128, 128),
                                    rhs=PB(WS + dc * 512, 512),
                                    start=(dc == 0), stop=(dc == 7))
                            tick(i)
                        op("tensor", mmv, reads=(pr(WS, 4096), pr(LN, 16384)),
                           writes=(PSC(sl * 512, 512),), tag=f"mmv{b}_{vc}_{kt}")
                        def evv(e, kt=kt, vc=vc, sl=sl):
                            vv = PB(VE + kt * VEW + vc * 8 * VW, 8 * VW)
                            vv = vv.rearrange("p (hh w) -> p hh w", w=VW)[:, :, 0:64]
                            tick(e.tensor_copy(
                                out=vv,
                                in_=ps_sc[:, sl * 512:(sl + 1) * 512].rearrange("p (hh n) -> p hh n", n=64)))
                        op("vector", evv, reads=(PSC(sl * 512, 512),),
                           writes=(PSC(sl * 512, 512),
                                   pr(VE + kt * VEW + vc * 8 * VW, 8 * VW)),
                           tag=f"evv{b}_{vc}_{kt}")
                        ev += 1

            for h in range(H):
                hp = h // 2
                for qc in range(4):
                    qi = (b * H + h) * 4 + qc
                    cxb = ps_cx if qi % 2 == 0 else ps_dm
                    cxr = PCX() if qi % 2 == 0 else PDM()
                    def mk_qk(p):
                        gs = p % 2
                        def qk(e, hp=hp, h=h, qc=qc, p=p, gs=gs):
                            for j in range(2):
                                kt = 2 * p + j
                                i = e.matmul(
                                    out=ps_sc[:, gs * 1024 + j * 512: gs * 1024 + (j + 1) * 512],
                                    lhsT=PB(KF + hp * 2048 + kt * 128, 128),
                                    rhs=PB(QP + h * 2048 + qc * 512, 512),
                                    start=True, stop=True)
                            tick(i)
                        op("tensor", qk,
                           reads=(pr(KF + hp * 2048, 2048), pr(QP + h * 2048 + qc * 512, 512)),
                           writes=(PSC(gs * 1024, 1024),), tag=f"qk{qi}_{p}")
                        def ex(e, gs=gs):
                            tick(e.activation(out=PB(EB + gs * 1024, 1024),
                                              in_=ps_sc[:, gs * 1024:(gs + 1) * 1024],
                                              func=AF.Exp, scale=0.125,
                                              bias=sb["cbias"][:, 0:1]))
                        op("scalar", ex, reads=(PSC(gs * 1024, 1024), rr("cbias", 0, 1)),
                           writes=(PSC(gs * 1024, 1024), pr(EB + gs * 1024, 1024)),
                           tag=f"ex{qi}_{p}")

                    def mk_av(p):
                        gs = p % 2
                        def av(e, h=h, p=p, gs=gs, cxb=cxb):
                            for j in range(2):
                                kt = 2 * p + j
                                i = e.matmul(
                                    out=cxb[:, 0:512],
                                    lhsT=PB(VE + kt * VEW + h * VW, 128),
                                    rhs=PB(EB + gs * 1024 + j * 512, 512),
                                    start=(p == 0 and j == 0), stop=(p == 7 and j == 1))
                            tick(i)
                        op("tensor", av,
                           reads=(pr(EB + gs * 1024, 1024), pr(VE, NTB * VEW)),
                           writes=(cxr,), tag=f"av{qi}_{p}")

                    mk_qk(0)
                    for p in range(1, 8):
                        mk_qk(p)
                        mk_av(p - 1)
                    mk_av(7)
                    def cxe(e, h=h, qc=qc, qi=qi, cxb=cxb):
                        e.tensor_copy(
                            out=PBr(0, 64, CX + h * 2048 + qc * 512, 512),
                            in_=cxb[0:64, 0:512])
                        e.drain()
                        tick(e.tensor_copy(
                            out=sb["dstage"][64:65, (qi % 2) * 512:(qi % 2) * 512 + 512],
                            in_=cxb[64:65, 0:512]))
                    op("vector", cxe, reads=(cxr,),
                       writes=(cxr, pr(CX + h * 2048 + qc * 512, 512),
                               rr("dstage", (qi % 2) * 512, (qi % 2) * 512 + 512)),
                       tag=f"cxe{qi}")
                    op("gpsimd", is_dma_op=True, emit=lambda e, h=h, qc=qc, qi=qi: dma(e.dma_start(
                        out=ap["den_hb"][h * 4 + qc: h * 4 + qc + 1, :],
                        in_=sb["dstage"][64:65, (qi % 2) * 512:(qi % 2) * 512 + 512])),
                       reads=(rr("dstage", (qi % 2) * 512, (qi % 2) * 512 + 512),),
                       writes=(rr("den_hb", h * 4 + qc, h * 4 + qc + 1),), tag=f"dh{qi}")

            for s in range(4):
                odt = op("sync", is_dma_op=True, emit=lambda e, s=s: dma(e.dma_start_transpose(
                    out=sb["dcolb"][:, s * 64:(s + 1) * 64],
                    in_=ap["den_hb"][0:64, s * 128:(s + 1) * 128])),
                   reads=(rr("den_hb"),), writes=(rr("dcolb", s * 64, (s + 1) * 64),),
                   tag=f"dt{b}_{s}")
                heat(4, after=odt)

            def recip(e):
                e.reciprocal(out=sb["dcol"][:, :], in_=sb["dcolb"][:, :])
                e.drain()
                tick(e.tensor_copy(out=sb["dcolb"][:, :], in_=sb["dcol"][:, :]))
            orc = op("vector", recip, reads=(rr("dcolb"),),
               writes=(rr("dcolb"), rr("dcol")), tag=f"recip{b}")
            heat(6, after=orc)

            for Bb in range(2):
                op("sync", is_dma_op=True, emit=lambda e, Bb=Bb: dma(e.dma_start_transpose(
                    out=sb["drows"][:, Bb * 128:(Bb + 1) * 128],
                    in_=sb["dcolb"][:, Bb * 128:(Bb + 1) * 128])),
                   reads=(rr("dcolb", Bb * 128, (Bb + 1) * 128),),
                   writes=(rr("drows", Bb * 128, (Bb + 1) * 128),), tag=f"dtr{b}_{Bb}")
            ordw = op("gpsimd", is_dma_op=True, emit=lambda e: dma(e.dma_start(
                out=ap["rden_hb"].rearrange("(bb r) q -> r bb q", bb=2),
                in_=sb["drows"][:, :].rearrange("r (bb q) -> r bb q", q=128))),
               reads=(rr("drows"),), writes=(rr("rden_hb"),), tag=f"rdw{b}")
            heat(6, after=ordw)

            for h in range(H):
                for qc in range(4):
                    i4 = (b * H + h) * 4 + qc
                    dmb = ps_cx if i4 % 2 == 0 else ps_dm
                    dmr = PCX() if i4 % 2 == 0 else PDM()
                    hq = h * 4 + qc
                    rsl = (i4 % 2) * 512
                    op("sync", is_dma_op=True, emit=lambda e, hq=hq, rsl=rsl: dma(e.dma_start(
                        out=sb["dstage"][0:1, rsl:rsl + 512].rearrange("p (s n) -> p s n", n=128),
                        in_=ap["rden_hb"][hq::64, :].unsqueeze(0))),
                       reads=(rr("rden_hb"),),
                       writes=(rr("dstage", 1024 + rsl, 1024 + rsl + 512),), tag=f"rdg{i4}")
                    def dmm(e, rsl=rsl, dmb=dmb):
                        tick(e.matmul(out=dmb[0:64, 0:512],
                                      lhsT=sb["ones"][0:1, 0:64],
                                      rhs=sb["dstage"][0:1, rsl:rsl + 512],
                                      start=True, stop=True))
                    op("tensor", dmm,
                       reads=(rr("dstage", 1024 + rsl, 1024 + rsl + 512), rr("c_ones")),
                       writes=(dmr,), tag=f"dmm{i4}")
                    def nrm(e, h=h, qc=qc, dmb=dmb):
                        tick(e.tensor_tensor(
                            out=PBr(0, 64, CX + h * 2048 + qc * 512, 512),
                            in0=PBr(0, 64, CX + h * 2048 + qc * 512, 512),
                            in1=dmb[0:64, 0:512], op=ALU.mult))
                    onrm = op("vector", nrm,
                       reads=(dmr, pr(CX + h * 2048 + qc * 512, 512)),
                       writes=(dmr, pr(CX + h * 2048 + qc * 512, 512)), tag=f"nrm{i4}")
                    if i4 % 4 == 3:
                        heat(2, after=onrm)

            for h in range(H):
                op("sync", is_dma_op=True, emit=lambda e, h=h: dma(e.dma_start(
                    out=PBr(0, 64, WO + h * 1024, 1024),
                    in_=ap["wout_b"][h * 64:(h + 1) * 64, :])),
                   reads=(rr("wout_b"),), writes=(pr(WO + h * 1024, 1024),),
                   tag=f"wo{b}_{h}")
            op("sync", is_dma_op=True, emit=lambda e: dma(e.dma_start(out=g.pool[0:1, BO:BO + 1024],
                                                 in_=ap["bout_b"])),
               reads=(rr("bout_b"),), writes=(pr(BO, 1024),), tag=f"bo{b}")
            for t in range(NTB):
                ti = b * NTB + t
                sl2 = (t % 2) * 2048
                op("sync", is_dma_op=True, emit=lambda e, b=b, t=t, sl2=sl2: dma(e.dma_start(
                    out=PF(XD + sl2, 1024),
                    in_=ap["x_in"][b, t * 128:(t + 1) * 128, :])),
                   reads=(rr("x_in"),), writes=(prf(XD + sl2, 1024),), tag=f"xd{ti}")
                def opj(e, t=t):
                    for dd in range(2):
                        for h in range(H):
                            i = e.matmul(out=ps_op[:, dd * 512:(dd + 1) * 512],
                                         lhsT=PBr(0, 64, CX + h * 2048 + t * 128, 128),
                                         rhs=PBr(0, 64, WO + h * 1024 + dd * 512, 512),
                                         start=(h == 0), stop=False)
                        i = e.matmul(out=ps_op[:, dd * 512:(dd + 1) * 512],
                                     lhsT=sb["ones"][0:1, 0:128],
                                     rhs=g.pool[0:1, BO + dd * 512: BO + (dd + 1) * 512],
                                     start=False, stop=True)
                    tick(i)
                op("tensor", opj,
                   reads=(pr(CX, 32768), pr(WO, 16384), pr(BO, 1024), rr("c_ones")),
                   writes=(POP(0, 1024),), tag=f"opj{ti}")
                def resid(e, sl2=sl2):
                    tick(e.scalar_tensor_tensor(out=PF(XD + sl2, 1024),
                                                in0=ps_op[:, 0:1024], scalar=1.0,
                                                in1=PF(XD + sl2, 1024),
                                                op0=ALU.mult, op1=ALU.add))
                op("vector", resid, reads=(POP(0, 1024), prf(XD + sl2, 1024)),
                   writes=(POP(0, 1024), prf(XD + sl2, 1024)), tag=f"res{ti}")
                op("gpsimd", is_dma_op=True, emit=lambda e, b=b, t=t, sl2=sl2: dma(e.dma_start(
                    out=ap["xres"][(b * T + t * 128):(b * T + (t + 1) * 128), :],
                    in_=PF(XD + sl2, 1024))),
                   reads=(prf(XD + sl2, 1024),),
                   writes=(rr("xres", b * T + t * 128, b * T + (t + 1) * 128),),
                   tag=f"xw{ti}")

    l0_attn()

    XF2 = 0
    YJ = 4096
    XQ = 12288
    WL = 45056
    WB2 = 49152
    GU = 53248
    EL = 61440
    WT = 69632
    EL2 = 12288
    OI = 49152
    DS8 = 53248
    NIC = I0 // 128

    def lpbq_prep(wname, bname, ntile, ncol):
        nct = ncol // 1024
        tiles = [(it, ct) for it in range(ntile) for ct in range(nct)]

        def lp_load(ti):
            it, ct = tiles[ti]
            sl = ti % 2
            src = ap[wname][it * 128:(it + 1) * 128, ct * 1024:(ct + 1) * 1024]
            op("sync", is_dma_op=True,
               emit=lambda e, src=src, sl=sl: dma(e.dma_start(out=PF(WL + sl * 2048, 1024), in_=src)),
               reads=(rr(wname),), writes=(prf(WL + sl * 2048, 1024),),
               tag=f"lq{wname}{ti}")

        def lp_step(ti, k):
            it, ct = tiles[ti]
            sl = ti % 2
            wv = PF(WL + sl * 2048, 1024)
            wvr = prf(WL + sl * 2048, 1024)
            R = sb["cols"] if sl == 0 else sb["dcol"]
            Rn = "cols" if sl == 0 else "dcol"
            ev = PF(EL + sl * 2048, 1024)
            evr = prf(EL + sl * 2048, 1024)
            if k == 0:
                return op("vector", lambda e, wv=wv, R=R: tick(e.tensor_reduce(
                    out=R[:, 0:32], in_=wv.rearrange("p (b k) -> p b k", k=32),
                    axis=mybir.AxisListType.X, op=ALU.max,
                    apply_absolute_value=True)),
                   reads=(wvr,), writes=(rr(Rn, 0, 32),), tag=f"dqa{wname}{ti}")
            elif k == 1:
                return op("vector", lambda e, R=R: tick(e.tensor_scalar(
                    out=R[:, 32:64], in0=R[:, 0:32], scalar1=1.0 / 7.0,
                    scalar2=1e-12, op0=ALU.mult, op1=ALU.max)),
                   reads=(rr(Rn, 0, 32),), writes=(rr(Rn, 32, 64),), tag=f"dqb{wname}{ti}")
            elif k == 2:
                return op("vector", lambda e, R=R: tick(e.reciprocal(
                    out=R[:, 0:32], in_=R[:, 32:64])),
                   reads=(rr(Rn, 32, 64),), writes=(rr(Rn, 0, 32),), tag=f"dqc{wname}{ti}")
            elif k == 3:
                return op("vector", lambda e, wv=wv, ev=ev, R=R: tick(e.tensor_tensor(
                    out=ev.rearrange("p (b k) -> p b k", k=32),
                    in0=wv.rearrange("p (b k) -> p b k", k=32),
                    in1=R[:, 0:32].unsqueeze(2).broadcast_to([128, 32, 32]),
                    op=ALU.mult)),
                   reads=(wvr, rr(Rn, 0, 32)), writes=(evr,), tag=f"dqd{wname}{ti}")
            elif k == 4:
                return op("vector", lambda e, ev=ev: tick(e.tensor_scalar(
                    out=ev, in0=ev, scalar1=MAGIC, scalar2=MAGIC,
                    op0=ALU.add, op1=ALU.subtract)),
                   reads=(evr,), writes=(evr,), tag=f"dqe{wname}{ti}")
            elif k == 5:
                return op("vector", lambda e, ev=ev: tick(e.tensor_scalar(
                    out=ev, in0=ev, scalar1=7.0, scalar2=-8.0,
                    op0=ALU.min, op1=ALU.max)),
                   reads=(evr,), writes=(evr,), tag=f"dqf{wname}{ti}")
            else:
                return op("vector", lambda e, ev=ev, R=R, sl=sl: tick(e.tensor_tensor(
                    out=PB(WB2 + sl * 1024, 1024).rearrange("p (b k) -> p b k", k=32),
                    in0=ev.rearrange("p (b k) -> p b k", k=32),
                    in1=R[:, 32:64].unsqueeze(2).broadcast_to([128, 32, 32]),
                    op=ALU.mult)),
                   reads=(evr, rr(Rn, 32, 64)),
                   writes=(pr(WB2 + sl * 1024, 1024),), tag=f"dqg{wname}{ti}")

        def lp_store(ti):
            it, ct = tiles[ti]
            sl = ti % 2
            dst = ap[bname][it * 128:(it + 1) * 128, ct * 1024:(ct + 1) * 1024]
            op("gpsimd", is_dma_op=True,
               emit=lambda e, dst=dst, sl=sl: dma(e.dma_start(out=dst, in_=PB(WB2 + sl * 1024, 1024))),
               reads=(pr(WB2 + sl * 1024, 1024),),
               writes=(rr(bname, ti, ti + 1),), tag=f"wqw{wname}{ti}")

        n = len(tiles)
        for p in range(0, n, 2):
            pts = [p] + ([p + 1] if p + 1 < n else [])
            for ti in pts:
                lp_load(ti)
            pace = []
            for k in range(7):
                for ti in pts:
                    o_ = lp_step(ti, k)
                if k in (3, 6):
                    pace.append(o_)
            for pa in pace:
                heat(6, after=pa)
            for ti in pts:
                lp_store(ti)

    def ground(gi, nsrc, cci, cco):
        def comb(e, nsrc=nsrc, gi=gi):
            e.scalar_tensor_tensor(out=sb["colx"][:, 128:128 + nsrc],
                                   in0=sb["colx"][:, 64:64 + nsrc], scalar=-1.0,
                                   in1=sb["colx"][:, 0:nsrc], op0=ALU.mult, op1=ALU.max)
            e.drain()
            tick(e.tensor_reduce(out=sb["tcc2"][:, gi:gi + 1],
                                 in_=sb["colx"][:, 128:128 + nsrc],
                                 axis=mybir.AxisListType.X, op=ALU.max))
        op("vector", comb, reads=(rr("colx", 0, 64 + nsrc),),
           writes=(rr("colx", 128, 128 + nsrc), rr("tcc2", gi, gi + 1)), tag=f"gc{gi}")
        op("gpsimd", lambda e, gi=gi: tick(e.tensor_reduce(
            out=sb["tcc"][0:1, gi:gi + 1], in_=sb["tcc2"][:, gi:gi + 1],
            axis=mybir.AxisListType.C, op=ALU.max)),
           reads=(rr("tcc2", gi, gi + 1),), writes=(rr("tcc", gi, gi + 1),), tag=f"gp{gi}")
        op("gpsimd", is_dma_op=True,
           emit=lambda e, cci=cci: dma(e.dma_start(out=ap[cci], in_=sb["tcc"][0:1, :])),
           reads=(rr("tcc"),), writes=(rr(cci),), tag=f"gw{gi}")
        op("gpsimd", lambda e, cci=cci, cco=cco: tick(e.collective_compute(
            "AllReduce", ALU.max, replica_groups=[list(range(8))],
            ins=[ap[cci].opt()], outs=[ap[cco].opt()])),
           reads=(rr(cci),), writes=(rr(cco),), tag=f"cc{gi}", is_coll=True)
        op("gpsimd", is_dma_op=True,
           emit=lambda e, cco=cco, gi=gi: dma(e.dma_start(
               out=sb["tcc"][0:1, 8 + gi:8 + gi + 1], in_=ap[cco][0:1, gi:gi + 1])),
           reads=(rr(cco),), writes=(rr("tcc", 8 + gi, 8 + gi + 1),), tag=f"gr{gi}")
        op("tensor", lambda e, gi=gi: tick(e.matmul(
            out=ps_dm[:, 0:1], lhsT=sb["onesf"][0:1, :],
            rhs=sb["tcc"][0:1, 8 + gi:8 + gi + 1], start=True, stop=True)),
           reads=(rr("tcc", 8 + gi, 8 + gi + 1), rr("c_onesf")), writes=(PDM(),),
           tag=f"gb{gi}")
        op("vector", lambda e, gi=gi: tick(e.tensor_copy(
            out=sb["glob"][:, gi:gi + 1], in_=ps_dm[:, 0:1])),
           reads=(PDM(),), writes=(PDM(), rr("glob", gi, gi + 1)), tag=f"ge{gi}")

    def l0_ffn():
        lpbq_prep("wg0", "wg0_b", NIC, D)
        lpbq_prep("wu0", "wu0_b", NIC, D)
        lpbq_prep("wd0", "wd0_b", D // 128, I0)

        for t in range(NT):
            sl = t % 2
            op("sync", is_dma_op=True, emit=lambda e, t=t, sl=sl: dma(e.dma_start(
                out=PF(XF2 + sl * 2048, 1024), in_=ap["xres"][t * 128:(t + 1) * 128, :])),
               reads=(rr("xres", t * 128, (t + 1) * 128),),
               writes=(prf(XF2 + sl * 2048, 1024),), tag=f"f2x{t}")

            def stats(e, t=t, sl=sl):
                e.bn_stats(out=sb["tiny"][:, 0:6], in_=PF(XF2 + sl * 2048, 512))
                e.bn_stats(out=sb["tiny"][:, 6:12], in_=PF(XF2 + sl * 2048 + 1024, 512))
                e.drain()
                e.bn_aggr(out=sb["tiny"][:, 12:14], in_=sb["tiny"][:, 0:12])
                e.drain()
                e.tensor_copy(out=sb["colm"][:, t:t + 1], in_=sb["tiny"][:, 12:13])
                tick(e.tensor_scalar(out=sb["colx"][:, 160 + t:161 + t], in0=sb["tiny"][:, 13:14],
                                     scalar1=EPS, scalar2=None, op0=ALU.add))
            op("vector", stats, reads=(prf(XF2 + sl * 2048, 1024),),
               writes=(rr("tiny", 0, 15), rr("colm", t, t + 1),
                       rr("colx", 160 + t, 161 + t)),
               tag=f"f2s{t}")
            jx = PF(YJ, 1024) if sl == 0 else PF(XQ, 1024)
            jn = PF(YJ + 2048, 1024) if sl == 0 else PF(XQ + 2048, 1024)
            jxr = prf(YJ, 1024) if sl == 0 else prf(XQ, 1024)
            jnr = prf(YJ + 2048, 1024) if sl == 0 else prf(XQ + 2048, 1024)
            op("vector", lambda e, t=t, sl=sl, jx=jx: tick(e.tensor_scalar(
                out=jx, in0=PF(XF2 + sl * 2048, 1024),
                scalar1=1.0, scalar2=-1e30, op0=ALU.mult, op1=ALU.max,
                accum_out=sb["dcol"][:, t:t + 1])),
               reads=(prf(XF2 + sl * 2048, 1024),),
               writes=(jxr, rr("dcol", t, t + 1)), tag=f"f2gx{t}")
            og = op("vector", lambda e, t=t, sl=sl, jn=jn: tick(e.tensor_scalar(
                out=jn, in0=PF(XF2 + sl * 2048, 1024),
                scalar1=1.0, scalar2=1e30, op0=ALU.mult, op1=ALU.min,
                accum_out=sb["dcol"][:, 32 + t:33 + t])),
               reads=(prf(XF2 + sl * 2048, 1024),),
               writes=(jnr, rr("dcol", 32 + t, 33 + t)), tag=f"f2gn{t}")
            heat(3, after=og)
        op("scalar", lambda e: (
            e.activation(out=sb["colx"][:, 160:160 + NT], in_=sb["colx"][:, 160:160 + NT], func=AF.Ln),
            e.drain(),
            tick(e.activation(out=sb["colr"][:, 0:NT], in_=sb["colx"][:, 160:160 + NT],
                              func=AF.Exp, scale=-0.5))),
           reads=(rr("colx", 160, 160 + NT),),
           writes=(rr("colx", 160, 160 + NT), rr("colr", 0, NT)), tag="f2rsb")
        def scale_cols(e):
            e.tensor_tensor(out=sb["colx"][:, 0:NT], in0=sb["dcol"][:, 0:NT],
                            in1=sb["colm"][:, 0:NT], op=ALU.subtract)
            e.tensor_tensor(out=sb["colx"][:, 64:64 + NT], in0=sb["dcol"][:, 32:32 + NT],
                            in1=sb["colm"][:, 0:NT], op=ALU.subtract)
            e.drain()
            e.tensor_tensor(out=sb["colx"][:, 0:NT], in0=sb["colx"][:, 0:NT],
                            in1=sb["colr"][:, 0:NT], op=ALU.mult)
            tick(e.tensor_tensor(out=sb["colx"][:, 64:64 + NT], in0=sb["colx"][:, 64:64 + NT],
                                 in1=sb["colr"][:, 0:NT], op=ALU.mult))
        op("vector", scale_cols, reads=(rr("colr", 0, NT), rr("dcol", 0, 64),
                                        rr("colm", 0, NT)),
           writes=(rr("colx", 0, NT), rr("colx", 64, 64 + NT)), tag="f2sc")
        ground(0, NT, "cci1", "cco1")
        def mkscale0(e):
            e.reciprocal(out=sb["cols"][:, 0:1], in_=sb["glob"][:, 0:1])
            e.drain()
            e.tensor_scalar(out=sb["cols"][:, 0:1], in0=sb["cols"][:, 0:1],
                            scalar1=32767.0, scalar2=None, op0=ALU.mult)
            e.tensor_scalar(out=sb["cols"][:, 1:2], in0=sb["glob"][:, 0:1],
                            scalar1=1.0 / 32767.0, scalar2=None, op0=ALU.mult)
            e.drain()
            tick(e.tensor_tensor(out=sb["cols"][:, 8:8 + NT], in0=sb["colr"][:, 0:NT],
                                 in1=sb["cols"][:, 0:1].broadcast_to([128, NT]),
                                 op=ALU.mult))
        op("vector", mkscale0, reads=(rr("glob", 0, 1), rr("colr", 0, NT)),
           writes=(rr("cols", 0, 2), rr("cols", 8, 8 + NT)), tag="f2k0")

        def f2_ldx(t):
            sl = t % 2
            op("sync", is_dma_op=True, emit=lambda e, t=t, sl=sl: dma(e.dma_start(
                out=PF(XF2 + sl * 2048, 1024), in_=ap["xres"][t * 128:(t + 1) * 128, :])),
               reads=(rr("xres", t * 128, (t + 1) * 128),),
               writes=(prf(XF2 + sl * 2048, 1024),), tag=f"f2x2_{t}")

        def f2_tev(t):
            def xqT(e, t=t):
                for dc in range(8):
                    i = e.transpose(out=pt_op[:, (t % 2) * 1024 + dc * 128:(t % 2) * 1024 + (dc + 1) * 128],
                                    in_=PB(WB2 + (t % 2) * 1024 + dc * 128, 128),
                                    identity=sb["id"][:, :])
                tick(i)
            op("tensor", xqT, reads=(pr(WB2 + (t % 2) * 1024, 1024), rr("c_id")),
               writes=(POP((t % 2) * 512, 512),), tag=f"f2t{t}")
            op("vector", lambda e, t=t: tick(e.tensor_copy(
                out=PB(XQ, 32768).rearrange("p (dc n) -> p dc n", n=4096)[:, :, t * 128:(t + 1) * 128],
                in_=pt_op[:, (t % 2) * 1024:(t % 2) * 1024 + 1024].rearrange("p (dc n) -> p dc n", n=128))),
               reads=(POP((t % 2) * 512, 512),),
               writes=(POP((t % 2) * 512, 512),) + tuple(
                   pr(XQ + dc * 4096 + t * 128, 128) for dc in range(8)),
               tag=f"f2e{t}")

        for tp in range(NT // 2):
            t0, t1 = 2 * tp, 2 * tp + 1
            f2_ldx(t0)
            f2_ldx(t1)
            _last = []
            def one(t, k, _last=_last):
                sl = t % 2
                xt = PF(XF2 + sl * 2048, 1024)
                yv = PF(YJ + sl * 2048, 1024)
                if k == 0:
                    return op("vector", lambda e, t=t, xt=xt, yv=yv: tick(e.tensor_scalar(
                        out=yv, in0=xt, scalar1=sb["colm"][:, t:t + 1],
                        scalar2=sb["cols"][:, 8 + t:8 + t + 1],
                        op0=ALU.subtract, op1=ALU.mult)),
                       reads=(prf(XF2 + sl * 2048, 1024), rr("colm", t, t + 1),
                              rr("cols", 8 + t, 9 + t)),
                       writes=(prf(YJ + sl * 2048, 1024),), tag=f"f2qa{t}")
                elif k == 1:
                    return op("vector", lambda e, yv=yv: tick(e.tensor_scalar(
                        out=yv, in0=yv, scalar1=MAGIC, scalar2=MAGIC,
                        op0=ALU.add, op1=ALU.subtract)),
                       reads=(prf(YJ + sl * 2048, 1024),),
                       writes=(prf(YJ + sl * 2048, 1024),), tag=f"f2qb{t}")
                elif k == 2:
                    return op("vector", lambda e, yv=yv: tick(e.tensor_scalar(
                        out=yv, in0=yv, scalar1=32767.0, scalar2=-32768.0,
                        op0=ALU.min, op1=ALU.max)),
                       reads=(prf(YJ + sl * 2048, 1024),),
                       writes=(prf(YJ + sl * 2048, 1024),), tag=f"f2qc{t}")
                else:
                    return op("vector", lambda e, t=t, yv=yv: tick(e.tensor_scalar(
                        out=PB(WB2 + (t % 2) * 1024, 1024), in0=yv,
                        scalar1=sb["cols"][:, 1:2], scalar2=None, op0=ALU.mult)),
                       reads=(prf(YJ + sl * 2048, 1024), rr("cols", 1, 2)),
                       writes=(pr(WB2 + (t % 2) * 1024, 1024),), tag=f"f2qd{t}")
            for k in range(4):
                ok_ = one(t0, k)
                one(t1, k)
                if k == 2:
                    heat(4, after=ok_)
            f2_tev(t0)
            f2_tev(t1)

        def wT_loads(bname, tagp):
            for dc in range(8):
                op("sync", is_dma_op=True, emit=lambda e, dc=dc, bname=bname: dma(
                    e.dma_start_transpose(out=PB(WT + dc * 3072, 3072),
                                          in_=ap[bname][:, dc * 128:(dc + 1) * 128])),
                   reads=(rr(bname),), writes=(pr(WT + dc * 3072, 3072),),
                   tag=f"{tagp}{dc}")

        def mm_ug(scname, colbase, tagp):
            for ic in range(NIC):
                gsl = ic % 2
                for tc in range(8):
                    psl = tc % 4
                    def mm(e, ic=ic, tc=tc, psl=psl):
                        for dc in range(8):
                            i = e.matmul(out=ps_sc[:, psl * 512:(psl + 1) * 512],
                                         lhsT=PB(WT + dc * 3072 + ic * 128, 128),
                                         rhs=PB(XQ + dc * 4096 + tc * 512, 512),
                                         start=(dc == 0), stop=(dc == 7))
                        tick(i)
                    op("tensor", mm, reads=(pr(WT, 24576), pr(XQ, 32768)),
                       writes=(PSC(psl * 512, 512),), tag=f"{tagp}m{ic}_{tc}")

                    def ev(e, ic=ic, tc=tc, psl=psl, gsl=gsl):
                        e.tensor_scalar(out=PB(GU + gsl * 4096 + tc * 512, 512),
                                        in0=ps_sc[:, psl * 512:(psl + 1) * 512],
                                        scalar1=1.0, scalar2=-1e30,
                                        op0=ALU.mult, op1=ALU.max,
                                        accum_out=sb["tiny"][:, 16 + tc:17 + tc])
                        tick(e.tensor_scalar(out=PB(YJ + (tc % 2) * 2048, 512),
                                             in0=ps_sc[:, psl * 512:(psl + 1) * 512],
                                             scalar1=1.0, scalar2=1e30,
                                             op0=ALU.mult, op1=ALU.min,
                                             accum_out=sb["tiny"][:, 24 + tc:25 + tc]))
                    op("vector", ev,
                       reads=(PSC(psl * 512, 512),),
                       writes=(PSC(psl * 512, 512), pr(GU + gsl * 4096 + tc * 512, 512),
                               pr(YJ + (tc % 2) * 2048, 512),
                               rr("tiny", 16 + tc, 17 + tc),
                               rr("tiny", 24 + tc, 25 + tc)),
                       tag=f"{tagp}e{ic}_{tc}")
                def fold(e, ic=ic, colbase=colbase):
                    e.tensor_reduce(out=sb["colx"][:, (0 if colbase == 0 else 32) + ic:(0 if colbase == 0 else 32) + ic + 1],
                                    in_=sb["tiny"][:, 16:24], axis=mybir.AxisListType.X, op=ALU.max)
                    e.drain()
                    tick(e.tensor_reduce(out=sb["colx"][:, (64 if colbase == 0 else 96) + ic:(64 if colbase == 0 else 96) + ic + 1],
                                         in_=sb["tiny"][:, 24:32], axis=mybir.AxisListType.X, op=ALU.min))
                op("vector", fold, reads=(rr("tiny", 16, 32),),
                   writes=(rr("colx", (0 if colbase == 0 else 32) + ic, (0 if colbase == 0 else 32) + ic + 1),
                           rr("colx", (64 if colbase == 0 else 96) + ic, (64 if colbase == 0 else 96) + ic + 1)),
                   tag=f"{tagp}f{ic}")
                op("gpsimd", is_dma_op=True, emit=lambda e, ic=ic, gsl=gsl, scname=scname: dma(
                    e.dma_start(out=ap[scname][ic], in_=PB(GU + gsl * 4096, 4096))),
                   reads=(pr(GU + gsl * 4096, 4096),),
                   writes=(rr(scname, ic, ic + 1),), tag=f"{tagp}s{ic}")

        wT_loads("wg0_b", "wgT")
        mm_ug("gate_sc", 0, "g")
        ground(1, NIC, "cci2", "cco2")
        wT_loads("wu0_b", "wuT")
        mm_ug("up_sc", 32, "u")
        def upmix(e):
            e.scalar_tensor_tensor(out=sb["colx"][:, 128:128 + NIC],
                                   in0=sb["colx"][:, 96:96 + NIC], scalar=-1.0,
                                   in1=sb["colx"][:, 32:32 + NIC], op0=ALU.mult, op1=ALU.max)
            e.drain()
            tick(e.tensor_reduce(out=sb["tcc2"][:, 2:3], in_=sb["colx"][:, 128:128 + NIC],
                                 axis=mybir.AxisListType.X, op=ALU.max))
        op("vector", upmix, reads=(rr("colx", 32, 32 + NIC), rr("colx", 96, 96 + NIC)),
           writes=(rr("colx", 128, 128 + NIC), rr("tcc2", 2, 3)), tag="gcup")
        op("gpsimd", lambda e: tick(e.tensor_reduce(
            out=sb["tcc"][0:1, 2:3], in_=sb["tcc2"][:, 2:3],
            axis=mybir.AxisListType.C, op=ALU.max)),
           reads=(rr("tcc2", 2, 3),), writes=(rr("tcc", 2, 3),), tag="gpup")
        op("gpsimd", is_dma_op=True,
           emit=lambda e: dma(e.dma_start(out=ap["cci3"], in_=sb["tcc"][0:1, :])),
           reads=(rr("tcc"),), writes=(rr("cci3"),), tag="gwup")
        op("gpsimd", lambda e: tick(e.collective_compute(
            "AllReduce", ALU.max, replica_groups=[list(range(8))],
            ins=[ap["cci3"].opt()], outs=[ap["cco3"].opt()])),
           reads=(rr("cci3"),), writes=(rr("cco3"),), tag="ccup", is_coll=True)
        op("gpsimd", is_dma_op=True,
           emit=lambda e: dma(e.dma_start(out=sb["tcc"][0:1, 10:11], in_=ap["cco3"][0:1, 2:3])),
           reads=(rr("cco3"),), writes=(rr("tcc", 10, 11),), tag="grup")
        op("tensor", lambda e: tick(e.matmul(
            out=ps_dm[:, 0:1], lhsT=sb["onesf"][0:1, :],
            rhs=sb["tcc"][0:1, 10:11], start=True, stop=True)),
           reads=(rr("tcc", 10, 11), rr("c_onesf")), writes=(PDM(),), tag="gbup")
        op("vector", lambda e: tick(e.tensor_copy(
            out=sb["glob"][:, 2:3], in_=ps_dm[:, 0:1])),
           reads=(PDM(),), writes=(PDM(), rr("glob", 2, 3)), tag="geup")

        def mkscale12(e):
            e.reciprocal(out=sb["cols"][:, 2:3], in_=sb["glob"][:, 1:2])
            e.reciprocal(out=sb["cols"][:, 4:5], in_=sb["glob"][:, 2:3])
            e.drain()
            e.tensor_scalar(out=sb["cols"][:, 2:3], in0=sb["cols"][:, 2:3],
                            scalar1=32767.0, scalar2=None, op0=ALU.mult)
            e.tensor_scalar(out=sb["cols"][:, 3:4], in0=sb["glob"][:, 1:2],
                            scalar1=1.0 / 32767.0, scalar2=None, op0=ALU.mult)
            e.tensor_scalar(out=sb["cols"][:, 4:5], in0=sb["cols"][:, 4:5],
                            scalar1=32767.0, scalar2=None, op0=ALU.mult)
            e.tensor_scalar(out=sb["cols"][:, 5:6], in0=sb["glob"][:, 2:3],
                            scalar1=1.0 / 32767.0, scalar2=None, op0=ALU.mult)
            e.drain()
            tick(e.tensor_scalar(out=sb["cols"][:, 44:45], in0=sb["cols"][:, 3:4],
                                 scalar1=1.0 / 65536.0, scalar2=None, op0=ALU.mult))
        op("vector", mkscale12, reads=(rr("glob", 1, 3),),
           writes=(rr("cols", 2, 6), rr("cols", 44, 45)), tag="f2k12")

        P2Q = lambda r: XQ + r * 8192
        for icp in range(NIC // 2):
            for ic in (2 * icp, 2 * icp + 1):
                rho = ic % 2
                op("sync", is_dma_op=True, emit=lambda e, ic=ic, rho=rho: dma(e.dma_start(
                    out=PB(GU + rho * 4096, 4096), in_=ap["gate_sc"][ic])),
                   reads=(rr("gate_sc", ic, ic + 1),),
                   writes=(pr(GU + rho * 4096, 4096),), tag=f"tld{ic}")

            def p2s(ic, k):
                rho = ic % 2
                q, qr = PF(P2Q(rho), 4096), prf(P2Q(rho), 4096)
                sg, sgr = PF(P2Q(2 + rho), 4096), prf(P2Q(2 + rho), 4096)
                tv, tvr = PF(WT + rho * 8192, 4096), prf(WT + rho * 8192, 4096)
                ob, obr = PB(WL + rho * 4096, 4096), pr(WL + rho * 4096, 4096)
                if k == 0:
                    return op("scalar", lambda e, rho=rho, q=q: tick(e.activation(
                        out=q, in_=PB(GU + rho * 4096, 4096), func=AF.Copy,
                        scale=sb["cols"][:, 2:3], bias=MAGIC)),
                       reads=(pr(GU + rho * 4096, 4096), rr("cols", 2, 3)),
                       writes=(qr,), tag=f"tqa{ic}")
                elif k == 1:
                    return op("scalar", lambda e, q=q: tick(e.activation(
                        out=q, in_=q, func=AF.Copy, bias=-MAGIC)),
                       reads=(qr,), writes=(qr,), tag=f"tqa2{ic}")
                elif k == 2:
                    return op("vector", lambda e, q=q: tick(e.tensor_scalar(
                        out=q, in0=q, scalar1=32767.0, scalar2=-32768.0,
                        op0=ALU.min, op1=ALU.max)),
                       reads=(qr,), writes=(qr,), tag=f"tqb{ic}")
                elif k == 3:
                    return op("scalar", lambda e, q=q, sg=sg: tick(e.activation(
                        out=sg, in_=q, func=AF.Sigmoid, scale=sb["cols"][:, 3:4])),
                       reads=(qr, rr("cols", 3, 4)), writes=(sgr,), tag=f"tsg{ic}")
                elif k == 4:
                    return op("scalar", lambda e, sg=sg: tick(e.activation(
                        out=sg, in_=sg, func=AF.Copy, scale=65536.0,
                        bias=MAGIC)),
                       reads=(sgr,), writes=(sgr,), tag=f"tfa{ic}")
                elif k == 5:
                    return op("scalar", lambda e, sg=sg: tick(e.activation(
                        out=sg, in_=sg, func=AF.Copy, bias=-MAGIC)),
                       reads=(sgr,), writes=(sgr,), tag=f"tfa2{ic}")
                elif k == 6:
                    return op("vector", lambda e, sg=sg: tick(e.tensor_scalar(
                        out=sg, in0=sg, scalar1=65535.0, scalar2=0.0,
                        op0=ALU.min, op1=ALU.max)),
                       reads=(sgr,), writes=(sgr,), tag=f"tfb{ic}")
                elif k == 9:
                    return op("vector", lambda e, q=q, sg=sg, tv=tv: tick(e.tensor_tensor(
                        out=tv, in0=sg, in1=q, op=ALU.mult)),
                       reads=(sgr, qr), writes=(tvr,), tag=f"tfc{ic}")
                elif k == 7:
                    return op("vector", lambda e, ic=ic, tv=tv, ob=ob: tick(e.tensor_scalar(
                        out=ob, in0=tv, scalar1=sb["cols"][:, 44:45], scalar2=-1e30,
                        op0=ALU.mult, op1=ALU.max,
                        accum_out=sb["colx"][:, ic:ic + 1])),
                       reads=(tvr, rr("cols", 44, 45)),
                       writes=(obr, rr("colx", ic, ic + 1)), tag=f"tfd{ic}")
                else:
                    return op("vector", lambda e, ic=ic, tv=tv, q=q: tick(e.tensor_scalar(
                        out=q, in0=tv, scalar1=sb["cols"][:, 44:45], scalar2=1e30,
                        op0=ALU.mult, op1=ALU.min,
                        accum_out=sb["colx"][:, 64 + ic:64 + ic + 1])),
                       reads=(tvr, rr("cols", 44, 45)),
                       writes=(qr, rr("colx", 64 + ic, 64 + ic + 1)), tag=f"tfe{ic}")

            pace = []
            for k in (0, 1, 2, 3, 4, 5, 6, 9, 7, 8):
                oa = p2s(2 * icp, k)
                p2s(2 * icp + 1, k)
                if k in (2, 6, 9, 7, 8):
                    pace.append(oa)
            for pa in pace:
                heat(10, after=pa)
            for ic in (2 * icp, 2 * icp + 1):
                rho = ic % 2
                op("gpsimd", is_dma_op=True, emit=lambda e, ic=ic, rho=rho: dma(e.dma_start(
                    out=ap["gate_sc"][ic], in_=PB(WL + rho * 4096, 4096))),
                   reads=(pr(WL + rho * 4096, 4096),),
                   writes=(rr("gate_sc", ic, ic + 1),), tag=f"tst{ic}")
        ground(3, NIC, "cci4", "cco4")
        def mkscale3(e):
            e.reciprocal(out=sb["cols"][:, 6:7], in_=sb["glob"][:, 3:4])
            e.drain()
            e.tensor_scalar(out=sb["cols"][:, 6:7], in0=sb["cols"][:, 6:7],
                            scalar1=32767.0, scalar2=None, op0=ALU.mult)
            tick(e.tensor_scalar(out=sb["cols"][:, 7:8], in0=sb["glob"][:, 3:4],
                                 scalar1=1.0 / 32767.0, scalar2=None, op0=ALU.mult))
        op("vector", mkscale3, reads=(rr("glob", 3, 4),), writes=(rr("cols", 6, 8),), tag="f2k3")

        for icp in range(NIC // 2):
            for ic in (2 * icp, 2 * icp + 1):
                rho = ic % 2
                op("sync", is_dma_op=True, emit=lambda e, ic=ic, rho=rho: dma(e.dma_start(
                    out=PB(GU + rho * 4096, 4096), in_=ap["gate_sc"][ic])),
                   reads=(rr("gate_sc", ic, ic + 1),),
                   writes=(pr(GU + rho * 4096, 4096),), tag=f"old{ic}")
                op("scalar", is_dma_op=True, emit=lambda e, ic=ic, rho=rho: dma(e.dma_start(
                    out=PB(EL + rho * 4096, 4096), in_=ap["up_sc"][ic])),
                   reads=(rr("up_sc", ic, ic + 1),),
                   writes=(pr(EL + rho * 4096, 4096),), tag=f"old2{ic}")

            def p3s(ic, k):
                rho = ic % 2
                g2, g2r = PF(P2Q(rho), 4096), prf(P2Q(rho), 4096)
                uq, uqr = PF(P2Q(2 + rho), 4096), prf(P2Q(2 + rho), 4096)
                tv, tvr = PF(WT + rho * 8192, 4096), prf(WT + rho * 8192, 4096)
                ob, obr = PB(WL + rho * 4096, 4096), pr(WL + rho * 4096, 4096)
                if k == 0:
                    return op("scalar", lambda e, rho=rho, g2=g2: tick(e.activation(
                        out=g2, in_=PB(GU + rho * 4096, 4096), func=AF.Copy,
                        scale=sb["cols"][:, 6:7], bias=MAGIC)),
                       reads=(pr(GU + rho * 4096, 4096), rr("cols", 6, 7)),
                       writes=(g2r,), tag=f"opa{ic}")
                elif k == 1:
                    return op("scalar", lambda e, g2=g2: tick(e.activation(
                        out=g2, in_=g2, func=AF.Copy, bias=-MAGIC)),
                       reads=(g2r,), writes=(g2r,), tag=f"opa2{ic}")
                elif k == 2:
                    return op("vector", lambda e, g2=g2: tick(e.tensor_scalar(
                        out=g2, in0=g2, scalar1=32767.0, scalar2=-32768.0,
                        op0=ALU.min, op1=ALU.max)),
                       reads=(g2r,), writes=(g2r,), tag=f"opb{ic}")
                elif k == 3:
                    return op("scalar", lambda e, rho=rho, uq=uq: tick(e.activation(
                        out=uq, in_=PB(EL + rho * 4096, 4096), func=AF.Copy,
                        scale=sb["cols"][:, 4:5], bias=MAGIC)),
                       reads=(pr(EL + rho * 4096, 4096), rr("cols", 4, 5)),
                       writes=(uqr,), tag=f"opd{ic}")
                elif k == 4:
                    return op("scalar", lambda e, uq=uq: tick(e.activation(
                        out=uq, in_=uq, func=AF.Copy, bias=-MAGIC)),
                       reads=(uqr,), writes=(uqr,), tag=f"opd2{ic}")
                elif k == 5:
                    return op("vector", lambda e, uq=uq: tick(e.tensor_scalar(
                        out=uq, in0=uq, scalar1=32767.0, scalar2=-32768.0,
                        op0=ALU.min, op1=ALU.max)),
                       reads=(uqr,), writes=(uqr,), tag=f"ope{ic}")
                elif k == 6:
                    return op("vector", lambda e, g2=g2, uq=uq, tv=tv: tick(e.scalar_tensor_tensor(
                        out=tv, in0=g2, scalar=sb["cols"][:, 7:8], in1=uq,
                        op0=ALU.mult, op1=ALU.mult)),
                       reads=(g2r, uqr, rr("cols", 7, 8)), writes=(tvr,), tag=f"opg{ic}")
                elif k == 7:
                    return op("vector", lambda e, ic=ic, tv=tv, ob=ob: tick(e.tensor_scalar(
                        out=ob, in0=tv, scalar1=sb["cols"][:, 5:6], scalar2=-1e30,
                        op0=ALU.mult, op1=ALU.max,
                        accum_out=sb["colx"][:, 32 + ic:32 + ic + 1])),
                       reads=(tvr, rr("cols", 5, 6)),
                       writes=(obr, rr("colx", 32 + ic, 32 + ic + 1)), tag=f"oph{ic}")
                else:
                    return op("vector", lambda e, ic=ic, tv=tv, g2=g2: tick(e.tensor_scalar(
                        out=g2, in0=tv, scalar1=sb["cols"][:, 5:6], scalar2=1e30,
                        op0=ALU.mult, op1=ALU.min,
                        accum_out=sb["colx"][:, 96 + ic:96 + ic + 1])),
                       reads=(tvr, rr("cols", 5, 6)),
                       writes=(g2r, rr("colx", 96 + ic, 96 + ic + 1)), tag=f"opi{ic}")

            pace = []
            for k in range(9):
                oa = p3s(2 * icp, k)
                p3s(2 * icp + 1, k)
                if k in (2, 5, 6, 7, 8):
                    pace.append(oa)
            for pa in pace:
                heat(10, after=pa)
            for ic in (2 * icp, 2 * icp + 1):
                rho = ic % 2
                op("gpsimd", is_dma_op=True, emit=lambda e, ic=ic, rho=rho: dma(e.dma_start(
                    out=ap["up_sc"][ic], in_=PB(WL + rho * 4096, 4096))),
                   reads=(pr(WL + rho * 4096, 4096),),
                   writes=(rr("up_sc", ic, ic + 1),), tag=f"ost{ic}")
        def omix(e):
            e.scalar_tensor_tensor(out=sb["colx"][:, 128:128 + NIC],
                                   in0=sb["colx"][:, 96:96 + NIC], scalar=-1.0,
                                   in1=sb["colx"][:, 32:32 + NIC], op0=ALU.mult, op1=ALU.max)
            e.drain()
            tick(e.tensor_reduce(out=sb["tcc2"][:, 4:5], in_=sb["colx"][:, 128:128 + NIC],
                                 axis=mybir.AxisListType.X, op=ALU.max))
        op("vector", omix, reads=(rr("colx", 32, 32 + NIC), rr("colx", 96, 96 + NIC)),
           writes=(rr("colx", 128, 128 + NIC), rr("tcc2", 4, 5)), tag="gco")
        op("gpsimd", lambda e: tick(e.tensor_reduce(
            out=sb["tcc"][0:1, 4:5], in_=sb["tcc2"][:, 4:5],
            axis=mybir.AxisListType.C, op=ALU.max)),
           reads=(rr("tcc2", 4, 5),), writes=(rr("tcc", 4, 5),), tag="gpo")
        op("gpsimd", is_dma_op=True,
           emit=lambda e: dma(e.dma_start(out=ap["cci5"], in_=sb["tcc"][0:1, :])),
           reads=(rr("tcc"),), writes=(rr("cci5"),), tag="gwo")
        op("gpsimd", lambda e: tick(e.collective_compute(
            "AllReduce", ALU.max, replica_groups=[list(range(8))],
            ins=[ap["cci5"].opt()], outs=[ap["cco5"].opt()])),
           reads=(rr("cci5"),), writes=(rr("cco5"),), tag="cco", is_coll=True)
        op("gpsimd", is_dma_op=True,
           emit=lambda e: dma(e.dma_start(out=sb["tcc"][0:1, 12:13], in_=ap["cco5"][0:1, 4:5])),
           reads=(rr("cco5"),), writes=(rr("tcc", 12, 13),), tag="gro")
        op("tensor", lambda e: tick(e.matmul(
            out=ps_dm[:, 0:1], lhsT=sb["onesf"][0:1, :],
            rhs=sb["tcc"][0:1, 12:13], start=True, stop=True)),
           reads=(rr("tcc", 12, 13), rr("c_onesf")), writes=(PDM(),), tag="gbo")
        op("vector", lambda e: (
            e.tensor_copy(out=sb["glob"][:, 4:5], in_=ps_dm[:, 0:1]),
            e.drain(),
            e.reciprocal(out=sb["cols"][:, 12:13], in_=sb["glob"][:, 4:5]),
            e.drain(),
            e.tensor_scalar(out=sb["cols"][:, 12:13], in0=sb["cols"][:, 12:13],
                            scalar1=32767.0, scalar2=None, op0=ALU.mult),
            tick(e.tensor_scalar(out=sb["cols"][:, 13:14], in0=sb["glob"][:, 4:5],
                                 scalar1=1.0 / 32767.0, scalar2=None, op0=ALU.mult))),
           reads=(PDM(),), writes=(PDM(), rr("glob", 4, 5), rr("cols", 12, 14)), tag="geo")

        for ic in range(NIC):
            op("sync", is_dma_op=True, emit=lambda e, ic=ic: dma(
                e.dma_start_transpose(out=PB(WT + ic * 1024, 1024),
                                      in_=ap["wd0_b"][:, ic * 128:(ic + 1) * 128])),
               reads=(rr("wd0_b"),), writes=(pr(WT + ic * 1024, 1024),), tag=f"wdT{ic}")
        for tg in range(8):
            for ic in range(NIC):
                osl = ic % 4
                op("sync", is_dma_op=True, emit=lambda e, tg=tg, ic=ic: dma(e.dma_start(
                    out=PB(EL + (ic % 2) * 1024, 512),
                    in_=ap["up_sc"][ic, :, tg * 512:(tg + 1) * 512])),
                   reads=(rr("up_sc", ic, ic + 1),),
                   writes=(pr(EL + (ic % 2) * 1024, 512),), tag=f"dol{tg}_{ic}")

                def oq(e, tg=tg, ic=ic, osl=osl):
                    ov = PF(EL2, 512)
                    e.tensor_scalar(out=ov, in0=PB(EL + (ic % 2) * 1024, 512),
                                    scalar1=sb["cols"][:, 12:13], scalar2=MAGIC,
                                    op0=ALU.mult, op1=ALU.add)
                    e.drain()
                    e.tensor_scalar(out=ov, in0=ov, scalar1=MAGIC, scalar2=32767.0,
                                    op0=ALU.subtract, op1=ALU.min)
                    e.drain()
                    tick(e.tensor_scalar(out=PB(OI + osl * 512, 512), in0=ov,
                                         scalar1=-32768.0, scalar2=sb["cols"][:, 13:14],
                                         op0=ALU.max, op1=ALU.mult))
                op("vector", oq, reads=(pr(EL + (ic % 2) * 1024, 512), rr("cols", 12, 14)),
                   writes=(prf(EL2, 512), pr(OI + osl * 512, 512)), tag=f"doq{tg}_{ic}")

                def dmm(e, tg=tg, ic=ic, osl=osl):
                    for hc in range(8):
                        if hc < 4:
                            out = ps_sc[:, hc * 512:(hc + 1) * 512]
                        elif hc == 4:
                            out = ps_cx[:, 0:512]
                        elif hc == 5:
                            out = ps_dm[:, 0:512]
                        else:
                            out = ps_op[:, (hc - 6) * 512:(hc - 5) * 512]
                        i = e.matmul(out=out,
                                     lhsT=PB(WT + ic * 1024 + hc * 128, 128),
                                     rhs=PB(OI + osl * 512, 512),
                                     start=(ic == 0), stop=(ic == NIC - 1))
                    tick(i)
                op("tensor", dmm, reads=(pr(WT, 24576), pr(OI + osl * 512, 512)),
                   writes=(PSC(0, 2048), PCX(), PDM(), POP(0, 1024)), tag=f"dmm{tg}_{ic}")
            def dev(e, tg=tg):
                for hc in range(8):
                    if hc < 4:
                        src = ps_sc[:, hc * 512:(hc + 1) * 512]
                    elif hc == 4:
                        src = ps_cx[:, 0:512]
                    elif hc == 5:
                        src = ps_dm[:, 0:512]
                    else:
                        src = ps_op[:, (hc - 6) * 512:(hc - 5) * 512]
                    i = e.tensor_copy(out=PF(DS8 + hc * 1024, 512), in_=src)
                tick(i)
            op("vector", dev,
               reads=(PSC(0, 2048), PCX(), PDM(), POP(0, 1024)),
               writes=(PSC(0, 2048), PCX(), PDM(), POP(0, 1024), prf(DS8, 4096)),
               tag=f"dev{tg}")

            def dtr2(e, tg=tg):
                for hc in range(8):
                    for tc2 in range(4):
                        if tc2 < 2:
                            out = ps_sc[:, tc2 * 1024 + hc * 128: tc2 * 1024 + (hc + 1) * 128]
                        elif tc2 == 2:
                            out = (ps_cx if hc < 4 else ps_dm)[:, (hc % 4) * 128:(hc % 4 + 1) * 128]
                        else:
                            out = ps_op[:, hc * 128:(hc + 1) * 128]
                        i = e.transpose(out=out,
                                        in_=PF(DS8 + hc * 1024 + tc2 * 256, 128),
                                        identity=sb["idf"][:, :])
                tick(i)
            op("tensor", dtr2, reads=(prf(DS8, 4096), rr("c_idf")),
               writes=(PSC(0, 2048), PCX(), PDM(), POP(0, 1024)), tag=f"dtr{tg}")
            for tc2 in range(4):
                t = tg * 4 + tc2
                op("sync", is_dma_op=True, emit=lambda e, t=t, tc2=tc2: dma(e.dma_start(
                    out=PF(XF2 + (tc2 % 2) * 2048, 1024),
                    in_=ap["xres"][t * 128:(t + 1) * 128, :])),
                   reads=(rr("xres", t * 128, (t + 1) * 128),),
                   writes=(prf(XF2 + (tc2 % 2) * 2048, 1024),), tag=f"dx{t}")

                def dres(e, tc2=tc2):
                    xv = PF(XF2 + (tc2 % 2) * 2048, 1024)
                    if tc2 < 2:
                        i = e.scalar_tensor_tensor(out=xv, in0=ps_sc[:, tc2 * 1024:(tc2 + 1) * 1024],
                                                   scalar=1.0, in1=xv, op0=ALU.mult, op1=ALU.add)
                    elif tc2 == 2:
                        e.scalar_tensor_tensor(out=PF(XF2 + (tc2 % 2) * 2048, 512), in0=ps_cx[:, 0:512],
                                               scalar=1.0, in1=PF(XF2 + (tc2 % 2) * 2048, 512),
                                               op0=ALU.mult, op1=ALU.add)
                        e.drain()
                        i = e.scalar_tensor_tensor(out=PF(XF2 + (tc2 % 2) * 2048 + 1024, 512),
                                                   in0=ps_dm[:, 0:512], scalar=1.0,
                                                   in1=PF(XF2 + (tc2 % 2) * 2048 + 1024, 512),
                                                   op0=ALU.mult, op1=ALU.add)
                    else:
                        i = e.scalar_tensor_tensor(out=xv, in0=ps_op[:, 0:1024],
                                                   scalar=1.0, in1=xv, op0=ALU.mult, op1=ALU.add)
                    tick(i)
                rdsets = {0: (PSC(0, 1024),), 1: (PSC(1024, 1024),),
                          2: (PCX(), PDM()), 3: (POP(0, 1024),)}[tc2]
                op("vector", dres,
                   reads=rdsets + (prf(XF2 + (tc2 % 2) * 2048, 1024),),
                   writes=rdsets + (prf(XF2 + (tc2 % 2) * 2048, 1024),), tag=f"dr{t}")
                op("gpsimd", is_dma_op=True, emit=lambda e, t=t, tc2=tc2: dma(e.dma_start(
                    out=ap["xres"][t * 128:(t + 1) * 128, :],
                    in_=PF(XF2 + (tc2 % 2) * 2048, 1024))),
                   reads=(prf(XF2 + (tc2 % 2) * 2048, 1024),),
                   writes=(rr("xres", t * 128, (t + 1) * 128),), tag=f"dw{t}")

    if n_phases >= 2:
        l0_ffn()

    XF3 = 0
    LNT = 4096
    LNF = 20480
    W1T = 36864
    HF = 45056
    YB = 61440
    CSB = 65536
    SR = 69632
    ER = 73728
    WC = 77824

    def l1_attn():
        op("sync", is_dma_op=True, emit=lambda e: dma(e.dma_start(
            out=sb["cols"][:, 16:24],
            in_=ap["b1"][0:1, :].rearrange("o (c p) -> (o p) c", p=128),
            allow_slow_non_contiguous=True)),
           reads=(rr("b1"),), writes=(rr("cols", 16, 24),), tag="b1l")
        op("sync", is_dma_op=True, emit=lambda e: dma(e.dma_start(
            out=sb["cols"][:, 24:32],
            in_=ap["w2"][0:1, :].rearrange("o (c p) -> (o p) c", p=128),
            allow_slow_non_contiguous=True)),
           reads=(rr("w2"),), writes=(rr("cols", 24, 32),), tag="w2l")
        op("vector", lambda e: tick(e.tensor_copy(out=PB(WC, 8), in_=sb["cols"][:, 24:32])),
           reads=(rr("cols", 24, 32),), writes=(pr(WC, 8),), tag="w2c")
        op("sync", is_dma_op=True, emit=lambda e: dma(e.dma_start(
            out=sb["b2sb"][0:1, 0:1], in_=ap["b2"])),
           reads=(rr("b2"),), writes=(rr("b2sb", 0, 1),), tag="b2l")
        for dc in range(8):
            op("sync", is_dma_op=True, emit=lambda e, dc=dc: dma(
                e.dma_start_transpose(out=PB(W1T + dc * 1024, 1024),
                                      in_=ap["w1_b"][:, dc * 128:(dc + 1) * 128])),
               reads=(rr("w1_b"),), writes=(pr(W1T + dc * 1024, 1024),), tag=f"w1T{dc}")

        for b in range(B_PC):
            for t in range(NTB):
                ti = b * NTB + t
                sl2 = (t % 2) * 2048
                op("sync", is_dma_op=True, emit=lambda e, b=b, t=t, sl2=sl2: dma(e.dma_start(
                    out=PF(XF3 + sl2, 1024),
                    in_=ap["xres"][(b * T + t * 128):(b * T + (t + 1) * 128), :])),
                   reads=(rr("xres", b * T + t * 128, b * T + (t + 1) * 128),),
                   writes=(prf(XF3 + sl2, 1024),), tag=f"a3x{ti}")

                def stats(e, t=t, ti=ti, sl2=sl2):
                    e.bn_stats(out=sb["tiny"][:, 0:6], in_=PF(XF3 + sl2, 512))
                    e.bn_stats(out=sb["tiny"][:, 6:12], in_=PF(XF3 + sl2 + 1024, 512))
                    e.drain()
                    e.bn_aggr(out=sb["tiny"][:, 12:14], in_=sb["tiny"][:, 0:12])
                    e.drain()
                    e.tensor_copy(out=sb["colm"][:, ti:ti + 1], in_=sb["tiny"][:, 12:13])
                    tick(e.tensor_scalar(out=sb["colx"][:, 160 + t:161 + t],
                                         in0=sb["tiny"][:, 13:14],
                                         scalar1=EPS, scalar2=None, op0=ALU.add))
                os_ = op("vector", stats, reads=(prf(XF3 + sl2, 1024),),
                   writes=(rr("tiny", 0, 15), rr("colm", ti, ti + 1),
                           rr("colx", 160 + t, 161 + t)), tag=f"a3s{ti}")
                heat(2, after=os_)
            op("scalar", lambda e, b=b: (
                e.activation(out=sb["colx"][:, 160:160 + NTB],
                             in_=sb["colx"][:, 160:160 + NTB], func=AF.Ln),
                e.drain(),
                tick(e.activation(out=sb["colr"][:, b * NTB:(b + 1) * NTB],
                                  in_=sb["colx"][:, 160:160 + NTB],
                                  func=AF.Exp, scale=-0.5))),
               reads=(rr("colx", 160, 160 + NTB),),
               writes=(rr("colx", 160, 160 + NTB),
                       rr("colr", b * NTB, (b + 1) * NTB)), tag=f"a3rb{b}")
            for t in range(NTB):
                ti = b * NTB + t
                sl2 = (t % 2) * 2048
                op("sync", is_dma_op=True, emit=lambda e, b=b, t=t, sl2=sl2: dma(e.dma_start(
                    out=PF(XF3 + sl2, 1024),
                    in_=ap["xres"][(b * T + t * 128):(b * T + (t + 1) * 128), :])),
                   reads=(rr("xres", b * T + t * 128, b * T + (t + 1) * 128),),
                   writes=(prf(XF3 + sl2, 1024),), tag=f"a3xb{ti}")
                op("vector", lambda e, t=t, ti=ti, sl2=sl2: tick(e.tensor_scalar(
                    out=PB(LNT + t * 1024, 1024), in0=PF(XF3 + sl2, 1024),
                    scalar1=sb["colm"][:, ti:ti + 1], scalar2=sb["colr"][:, ti:ti + 1],
                    op0=ALU.subtract, op1=ALU.mult)),
                   reads=(prf(XF3 + sl2, 1024), rr("colm", ti, ti + 1), rr("colr", ti, ti + 1)),
                   writes=(pr(LNT + t * 1024, 1024),), tag=f"a3l{ti}")
                def lnT(e, t=t):
                    for dc in range(8):
                        i = e.transpose(out=pt_op[:, (t % 2) * 1024 + dc * 128:(t % 2) * 1024 + (dc + 1) * 128],
                                        in_=PB(LNT + t * 1024 + dc * 128, 128),
                                        identity=sb["id"][:, :])
                    tick(i)
                op("tensor", lnT, reads=(pr(LNT + t * 1024, 1024), rr("c_id")),
                   writes=(POP((t % 2) * 512, 512),), tag=f"a3T{ti}")
                op("vector", lambda e, t=t: tick(e.tensor_copy(
                    out=PB(LNF, 16384).rearrange("p (dc n) -> p dc n", n=2048)[:, :, t * 128:(t + 1) * 128],
                    in_=pt_op[:, (t % 2) * 1024:(t % 2) * 1024 + 1024].rearrange("p (dc n) -> p dc n", n=128))),
                   reads=(POP((t % 2) * 512, 512),),
                   writes=(POP((t % 2) * 512, 512),) + tuple(
                       pr(LNF + dc * 2048 + t * 128, 128) for dc in range(8)),
                   tag=f"a3e{ti}")

            for ec in range(8):
                for tc in range(4):
                    psl = (ec * 4 + tc) % 4
                    def hmm(e, ec=ec, tc=tc, psl=psl):
                        for dc in range(8):
                            i = e.matmul(out=ps_sc[:, psl * 512:(psl + 1) * 512],
                                         lhsT=PB(W1T + dc * 1024 + ec * 128, 128),
                                         rhs=PB(LNF + dc * 2048 + tc * 512, 512),
                                         start=(dc == 0), stop=(dc == 7))
                        tick(i)
                    op("tensor", hmm, reads=(pr(W1T, 8192), pr(LNF, 16384)),
                       writes=(PSC(psl * 512, 512),), tag=f"hm{b}_{ec}_{tc}")
                    op("scalar", lambda e, ec=ec, tc=tc, psl=psl: tick(e.activation(
                        out=PB(HF + ec * 2048 + tc * 512, 512),
                        in_=ps_sc[:, psl * 512:(psl + 1) * 512], func=AF.Tanh,
                        bias=sb["cols"][:, 16 + ec:17 + ec])),
                       reads=(PSC(psl * 512, 512), rr("cols", 16, 24)),
                       writes=(PSC(psl * 512, 512), pr(HF + ec * 2048 + tc * 512, 512)),
                       tag=f"he{b}_{ec}_{tc}")

            for tc in range(4):
                def smm(e, tc=tc):
                    for ec in range(8):
                        i = e.matmul(out=ps_dm[0:1, 0:512],
                                     lhsT=g.pool[:, WC + ec:WC + ec + 1],
                                     rhs=PB(HF + ec * 2048 + tc * 512, 512),
                                     start=(ec == 0), stop=(ec == 7))
                    tick(i)
                op("tensor", smm, reads=(pr(WC, 8), pr(HF, 16384)),
                   writes=(PDM(),), tag=f"sm{b}_{tc}")
                op("vector", lambda e, tc=tc: tick(e.tensor_scalar(
                    out=PF(SR + tc * 1024, 512)[0:1, :], in0=ps_dm[0:1, 0:512],
                    scalar1=sb["b2sb"][0:1, 0:1], scalar2=None, op0=ALU.add)),
                   reads=(PDM(), rr("b2sb", 0, 1)),
                   writes=(PDM(), prf(SR + tc * 1024, 512)), tag=f"se{b}_{tc}")
            def mrow(e):
                tick(e.tensor_reduce(out=sb["b2sb"][0:1, 1:2], in_=PF(SR, 2048)[0:1, :],
                                     axis=mybir.AxisListType.X, op=ALU.max, negate=True))
            op("vector", mrow, reads=(prf(SR, 2048),),
               writes=(rr("b2sb", 1, 2), rr("tcc", 15, 16)), tag=f"mr{b}")
            op("scalar", lambda e: tick(e.activation(
                out=PF(ER, 2048)[0:1, :], in_=PF(SR, 2048)[0:1, :], func=AF.Exp,
                bias=sb["b2sb"][0:1, 1:2])),
               reads=(prf(SR, 2048), rr("b2sb", 1, 2)),
               writes=(prf(ER, 2048),), tag=f"ex3{b}")
            def ecT(e):
                for t in range(NTB):
                    i = e.transpose(out=ps_cx[:, t:t + 1], in_=PF(ER + t * 256, 128)[0:1, :],
                                    identity=sb["idf"][0:1, 0:1])
                tick(i)
            op("tensor", ecT, reads=(prf(ER, 2048), rr("c_idf")), writes=(PCX(),),
               tag=f"ecT{b}")
            op("vector", lambda e: tick(e.tensor_copy(out=sb["glob"][:, 0:16],
                                                      in_=ps_cx[:, 0:16])),
               reads=(PCX(),), writes=(PCX(), rr("glob", 0, 16)), tag=f"ecE{b}")

            for m in range(NTB):
                ysl = (m % 2) * 2048
                def ymk(e, m=m, ysl=ysl):
                    e.tensor_scalar(out=PB(YB + ysl, 1024), in0=PB(LNT + m * 1024, 1024),
                                    scalar1=sb["glob"][:, m:m + 1], scalar2=None,
                                    op0=ALU.mult)
                    e.drain()
                    tick(e.tensor_copy(out=PB(YB + ysl + 1024, 1), in_=sb["glob"][:, m:m + 1]))
                op("vector", ymk, reads=(pr(LNT + m * 1024, 1024), rr("glob", 0, 16)),
                   writes=(pr(YB + ysl, 1025),), tag=f"y{b}_{m}")
                def csm(e, m=m, ysl=ysl):
                    e.matmul(out=ps_sc[0:16, 0:512], lhsT=sb["ind"][:, m * 16:(m + 1) * 16],
                             rhs=PB(YB + ysl, 512), start=(m == 0), stop=(m == NTB - 1))
                    e.matmul(out=ps_sc[0:16, 512:1024], lhsT=sb["ind"][:, m * 16:(m + 1) * 16],
                             rhs=PB(YB + ysl + 512, 512), start=(m == 0), stop=(m == NTB - 1))
                    tick(e.matmul(out=ps_sc[0:16, 1024:1025], lhsT=sb["ind"][:, m * 16:(m + 1) * 16],
                                  rhs=PB(YB + ysl + 1024, 1), start=(m == 0), stop=(m == NTB - 1)))
                op("tensor", csm, reads=(pr(YB + ysl, 1025), rr("c_ind")),
                   writes=(PSC(0, 1536),), tag=f"cs{b}_{m}")
            op("vector", lambda e: tick(e.tensor_copy(
                out=g.pool[0:16, CSB:CSB + 1024], in_=ps_sc[0:16, 0:1024])),
               reads=(PSC(0, 1536),), writes=(PSC(0, 1536), pr(CSB, 1024)), tag=f"cse{b}")
            op("vector", lambda e: tick(e.tensor_copy(
                out=g.pool[0:16, CSB + 1024:CSB + 1025], in_=ps_sc[0:16, 1024:1025])),
               reads=(PSC(0, 1536),), writes=(pr(CSB + 1024, 1),), tag=f"cs2{b}")
            def spre(e):
                e.matmul(out=ps_cx[0:16, 0:512], lhsT=sb["t16"][:, :],
                         rhs=g.pool[0:16, CSB:CSB + 512], start=True, stop=True)
                e.matmul(out=ps_dm[0:16, 0:512], lhsT=sb["t16"][:, :],
                         rhs=g.pool[0:16, CSB + 512:CSB + 1024], start=True, stop=True)
                tick(e.matmul(out=ps_op[0:16, 0:1], lhsT=sb["t16"][:, :],
                              rhs=g.pool[0:16, CSB + 1024:CSB + 1025], start=True, stop=True))
            op("tensor", spre, reads=(pr(CSB, 1025), rr("c_t16")),
               writes=(PCX(), PDM(), POP(0, 1)), tag=f"sp{b}")
            def sev(e):
                e.tensor_copy(out=g.pool[0:16, CSB + 2048:CSB + 2048 + 512], in_=ps_cx[0:16, 0:512])
                e.tensor_copy(out=g.pool[0:16, CSB + 2560:CSB + 2560 + 512], in_=ps_dm[0:16, 0:512])
                tick(e.tensor_copy(out=g.pool[0:16, CSB + 3072:CSB + 3072 + 1], in_=ps_op[0:16, 0:1]))
            op("vector", sev, reads=(PCX(), PDM(), POP(0, 1)),
               writes=(PCX(), PDM(), POP(0, 1), pr(CSB + 2048, 1025)), tag=f"sev{b}")

            for m in range(NTB):
                ysl = (m % 2) * 2048
                def ymk2(e, m=m, ysl=ysl):
                    e.tensor_scalar(out=PB(YB + ysl, 1024), in0=PB(LNT + m * 1024, 1024),
                                    scalar1=sb["glob"][:, m:m + 1], scalar2=None,
                                    op0=ALU.mult)
                    e.drain()
                    tick(e.tensor_copy(out=PB(YB + ysl + 1024, 1), in_=sb["glob"][:, m:m + 1]))
                op("vector", ymk2, reads=(pr(LNT + m * 1024, 1024), rr("glob", 0, 16)),
                   writes=(pr(YB + ysl, 1025),), tag=f"y2{b}_{m}")
                def numm(e, m=m, ysl=ysl):
                    e.matmul(out=ps_sc[:, 0:512], lhsT=sb["ut"][:, :],
                             rhs=PB(YB + ysl, 512), start=True, stop=False)
                    e.matmul(out=ps_sc[:, 512:1024], lhsT=sb["ut"][:, :],
                             rhs=PB(YB + ysl + 512, 512), start=True, stop=False)
                    e.matmul(out=ps_sc[:, 1024:1025], lhsT=sb["ut"][:, :],
                             rhs=PB(YB + ysl + 1024, 1), start=True, stop=False)
                    e.matmul(out=ps_sc[:, 0:512], lhsT=sb["selm"][:, m * 128:(m + 1) * 128],
                             rhs=g.pool[0:16, CSB + 2048:CSB + 2048 + 512],
                             start=False, stop=True)
                    e.matmul(out=ps_sc[:, 512:1024], lhsT=sb["selm"][:, m * 128:(m + 1) * 128],
                             rhs=g.pool[0:16, CSB + 2560:CSB + 2560 + 512],
                             start=False, stop=True)
                    tick(e.matmul(out=ps_sc[:, 1024:1025], lhsT=sb["selm"][:, m * 128:(m + 1) * 128],
                                  rhs=g.pool[0:16, CSB + 3072:CSB + 3072 + 1],
                                  start=False, stop=True))
                op("tensor", numm,
                   reads=(pr(YB + ysl, 1025), pr(CSB + 2048, 1025), rr("c_ut"), rr("c_selm")),
                   writes=(PSC(0, 1536),), tag=f"nm{b}_{m}")
                t = m
                sl2 = (t % 2) * 2048
                op("sync", is_dma_op=True, emit=lambda e, b=b, t=t, sl2=sl2: dma(e.dma_start(
                    out=PF(XF3 + sl2, 1024),
                    in_=ap["xres"][(b * T + t * 128):(b * T + (t + 1) * 128), :])),
                   reads=(rr("xres", b * T + t * 128, b * T + (t + 1) * 128),),
                   writes=(prf(XF3 + sl2, 1024),), tag=f"a3x2_{b}_{t}")
                def resid(e, sl2=sl2):
                    e.reciprocal(out=sb["tiny"][:, 16:17], in_=ps_sc[:, 1024:1025])
                    e.drain()
                    tick(e.scalar_tensor_tensor(out=PF(XF3 + sl2, 1024),
                                                in0=ps_sc[:, 0:1024],
                                                scalar=sb["tiny"][:, 16:17],
                                                in1=PF(XF3 + sl2, 1024),
                                                op0=ALU.mult, op1=ALU.add))
                op("vector", resid,
                   reads=(PSC(0, 1536), prf(XF3 + sl2, 1024)),
                   writes=(PSC(0, 1536), prf(XF3 + sl2, 1024), rr("tiny", 16, 17)),
                   tag=f"a3d{b}_{m}")
                op("gpsimd", is_dma_op=True, emit=lambda e, b=b, t=t, sl2=sl2: dma(e.dma_start(
                    out=ap["xres"][(b * T + t * 128):(b * T + (t + 1) * 128), :],
                    in_=PF(XF3 + sl2, 1024))),
                   reads=(prf(XF3 + sl2, 1024),),
                   writes=(rr("xres", b * T + t * 128, b * T + (t + 1) * 128),),
                   tag=f"a3w{b}_{t}")

    if n_phases >= 3:
        l1_attn()

    XF4 = 0
    YS4 = 4096
    XQ8 = 6144
    W4A = 38912
    W4B = 55296
    GT4 = 71680
    UP4 = 73728
    GUF = 75776
    GUB = 79872
    WL4 = 83968
    WB4 = 88064
    W4D = 38912
    OB4 = 71680
    Q8S = 79872
    Q8T = 88064

    def l1_ffn():
        for wi, (wname, bname) in enumerate([("wg1", "wg1_b"), ("wu1", "wu1_b"),
                                             ("wd1", "wd1_b")]):
            nrow = ap[wname].shape[0]
            ncol = ap[wname].shape[1]
            ntile = nrow // 128
            nct = ncol // 1024
            ci = 0
            for it in range(ntile):
                for ct in range(nct):
                    sl = ci % 2
                    op("sync", is_dma_op=True, emit=lambda e, wname=wname, it=it, ct=ct, sl=sl: dma(
                        e.dma_start(out=PF(WL4 + sl * 2048, 1024),
                                    in_=ap[wname][it * 128:(it + 1) * 128, ct * 1024:(ct + 1) * 1024])),
                       reads=(rr(wname),), writes=(prf(WL4 + sl * 2048, 1024),),
                       tag=f"m4l{wi}_{ci}")
                    om_ = op("scalar", lambda e, sl=sl, ci=ci, wi=wi: tick(e.activation(
                        out=PF(YS4, 1024), in_=PF(WL4 + sl * 2048, 1024), func=AF.Abs,
                        accum_out=sb["colx"][:, 160 + (ci % 32):161 + (ci % 32)])),
                       reads=(prf(WL4 + sl * 2048, 1024),),
                       writes=(prf(YS4, 1024), rr("colx", 160 + (ci % 32), 161 + (ci % 32))),
                       tag=f"m4a{wi}_{ci}")
                    if ci % 2 == 1:
                        heat(3, after=om_)
                    if ci % 32 == 31:
                        op("vector", lambda e, wi=wi, first=(ci == 31): (
                            e.tensor_reduce(out=sb["tiny"][:, 18:19], in_=sb["colx"][:, 160:192],
                                            axis=mybir.AxisListType.X, op=ALU.add),
                            e.drain(),
                            tick(e.tensor_tensor(out=sb["tiny"][:, 19:20],
                                                 in0=sb["tiny"][:, 18:19],
                                                 in1=sb["tiny"][:, 19:20], op=ALU.add)
                                 if not first else
                                 e.tensor_copy(out=sb["tiny"][:, 19:20], in_=sb["tiny"][:, 18:19]))),
                           reads=(rr("colx", 160, 192), rr("tiny", 19, 20)) if ci != 31 else (rr("colx", 160, 192),),
                           writes=(rr("tiny", 18, 20),), tag=f"m4c{wi}_{ci}")
                    ci += 1
            op("gpsimd", lambda e, wi=wi: tick(e.tensor_reduce(
                out=sb["tcc"][0:1, 5 + wi:6 + wi], in_=sb["tiny"][:, 19:20],
                axis=mybir.AxisListType.C, op=ALU.add)),
               reads=(rr("tiny", 19, 20),), writes=(rr("tcc", 5 + wi, 6 + wi),),
               tag=f"m4g{wi}")
            nelem = float(nrow * ncol)
            op("tensor", lambda e, wi=wi: tick(e.matmul(
                out=ps_dm[:, 0:1], lhsT=sb["onesf"][0:1, :],
                rhs=sb["tcc"][0:1, 5 + wi:6 + wi], start=True, stop=True)),
               reads=(rr("tcc", 5 + wi, 6 + wi), rr("c_onesf")), writes=(PDM(),),
               tag=f"m4b{wi}")
            def mkw(e, wi=wi, nelem=nelem):
                e.tensor_scalar(out=sb["glob"][:, 5 + wi:6 + wi], in0=ps_dm[:, 0:1],
                                scalar1=1.0 / nelem, scalar2=1e-5,
                                op0=ALU.mult, op1=ALU.max)
                e.drain()
                tick(e.reciprocal(out=sb["cols"][:, 40 + wi:41 + wi],
                                  in_=sb["glob"][:, 5 + wi:6 + wi]))
            op("vector", mkw, reads=(PDM(),),
               writes=(PDM(), rr("glob", 5 + wi, 6 + wi), rr("cols", 40 + wi, 41 + wi)),
               tag=f"m4k{wi}")
            ci = 0
            for it in range(ntile):
                for ct in range(nct):
                    sl = ci % 2
                    op("sync", is_dma_op=True, emit=lambda e, wname=wname, it=it, ct=ct, sl=sl: dma(
                        e.dma_start(out=PF(WL4 + sl * 2048, 1024),
                                    in_=ap[wname][it * 128:(it + 1) * 128, ct * 1024:(ct + 1) * 1024])),
                       reads=(rr(wname),), writes=(prf(WL4 + sl * 2048, 1024),),
                       tag=f"t4l{wi}_{ci}")
                    wv = PF(WL4 + sl * 2048, 1024)
                    wvr = prf(WL4 + sl * 2048, 1024)
                    op("vector", lambda e, sl=sl, wi=wi, wv=wv: tick(e.tensor_scalar(
                        out=wv, in0=wv, scalar1=sb["cols"][:, 40 + wi:41 + wi],
                        scalar2=MAGIC, op0=ALU.mult, op1=ALU.add)),
                       reads=(wvr, rr("cols", 40 + wi, 41 + wi)),
                       writes=(wvr,), tag=f"t4ta{wi}_{ci}")
                    op("vector", lambda e, wv=wv: tick(e.tensor_scalar(
                        out=wv, in0=wv, scalar1=MAGIC, scalar2=1.0,
                        op0=ALU.subtract, op1=ALU.min)),
                       reads=(wvr,), writes=(wvr,), tag=f"t4tb{wi}_{ci}")
                    ot_ = op("vector", lambda e, sl=sl, wv=wv: tick(e.tensor_scalar(
                        out=PB(WB4 + sl * 1024, 1024), in0=wv,
                        scalar1=-1.0, scalar2=None, op0=ALU.max)),
                       reads=(wvr,), writes=(pr(WB4 + sl * 1024, 1024),),
                       tag=f"t4tc{wi}_{ci}")
                    heat(3, after=ot_)
                    op("gpsimd", is_dma_op=True, emit=lambda e, bname=bname, it=it, ct=ct, sl=sl: dma(
                        e.dma_start(out=ap[bname][it * 128:(it + 1) * 128, ct * 1024:(ct + 1) * 1024],
                                    in_=PB(WB4 + sl * 1024, 1024))),
                       reads=(pr(WB4 + sl * 1024, 1024),),
                       writes=(rr(bname, it * nct + ct, it * nct + ct + 1),),
                       tag=f"t4w{wi}_{ci}")
                    ci += 1

        for t in range(NT):
            sl2 = (t % 2) * 2048
            op("sync", is_dma_op=True, emit=lambda e, t=t, sl2=sl2: dma(e.dma_start(
                out=PF(XF4 + sl2, 1024), in_=ap["xres"][t * 128:(t + 1) * 128, :])),
               reads=(rr("xres", t * 128, (t + 1) * 128),),
               writes=(prf(XF4 + sl2, 1024),), tag=f"q8x{t}")

            def stats(e, t=t, sl2=sl2):
                e.bn_stats(out=sb["tiny"][:, 0:6], in_=PF(XF4 + sl2, 512))
                e.bn_stats(out=sb["tiny"][:, 6:12], in_=PF(XF4 + sl2 + 1024, 512))
                e.drain()
                e.bn_aggr(out=sb["tiny"][:, 12:14], in_=sb["tiny"][:, 0:12])
                e.drain()
                e.tensor_copy(out=sb["colm"][:, t:t + 1], in_=sb["tiny"][:, 12:13])
                tick(e.tensor_scalar(out=sb["colx"][:, 160 + t:161 + t],
                                     in0=sb["tiny"][:, 13:14],
                                     scalar1=EPS, scalar2=None, op0=ALU.add))
            op("vector", stats, reads=(prf(XF4 + sl2, 1024),),
               writes=(rr("tiny", 0, 15), rr("colm", t, t + 1),
                       rr("colx", 160 + t, 161 + t)), tag=f"q8s{t}")
            jx = PF(YS4, 1024) if t % 2 == 0 else PF(UP4, 1024)
            jn = PF(GT4, 1024) if t % 2 == 0 else PF(GUF, 1024)
            jxr = prf(YS4, 1024) if t % 2 == 0 else prf(UP4, 1024)
            jnr = prf(GT4, 1024) if t % 2 == 0 else prf(GUF, 1024)
            op("vector", lambda e, t=t, sl2=sl2, jx=jx: tick(e.tensor_scalar(
                out=jx, in0=PF(XF4 + sl2, 1024),
                scalar1=1.0, scalar2=-1e30, op0=ALU.mult, op1=ALU.max,
                accum_out=sb["dcol"][:, t:t + 1])),
               reads=(prf(XF4 + sl2, 1024),),
               writes=(jxr, rr("dcol", t, t + 1)), tag=f"q8gx{t}")
            og = op("vector", lambda e, t=t, sl2=sl2, jn=jn: tick(e.tensor_scalar(
                out=jn, in0=PF(XF4 + sl2, 1024),
                scalar1=1.0, scalar2=1e30, op0=ALU.mult, op1=ALU.min,
                accum_out=sb["dcol"][:, 32 + t:33 + t])),
               reads=(prf(XF4 + sl2, 1024),),
               writes=(jnr, rr("dcol", 32 + t, 33 + t)), tag=f"q8gn{t}")
            heat(3, after=og)
        op("scalar", lambda e: (
            e.activation(out=sb["colx"][:, 160:160 + NT],
                         in_=sb["colx"][:, 160:160 + NT], func=AF.Ln),
            e.drain(),
            tick(e.activation(out=sb["colr"][:, 0:NT], in_=sb["colx"][:, 160:160 + NT],
                              func=AF.Exp, scale=-0.5))),
           reads=(rr("colx", 160, 160 + NT),),
           writes=(rr("colx", 160, 160 + NT), rr("colr", 0, NT)), tag="q8rb")
        def q8cols(e):
            e.tensor_tensor(out=sb["dcol"][:, 64:96], in0=sb["dcol"][:, 0:32],
                            in1=sb["colm"][:, 0:NT], op=ALU.subtract)
            e.scalar_tensor_tensor(out=sb["dcol"][:, 96:128], in0=sb["dcol"][:, 32:64],
                                   scalar=-1.0, in1=sb["colm"][:, 0:NT],
                                   op0=ALU.mult, op1=ALU.add)
            e.drain()
            e.tensor_tensor(out=sb["dcol"][:, 64:96], in0=sb["dcol"][:, 64:96],
                            in1=sb["dcol"][:, 96:128], op=ALU.max)
            e.drain()
            e.tensor_tensor(out=sb["dcol"][:, 64:96], in0=sb["dcol"][:, 64:96],
                            in1=sb["colr"][:, 0:NT], op=ALU.mult)
            e.drain()
            e.tensor_scalar(out=sb["colx"][:, 0:32], in0=sb["dcol"][:, 64:96],
                            scalar1=1e-5, scalar2=None, op0=ALU.max)
            e.drain()
            e.reciprocal(out=sb["dcol"][:, 96:128], in_=sb["colx"][:, 0:32])
            e.drain()
            tick(e.scalar_tensor_tensor(out=sb["dcol"][:, 96:128],
                                        in0=sb["dcol"][:, 96:128], scalar=127.0,
                                        in1=sb["colr"][:, 0:NT],
                                        op0=ALU.mult, op1=ALU.mult))
        op("vector", q8cols, reads=(rr("dcol", 0, 64), rr("colm", 0, NT),
                                    rr("colr", 0, NT)),
           writes=(rr("dcol", 64, 128), rr("colx", 0, 32)), tag="q8cols")

        def q8_ldx(t):
            sl2 = (t % 2) * 2048
            op("sync", is_dma_op=True, emit=lambda e, t=t, sl2=sl2: dma(e.dma_start(
                out=PF(XF4 + sl2, 1024), in_=ap["xres"][t * 128:(t + 1) * 128, :])),
               reads=(rr("xres", t * 128, (t + 1) * 128),),
               writes=(prf(XF4 + sl2, 1024),), tag=f"q8xb{t}")

        def q8_step(t, k):
            sl2 = (t % 2) * 2048
            xt = PF(XF4 + sl2, 1024)
            yv = PF(YS4, 1024) if t % 2 == 0 else PF(GT4, 1024)
            ybase = ("pool", YS4, YS4 + 2048) if t % 2 == 0 else ("pool", GT4, GT4 + 2048)
            if k == 0:
                return op("vector", lambda e, t=t, xt=xt, yv=yv: tick(e.scalar_tensor_tensor(
                    out=yv, in0=xt, scalar=sb["colm"][:, t:t + 1],
                    in1=sb["dcol"][:, 96 + t:97 + t].broadcast_to([128, 1024]),
                    op0=ALU.subtract, op1=ALU.mult)),
                   reads=(prf(XF4 + sl2, 1024), rr("colm", t, t + 1),
                          rr("dcol", 96 + t, 97 + t)),
                   writes=(ybase,), tag=f"q8qa{t}")
            elif k == 1:
                return op("vector", lambda e, yv=yv: tick(e.tensor_scalar(
                    out=yv, in0=yv, scalar1=MAGIC, scalar2=MAGIC,
                    op0=ALU.add, op1=ALU.subtract)),
                   reads=(ybase,), writes=(ybase,), tag=f"q8qb{t}")
            else:
                return op("vector", lambda e, t=t, yv=yv: tick(e.tensor_scalar(
                    out=PB(WB4 + (t % 2) * 1024, 1024), in0=yv,
                    scalar1=127.0, scalar2=-128.0, op0=ALU.min, op1=ALU.max)),
                   reads=(ybase,), writes=(pr(WB4 + (t % 2) * 1024, 1024),),
                   tag=f"q8qc{t}")

        def q8_tev(t):
            def xqT(e, t=t):
                for dc in range(8):
                    i = e.transpose(out=pt_op[:, (t % 2) * 1024 + dc * 128:(t % 2) * 1024 + (dc + 1) * 128],
                                    in_=PB(WB4 + (t % 2) * 1024 + dc * 128, 128),
                                    identity=sb["id"][:, :])
                tick(i)
            op("tensor", xqT, reads=(pr(WB4 + (t % 2) * 1024, 1024), rr("c_id")),
               writes=(POP((t % 2) * 512, 512),), tag=f"q8T{t}")
            op("vector", lambda e, t=t: tick(e.tensor_copy(
                out=PB(XQ8, 32768).rearrange("p (dc n) -> p dc n", n=4096)[:, :, t * 128:(t + 1) * 128],
                in_=pt_op[:, (t % 2) * 1024:(t % 2) * 1024 + 1024].rearrange("p (dc n) -> p dc n", n=128))),
               reads=(POP((t % 2) * 512, 512),),
               writes=(POP((t % 2) * 512, 512),) + tuple(
                   pr(XQ8 + dc * 4096 + t * 128, 128) for dc in range(8)),
               tag=f"q8e{t}")

        for tp in range(NT // 2):
            t0, t1 = 2 * tp, 2 * tp + 1
            q8_ldx(t0)
            q8_ldx(t1)
            for k in range(3):
                ok_ = q8_step(t0, k)
                q8_step(t1, k)
                if k == 1:
                    heat(4, after=ok_)
            q8_tev(t0)
            q8_tev(t1)

        def mkoscale(e):
            e.scalar_tensor_tensor(out=sb["colx"][:, 32:64], in0=sb["colx"][:, 0:32],
                                   scalar=1.0 / 127.0,
                                   in1=sb["glob"][:, 5:6].broadcast_to([128, 32]),
                                   op0=ALU.mult, op1=ALU.mult)
            tick(e.scalar_tensor_tensor(out=sb["colx"][:, 64:96], in0=sb["colx"][:, 0:32],
                                        scalar=1.0 / 127.0,
                                        in1=sb["glob"][:, 6:7].broadcast_to([128, 32]),
                                        op0=ALU.mult, op1=ALU.mult))
        op("vector", mkoscale, reads=(rr("colx", 0, 32), rr("glob", 5, 7)),
           writes=(rr("colx", 32, 96),), tag="q8os")

        for hf in range(2):
            for dc in range(8):
                op("sync", is_dma_op=True, emit=lambda e, dc=dc, hf=hf: dma(
                    e.dma_start_transpose(out=PB(W4A + dc * 2048, 2048),
                                          in_=ap["wg1_b"][hf * 2048:(hf + 1) * 2048, dc * 128:(dc + 1) * 128])),
                   reads=(rr("wg1_b"),), writes=(pr(W4A + dc * 2048, 2048),),
                   tag=f"wgT4{hf}_{dc}")
                op("sync", is_dma_op=True, emit=lambda e, dc=dc, hf=hf: dma(
                    e.dma_start_transpose(out=PB(W4B + dc * 2048, 2048),
                                          in_=ap["wu1_b"][hf * 2048:(hf + 1) * 2048, dc * 128:(dc + 1) * 128])),
                   reads=(rr("wu1_b"),), writes=(pr(W4B + dc * 2048, 2048),),
                   tag=f"wuT4{hf}_{dc}")
            for t in range(NT):
                for ic8 in range(4):
                    psl = ic8 % 2
                    def gmm(e, t=t, ic8=ic8, psl=psl):
                        for dc in range(8):
                            i = e.matmul(out=ps_sc[:, psl * 512:(psl + 1) * 512],
                                         lhsT=PB(XQ8 + dc * 4096 + t * 128, 128),
                                         rhs=PB(W4A + dc * 2048 + ic8 * 512, 512),
                                         start=(dc == 0), stop=(dc == 7))
                        tick(i)
                    op("tensor", gmm, reads=(pr(XQ8, 32768), pr(W4A, 16384)),
                       writes=(PSC(psl * 512, 512),), tag=f"gm4{hf}_{t}_{ic8}")
                    op("scalar", lambda e, t=t, ic8=ic8, psl=psl: tick(e.activation(
                        out=PB(GT4 + ic8 * 512, 512),
                        in_=ps_sc[:, psl * 512:(psl + 1) * 512], func=AF.Silu,
                        scale=sb["colx"][:, 32 + t:33 + t])),
                       reads=(PSC(psl * 512, 512), rr("colx", 32, 64)),
                       writes=(PSC(psl * 512, 512), pr(GT4 + ic8 * 512, 512)),
                       tag=f"ge4{hf}_{t}_{ic8}")
                    def umm(e, t=t, ic8=ic8, psl=psl):
                        for dc in range(8):
                            i = e.matmul(out=ps_sc[:, 1024 + psl * 512:1024 + (psl + 1) * 512],
                                         lhsT=PB(XQ8 + dc * 4096 + t * 128, 128),
                                         rhs=PB(W4B + dc * 2048 + ic8 * 512, 512),
                                         start=(dc == 0), stop=(dc == 7))
                        tick(i)
                    op("tensor", umm, reads=(pr(XQ8, 32768), pr(W4B, 16384)),
                       writes=(PSC(1024 + psl * 512, 512),), tag=f"um4{hf}_{t}_{ic8}")
                    op("vector", lambda e, t=t, ic8=ic8, psl=psl: tick(e.tensor_scalar(
                        out=PB(UP4 + ic8 * 512, 512),
                        in0=ps_sc[:, 1024 + psl * 512:1024 + (psl + 1) * 512],
                        scalar1=sb["colx"][:, 64 + t:65 + t], scalar2=None,
                        op0=ALU.mult)),
                       reads=(PSC(1024 + psl * 512, 512), rr("colx", 64, 96)),
                       writes=(PSC(1024 + psl * 512, 512), pr(UP4 + ic8 * 512, 512)),
                       tag=f"ue4{hf}_{t}_{ic8}")
                gsl = t % 2
                op("vector", lambda e, t=t: tick(e.tensor_tensor(
                    out=PF(GUF, 2048), in0=PB(GT4, 2048),
                    in1=PB(UP4, 2048), op=ALU.mult)),
                   reads=(pr(GT4, 2048), pr(UP4, 2048)),
                   writes=(prf(GUF, 2048),), tag=f"gu4{hf}_{t}")
                mxdst = sb["colx"][:, 96 + t:97 + t] if hf == 0 else sb["dcol"][:, t:t + 1]
                mndst = sb["colx"][:, 128 + t:129 + t] if hf == 0 else sb["dcol"][:, 32 + t:33 + t]
                mxr = rr("colx", 96 + t, 97 + t) if hf == 0 else rr("dcol", t, t + 1)
                mnr = rr("colx", 128 + t, 129 + t) if hf == 0 else rr("dcol", 32 + t, 33 + t)
                op("vector", lambda e, t=t, gsl=gsl, mxdst=mxdst: tick(e.tensor_scalar(
                    out=PB(GUB + gsl * 2048, 2048), in0=PF(GUF, 2048),
                    scalar1=1.0, scalar2=-1e30, op0=ALU.mult, op1=ALU.max,
                    accum_out=mxdst)),
                   reads=(prf(GUF, 2048),),
                   writes=(pr(GUB + gsl * 2048, 2048), mxr), tag=f"gux{hf}_{t}")
                op("vector", lambda e, t=t, mndst=mndst: tick(e.tensor_scalar(
                    out=PB(YS4, 2048), in0=PF(GUF, 2048),
                    scalar1=1.0, scalar2=1e30, op0=ALU.mult, op1=ALU.min,
                    accum_out=mndst)),
                   reads=(prf(GUF, 2048),),
                   writes=(pr(YS4, 2048), mnr), tag=f"gun{hf}_{t}")
                op("gpsimd", is_dma_op=True, emit=lambda e, t=t, hf=hf, gsl=gsl: dma(
                    e.dma_start(out=ap["gu_sc"][t, :, hf * 2048:(hf + 1) * 2048],
                                in_=PB(GUB + gsl * 2048, 2048))),
                   reads=(pr(GUB + gsl * 2048, 2048),),
                   writes=(rr("gu_sc", t * 2 + hf, t * 2 + hf + 1),), tag=f"gs4{hf}_{t}")

        def bfold(e):
            e.tensor_tensor(out=sb["colx"][:, 96:128], in0=sb["colx"][:, 96:128],
                            in1=sb["dcol"][:, 0:32], op=ALU.max)
            e.tensor_tensor(out=sb["colx"][:, 128:160], in0=sb["colx"][:, 128:160],
                            in1=sb["dcol"][:, 32:64], op=ALU.min)
            e.drain()
            e.scalar_tensor_tensor(out=sb["colx"][:, 96:128], in0=sb["colx"][:, 128:160],
                                   scalar=-1.0, in1=sb["colx"][:, 96:128],
                                   op0=ALU.mult, op1=ALU.max)
            e.drain()
            e.tensor_scalar(out=sb["colx"][:, 96:128], in0=sb["colx"][:, 96:128],
                            scalar1=1e-5, scalar2=None, op0=ALU.max)
            e.drain()
            e.reciprocal(out=sb["dcol"][:, 64:96], in_=sb["colx"][:, 96:128])
            e.scalar_tensor_tensor(out=sb["dcol"][:, 96:128], in0=sb["colx"][:, 96:128],
                                   scalar=1.0 / 127.0,
                                   in1=sb["glob"][:, 7:8].broadcast_to([128, 32]),
                                   op0=ALU.mult, op1=ALU.mult)
            e.drain()
            tick(e.tensor_scalar(out=sb["dcol"][:, 64:96], in0=sb["dcol"][:, 64:96],
                                 scalar1=127.0, scalar2=None, op0=ALU.mult))
        op("vector", bfold, reads=(rr("colx", 96, 160), rr("dcol", 0, 64),
                                   rr("glob", 7, 8)),
           writes=(rr("colx", 96, 160), rr("dcol", 64, 128)), tag="bfold")

        for ic in range(I1 // 128):
            op("sync", is_dma_op=True, emit=lambda e, ic=ic: dma(
                e.dma_start_transpose(out=PB(W4D + ic * 1024, 1024),
                                      in_=ap["wd1_b"][:, ic * 128:(ic + 1) * 128])),
               reads=(rr("wd1_b"),), writes=(pr(W4D + ic * 1024, 1024),), tag=f"wdT4{ic}")
        for t in range(NT):
            gsl = t % 2
            op("sync", is_dma_op=True, emit=lambda e, t=t, gsl=gsl: dma(e.dma_start(
                out=PB(OB4 + gsl * 4096, 4096), in_=ap["gu_sc"][t])),
               reads=(rr("gu_sc", t * 2, t * 2 + 2),),
               writes=(pr(OB4 + gsl * 4096, 4096),), tag=f"gl4{t}")

            for k in range(3):
                for hh in range(2):
                    qv = PF(XQ8 + hh * 4096, 2048)
                    qvr = prf(XQ8 + hh * 4096, 2048)
                    if k == 0:
                        op("vector", lambda e, t=t, gsl=gsl, hh=hh, qv=qv: tick(
                            e.tensor_scalar(
                                out=qv, in0=PB(OB4 + gsl * 4096 + hh * 2048, 2048),
                                scalar1=sb["dcol"][:, 64 + t:65 + t], scalar2=MAGIC,
                                op0=ALU.mult, op1=ALU.add)),
                           reads=(pr(OB4 + gsl * 4096 + hh * 2048, 2048),
                                  rr("dcol", 64 + t, 65 + t)),
                           writes=(qvr,), tag=f"q24a{t}_{hh}")
                    elif k == 1:
                        op("vector", lambda e, qv=qv: tick(e.tensor_scalar(
                            out=qv, in0=qv, scalar1=MAGIC, scalar2=127.0,
                            op0=ALU.subtract, op1=ALU.min)),
                           reads=(qvr,), writes=(qvr,), tag=f"q24b{t}_{hh}")
                    else:
                        op("vector", lambda e, hh=hh, qv=qv: tick(e.tensor_scalar(
                            out=PB(Q8T + hh * 2048, 2048), in0=qv,
                            scalar1=-128.0, scalar2=None, op0=ALU.max)),
                           reads=(qvr,), writes=(pr(Q8T + hh * 2048, 2048),),
                           tag=f"q24c{t}_{hh}")
            for half in range(2):
                def trh(e, t=t, half=half):
                    for c in range(16):
                        i = e.transpose(out=pt_op[:, c * 128:(c + 1) * 128],
                                        in_=PB(Q8T + (half * 16 + c) * 128, 128),
                                        identity=sb["id"][:, :])
                    tick(i)
                op("tensor", trh, reads=(pr(Q8T, 4096), rr("c_id")),
                   writes=(POP(0, 1024),), tag=f"q2T{t}_{half}")
                op("vector", lambda e, t=t, half=half, gsl=gsl: tick(e.tensor_copy(
                    out=PB(Q8S + gsl * 4096 + half * 2048, 2048),
                    in_=pt_op[:, 0:2048])),
                   reads=(POP(0, 1024),),
                   writes=(POP(0, 1024), pr(Q8S + gsl * 4096 + half * 2048, 2048)),
                   tag=f"q2E{t}_{half}")

            def dmm4(e, t=t, gsl=gsl):
                for dd in range(2):
                    for ic in range(32):
                        i = e.matmul(out=ps_sc[:, dd * 512:(dd + 1) * 512],
                                     lhsT=PB(Q8S + gsl * 4096 + ic * 128, 128),
                                     rhs=PB(W4D + ic * 1024 + dd * 512, 512),
                                     start=(ic == 0), stop=(ic == 31))
                    tick(i)
            op("tensor", dmm4, reads=(pr(Q8S + gsl * 4096, 4096), pr(W4D, 32768)),
               writes=(PSC(0, 1024),), tag=f"dm4{t}")
            sl2 = (t % 2) * 2048
            op("sync", is_dma_op=True, emit=lambda e, t=t, sl2=sl2: dma(e.dma_start(
                out=PF(XF4 + sl2, 1024), in_=ap["xres"][t * 128:(t + 1) * 128, :])),
               reads=(rr("xres", t * 128, (t + 1) * 128), prf(XF4, 4096)),
               writes=(prf(XF4 + sl2, 1024),), tag=f"q8x2_{t}")
            op("vector", lambda e, t=t, sl2=sl2: tick(e.scalar_tensor_tensor(
                out=PF(XF4 + sl2, 1024), in0=ps_sc[:, 0:1024],
                scalar=sb["dcol"][:, 96 + t:97 + t], in1=PF(XF4 + sl2, 1024),
                op0=ALU.mult, op1=ALU.add)),
               reads=(PSC(0, 1024), prf(XF4 + sl2, 1024), rr("dcol", 96 + t, 97 + t)),
               writes=(PSC(0, 1024), prf(XF4 + sl2, 1024)),
               tag=f"dr4{t}")
            op("gpsimd", is_dma_op=True, emit=lambda e, t=t, sl2=sl2: dma(e.dma_start(
                out=ap["out"].rearrange("b tt d -> (b tt) d")[t * 128:(t + 1) * 128, :],
                in_=PF(XF4 + sl2, 1024))),
               reads=(prf(XF4 + sl2, 1024),),
               writes=(rr("OUT", t * 128, (t + 1) * 128),), tag=f"ow4{t}")

    if n_phases >= 4:
        l1_ffn()

    if n_phases < 4:
        op("sync", is_dma_op=True, emit=lambda e: dma(e.dma_start(
            out=ap["out"].rearrange("b t d -> (b t) d"), in_=ap["xres"])),
           reads=(rr("xres"),), writes=(rr("OUT"),), tag="out")
    op("sync", emit=lambda e: None, reads=(rr("OUT"),), writes=(), tag="fin")


def _consts():
    idb = np.eye(128).astype(BF)
    idf = np.eye(128).astype(np.float32)
    ones = np.ones((128, 128), BF)
    ut = np.triu(np.ones((128, 128))).astype(BF)
    t16 = (np.arange(16)[:, None] < np.arange(16)[None, :]).astype(BF)
    ind = np.zeros((128, 256), BF)
    for m in range(16):
        ind[:, m * 16 + m] = 1
    selm = np.zeros((16, 2048), BF)
    for m in range(16):
        selm[m, m * 128:(m + 1) * 128] = 1
    return dict(id_bf=idb, id_f=idf, ones_bf=ones, ut128=ut, t16c=t16, ind16=ind,
                ones_f=np.ones((1, 128), np.float32), selm16=selm)


def kernel(L0_attn_x, L0_attn_w_q, L0_attn_w_kv, L0_attn_w_out, L0_attn_b_out,
           L0_ffn_w_gate, L0_ffn_w_up, L0_ffn_w_down, L1_attn_W1, L1_attn_b1_,
           L1_attn_W2, L1_attn_b2_, L1_ffn_w_gate, L1_ffn_w_up, L1_ffn_w_down):
    x = np.ascontiguousarray(L0_attn_x)
    shared = dict(
        wq=np.ascontiguousarray(L0_attn_w_q), wkv=np.ascontiguousarray(L0_attn_w_kv),
        wout=np.ascontiguousarray(L0_attn_w_out),
        bout=np.ascontiguousarray(L0_attn_b_out).reshape(1, D),
        wg0=np.ascontiguousarray(L0_ffn_w_gate), wu0=np.ascontiguousarray(L0_ffn_w_up),
        wd0=np.ascontiguousarray(L0_ffn_w_down), w1=np.ascontiguousarray(L1_attn_W1),
        b1=np.ascontiguousarray(L1_attn_b1_).reshape(1, D),
        w2=np.ascontiguousarray(L1_attn_W2).reshape(1, D),
        b2=np.ascontiguousarray(L1_attn_b2_).reshape(1, 1),
        wg1=np.ascontiguousarray(L1_ffn_w_gate), wu1=np.ascontiguousarray(L1_ffn_w_up),
        wd1=np.ascontiguousarray(L1_ffn_w_down), **_consts())
    nc = build_nc(N_PHASES)
    in_maps = []
    for ci in range(8):
        m = dict(shared)
        m["x_in"] = x[2 * ci:2 * ci + 2]
        in_maps.append(m)
    res = run_bass_kernel_spmd(nc, in_maps, core_ids=list(range(8)))
    return np.concatenate([r["out"] for r in res.results], axis=0)


# version 08 (final)
# speedup vs baseline: 1.0253x; 1.0189x over previous
import numpy as np
import ml_dtypes
from contextlib import ExitStack

import concourse.bass as bass
import concourse.mybir as mybir
from concourse.bass_utils import run_bass_kernel_spmd

F32 = mybir.dt.float32
BF16 = mybir.dt.bfloat16
AF = mybir.ActivationFunctionType
ALU = mybir.AluOpType
BF = ml_dtypes.bfloat16

MAGIC = float(np.float32(1.5 * 2 ** 23))
B_PC = 2
T = 2048
D = 1024
TOK = B_PC * T
NT = TOK // 128
NTB = T // 128
H = 16
DH = 64
I0 = 3072
I1 = 4096
EPS = 1e-5
VW = 65
VEW = H * VW + 63
PCOLS = 100288

ENGINES = ("sync", "tensor", "vector", "scalar", "gpsimd")

N_PHASES = 4


class Op:
    __slots__ = ("eng", "emit", "post", "idx", "ord", "is_dma", "deps", "tag",
                 "dsem", "dval", "dprev", "_demit", "is_coll")

    def __init__(self, eng, emit, idx, tag):
        self.eng = eng
        self.emit = emit
        self.idx = idx
        self.post = None
        self.ord = 0
        self.is_dma = False
        self.deps = set()
        self.tag = tag
        self.dsem = None
        self.dval = 0
        self.dprev = 0
        self._demit = 0
        self.is_coll = False


NDPOOL = 40


class Sched:
    def __init__(self, nc, ctx):
        self.nc = nc
        self.ops = []
        self.sems = {e: ctx.enter_context(nc.semaphore("prog_" + e)) for e in ENGINES}
        self.cnt = {e: 0 for e in ENGINES}
        self._cur = None
        self.segs = {}
        self.dpool = []
        self.dpool_of = {}
        for e in ("sync", "gpsimd", "scalar", "vector", "tensor"):
            npool = {"sync": 20, "gpsimd": 14}.get(e, 2)
            base = len(self.dpool)
            self.dpool += [ctx.enter_context(nc.semaphore(f"dma_{e}{i}")) for i in range(npool)]
            self.dpool_of[e] = (base, npool)
        self.dcnt = [0] * len(self.dpool)
        self.dhist = [[] for _ in self.dpool]
        self.dseq = {e: 0 for e in ENGINES}
        self.ordn = {e: 0 for e in ENGINES}

    def _touch(self, o, ranges, is_write):
        for (sp, lo, hi) in ranges:
            lst = self.segs.setdefault(sp, [])
            if is_write:
                keep = []
                for seg in lst:
                    if seg[0] < hi and lo < seg[1]:
                        if seg[2] is not None:
                            o.deps.add(seg[2])
                        for r in seg[3]:
                            o.deps.add(r)
                        if seg[0] < lo:
                            keep.append([seg[0], min(seg[1], lo), seg[2], list(seg[3])])
                        if seg[1] > hi:
                            keep.append([max(seg[0], hi), seg[1], seg[2], list(seg[3])])
                    else:
                        keep.append(seg)
                keep.append([lo, hi, o, []])
                self.segs[sp] = keep
            else:
                for seg in lst:
                    if seg[0] < hi and lo < seg[1]:
                        if seg[2] is not None:
                            o.deps.add(seg[2])
                        seg[3].append(o)

    def op(self, eng, emit, reads=(), writes=(), tag="", is_dma_op=False, ndma=1,
           is_coll=False):
        o = Op(eng, emit, len(self.ops), tag)
        o.ord = self.ordn[eng]
        self.ordn[eng] += 1
        o.is_coll = is_coll
        if is_dma_op:
            o.is_dma = True
            base, npool = self.dpool_of[eng]
            o.dsem = base + self.dseq[eng] % npool
            self.dseq[eng] += 1
            sidx = o.dsem
            o.dprev = self.dcnt[sidx]
            self.dcnt[sidx] += 16 * ndma
            o.dval = self.dcnt[sidx]
            self.dhist[sidx].append((o.idx, o.dval))
        self._touch(o, reads, False)
        self._touch(o, writes, True)
        o.deps.discard(o)
        for d in o.deps:
            assert d.idx < o.idx, (d.idx, o.idx, tag, d.tag)
        self.ops.append(o)
        return o

    def emit_all(self, block):
        import bisect
        watched_ord = {e: {f: -1 for f in ENGINES} for e in ENGINES}
        dwatched = {e: [0] * len(self.dpool) for e in ENGINES}
        fixups = []

        by_eng = {e: [] for e in ENGINES}
        for o in self.ops:
            by_eng[o.eng].append(o)

        def mk_emit(_run, _eng):
            def emit_run(e):
                for o in _run:
                    for d in sorted(o.deps, key=lambda d: d.idx):
                        if d.is_dma:
                            hist = self.dhist[d.dsem]
                            k = bisect.bisect_left(hist, (o.idx, -1)) - 1
                            val = hist[k][1]
                            if val > dwatched[_eng][d.dsem]:
                                e.wait_ge(self.dpool[d.dsem], val)
                                dwatched[_eng][d.dsem] = val
                        elif d.eng != _eng or d.is_coll or _eng != "tensor":
                            if d.ord > watched_ord[_eng][d.eng]:
                                if d.post is not None:
                                    e.wait_ge(self.sems[d.eng], d.post)
                                else:
                                    fixups.append(
                                        (e.wait_ge(self.sems[d.eng], 1), d))
                                watched_ord[_eng][d.eng] = d.ord
                    if o.is_dma and o.dprev > 0 and o.dprev > dwatched[_eng][o.dsem]:
                        e.wait_ge(self.dpool[o.dsem], o.dprev)
                        dwatched[_eng][o.dsem] = o.dprev
                    self._cur = o
                    o.emit(e)
                    self._cur = None
                    o.post = self.cnt[_eng]
            return emit_run

        for eng in ENGINES:
            if by_eng[eng]:
                getattr(block, eng)(mk_emit(by_eng[eng], eng))

        for inst, d in fixups:
            assert d.post is not None, d.tag
            ws = inst.ins.sync_info.on_wait
            assert len(ws) == 1, (d.tag, ws)
            ws[0].wait_value = d.post

    def tick(self, inst, n=1):
        o = self._cur
        self.cnt[o.eng] += n
        inst.then_inc(self.sems[o.eng], n)
        return inst

    def dma(self, inst):
        o = self._cur
        assert o.is_dma, f"op {o.tag} must be created with is_dma_op=True"
        o._demit += 16
        assert o.dprev + o._demit <= o.dval, f"too many dma_starts in {o.tag}"
        inst.then_inc(self.dpool[o.dsem], 16)
        return inst


class NS:
    def __init__(self, d):
        self.__dict__.update(d)


def build_nc(n_phases=4):
    nc = bass.Bass("TRN2", target_bir_lowering=False)
    ap = {}

    def DI(name, shape, dt=F32):
        ap[name] = nc.dram_tensor(name, shape, dt, kind="ExternalInput").ap()

    DI("x_in", [B_PC, T, D]); DI("wq", [D, H * DH]); DI("wkv", [D, 2 * H * DH])
    DI("wout", [H * DH, D]); DI("bout", [1, D])
    DI("wg0", [I0, D]); DI("wu0", [I0, D]); DI("wd0", [D, I0])
    DI("w1", [D, D]); DI("b1", [1, D]); DI("w2", [1, D]); DI("b2", [1, 1])
    DI("wg1", [I1, D]); DI("wu1", [I1, D]); DI("wd1", [D, I1])
    DI("id_bf", [128, 128], BF16); DI("id_f", [128, 128], F32)
    DI("ones_bf", [128, 128], BF16); DI("ut128", [128, 128], BF16)
    DI("t16c", [16, 16], BF16); DI("ind16", [128, 256], BF16)
    DI("ones_f", [1, 128], F32); DI("selm16", [16, 2048], BF16)
    ap["out"] = nc.dram_tensor("out", [B_PC, T, D], F32, kind="ExternalOutput").ap()

    def DS(name, shape, dt):
        ap[name] = nc.dram_tensor(name, shape, dt).ap()

    DS("xres", [TOK, D], F32)
    DS("wq_b", [D, H * DH], BF16); DS("wkv_b", [D, 2 * H * DH], BF16)
    DS("wout_b", [H * DH, D], BF16); DS("bout_b", [1, D], BF16)
    DS("w1_b", [D, D], BF16)
    DS("wg0_b", [I0, D], BF16); DS("wu0_b", [I0, D], BF16); DS("wd0_b", [D, I0], BF16)
    DS("wg1_b", [I1, D], BF16); DS("wu1_b", [I1, D], BF16); DS("wd1_b", [D, I1], BF16)
    DS("up_sc", [I0 // 128, 128, TOK], BF16)
    DS("gu_sc", [NT, 128, I1], BF16)
    DS("den_hb", [64, 512], BF16)
    DS("rden_hb", [256, 128], BF16)
    DS("gate_sc", [I0 // 128, 128, TOK], BF16)
    for i in range(1, 6):
        DS(f"cci{i}", [1, 16], F32)
        DS(f"cco{i}", [1, 16], F32)

    with ExitStack() as ctx0:
        sch = Sched(nc, ctx0)
        S = lambda n, sh, dt: ctx0.enter_context(nc.sbuf_tensor(n, sh, dt))
        P = lambda n, sh, dt: ctx0.enter_context(nc.psum_tensor(n, sh, dt))

        sb = {}
        for n, shdt in [("id", ([128, 128], BF16)), ("idf", ([128, 128], F32)),
                        ("ones", ([128, 128], BF16)), ("ut", ([128, 128], BF16)),
                        ("t16", ([16, 16], BF16)), ("ind", ([128, 256], BF16)),
                        ("colm", ([128, NT], F32)), ("colr", ([128, NT], F32)),
                        ("cols", ([128, 64], F32)), ("colx", ([128, 192], F32)),
                        ("glob", ([128, 16], F32)), ("cbias", ([128, 8], F32)),
                        ("dcol", ([128, 256], F32)), ("dcolb", ([128, 256], BF16)),
                        ("drows", ([128, 256], BF16)), ("dstage", ([128, 1024], BF16)),
                        ("tiny", ([128, 32], F32)), ("tcc", ([1, 16], F32)),
                        ("tcc2", ([128, 16], F32)), ("onesf", ([1, 128], F32)),
                        ("selm", ([16, 2048], BF16)), ("b2sb", ([1, 2], F32))]:
            sb[n] = S("sb_" + n, *shdt)

        pool = S("pool", [128, PCOLS], BF16)
        poolf = pool[:, :].bitcast(F32)

        def PB(c0, n):
            return pool[:, c0:c0 + n]

        def PBr(r0, r1, c0, n):
            return pool[r0:r1, c0:c0 + n]

        def PF(c0, n):
            return poolf[:, c0 // 2: c0 // 2 + n]

        ps_sc = P("ps_sc", [128, 2048], F32)
        ps_cx = P("ps_cx", [128, 512], F32)
        ps_dm = P("ps_dm", [128, 512], F32)
        ps_op = P("ps_op", [128, 1024], F32)
        pt_op = ps_op[:, :].bitcast(BF16)

        block = ctx0.enter_context(nc.Block())
        env = NS(dict(nc=nc, ap=ap, sb=sb, pool=pool, poolf=poolf, PB=PB,
                      PBr=PBr, PF=PF, ps_sc=ps_sc, ps_cx=ps_cx, ps_dm=ps_dm,
                      ps_op=ps_op, pt_op=pt_op, sch=sch))
        build_program(env, n_phases)
        sch.emit_all(block)
    return nc


def build_program(g, n_phases):
    sch = g.sch
    ap, sb = g.ap, g.sb
    PB, PBr, PF = g.PB, g.PBr, g.PF
    ps_sc, ps_cx, ps_dm, ps_op, pt_op = g.ps_sc, g.ps_cx, g.ps_dm, g.ps_op, g.pt_op
    op, tick, dma = sch.op, sch.tick, sch.dma

    def pr(c0, n):
        return ("pool", c0, c0 + n)

    def prf(c0, n):
        return ("pool", c0, c0 + 2 * n)

    def rr(name, lo=0, hi=1 << 30):
        return (name, lo, hi)

    PSC = lambda lo, n: ("ps_sc", lo, lo + n)
    PCX = lambda: ("ps_cx", 0, 512)
    PDM = lambda: ("ps_dm", 0, 512)
    POP = lambda lo, n: ("ps_op", lo, lo + n)

    for dst, src in [("id", "id_bf"), ("idf", "id_f"), ("ones", "ones_bf"),
                     ("ut", "ut128"), ("t16", "t16c"), ("ind", "ind16"),
                     ("onesf", "ones_f"), ("selm", "selm16")]:
        op("sync", is_dma_op=True, emit=lambda e, d=dst, s=src: dma(e.dma_start(out=sb[d][:, :], in_=ap[s])),
           reads=(rr(src),), writes=(rr("c_" + dst),), tag="const")
    for dst, src in [("wq_b", "wq"), ("wkv_b", "wkv"), ("wout_b", "wout"),
                     ("bout_b", "bout"), ("w1_b", "w1")]:
        op("gpsimd", is_dma_op=True, emit=lambda e, d=dst, s=src: dma(e.dma_start(out=ap[d], in_=ap[s])),
           reads=(rr(src),), writes=(rr(dst),), tag="cast")
    op("vector", lambda e: (e.memset(sb["cbias"][:, 0:1], -12.0),
                            e.memset(sb["cbias"][:, 1:2], 0.0),
                            e.memset(sb["cbias"][:, 2:3], MAGIC),
                            tick(e.memset(sb["cbias"][:, 3:4], -MAGIC))),
       writes=(rr("cbias", 0, 4),), tag="cbias")
    op("vector", lambda e: tick(e.memset(sb["tcc"][0:1, :], 0.0)),
       writes=(rr("tcc", 0, 16),), tag="tccz")

    XA = 0
    LN = 4096
    QP = 20480
    KF = 53248
    VE = 69632
    EB = 87296
    WS = 89344
    LB = 93440
    CX = 0
    WO = 53248
    XD = 69632
    BO = 87296

    def l0_attn():
        for b in range(B_PC):
            op("vector", lambda e: tick(e.memset(PB(QP, 32768), 0.0)),
               writes=(pr(QP, 32768),), tag=f"zq{b}")

            def vext_fill(e):
                e.tensor_copy(
                    out=PB(VE, NTB * VEW).rearrange("p (kt w) -> p kt w", w=VEW)[:, :, H * VW - 1:],
                    in_=sb["ones"][:, 0:1].unsqueeze(2).broadcast_to([128, NTB, 64]))
                e.drain()
                vv = PB(VE, NTB * VEW).rearrange("p (kt w) -> p kt w", w=VEW)[:, :, 0:H * VW]
                vv = vv.rearrange("p kt (hh w) -> p kt hh w", w=VW)
                tick(e.tensor_copy(out=vv[:, :, :, 64:65],
                                   in_=sb["ones"][:, 0:1].unsqueeze(2).unsqueeze(3).broadcast_to([128, NTB, H, 1])))
            op("vector", vext_fill, reads=(rr("c_ones"),),
               writes=(pr(VE, NTB * VEW),), tag=f"vones{b}")

            for t in range(NTB):
                ti = b * NTB + t
                sl2 = (t % 2) * 2048
                op("sync", is_dma_op=True, emit=lambda e, b=b, t=t, sl2=sl2: dma(e.dma_start(
                    out=PF(XA + sl2, 1024),
                    in_=ap["x_in"][b, t * 128:(t + 1) * 128, :])),
                   reads=(rr("x_in"),), writes=(prf(XA + sl2, 1024),), tag=f"xa{ti}")

                def stats(e, ti=ti, t=t, sl2=sl2):
                    e.bn_stats(out=sb["tiny"][:, 0:6], in_=PF(XA + sl2, 512))
                    e.bn_stats(out=sb["tiny"][:, 6:12], in_=PF(XA + sl2 + 1024, 512))
                    e.drain()
                    e.bn_aggr(out=sb["tiny"][:, 12:14], in_=sb["tiny"][:, 0:12])
                    e.drain()
                    e.tensor_copy(out=sb["colm"][:, ti:ti + 1], in_=sb["tiny"][:, 12:13])
                    tick(e.tensor_scalar(out=sb["colx"][:, 160 + t:161 + t],
                                         in0=sb["tiny"][:, 13:14],
                                         scalar1=EPS, scalar2=None, op0=ALU.add))
                op("vector", stats, reads=(prf(XA + sl2, 1024),),
                   writes=(rr("tiny", 0, 15), rr("colm", ti, ti + 1),
                           rr("colx", 160 + t, 161 + t)), tag=f"st{ti}")
            op("scalar", lambda e, b=b: (
                e.activation(out=sb["colx"][:, 160:160 + NTB], in_=sb["colx"][:, 160:160 + NTB], func=AF.Ln),
                e.drain(),
                tick(e.activation(out=sb["colr"][:, b * NTB:(b + 1) * NTB],
                                  in_=sb["colx"][:, 160:160 + NTB], func=AF.Exp, scale=-0.5))),
               reads=(rr("colx", 160, 160 + NTB),),
               writes=(rr("colx", 160, 160 + NTB), rr("colr", b * NTB, (b + 1) * NTB)),
               tag=f"rsb{b}")
            for t in range(NTB):
                ti = b * NTB + t
                sl2 = (t % 2) * 2048
                op("sync", is_dma_op=True, emit=lambda e, b=b, t=t, sl2=sl2: dma(e.dma_start(
                    out=PF(XA + sl2, 1024),
                    in_=ap["x_in"][b, t * 128:(t + 1) * 128, :])),
                   reads=(rr("x_in"),), writes=(prf(XA + sl2, 1024),), tag=f"xb{ti}")

                def lnb(e, ti=ti, sl2=sl2):
                    tick(e.tensor_scalar(out=PB(LB + (ti % 2) * 1024, 1024),
                                         in0=PF(XA + sl2, 1024),
                                         scalar1=sb["colm"][:, ti:ti + 1],
                                         scalar2=sb["colr"][:, ti:ti + 1],
                                         op0=ALU.subtract, op1=ALU.mult))
                op("vector", lnb,
                   reads=(prf(XA + sl2, 1024), rr("colm", ti, ti + 1), rr("colr", ti, ti + 1)),
                   writes=(pr(LB + (ti % 2) * 1024, 1024),), tag=f"lnb{ti}")

                def lnT(e, ti=ti):
                    sl = (ti % 2) * 1024
                    for dc in range(8):
                        i = e.transpose(out=pt_op[:, sl + dc * 128: sl + (dc + 1) * 128],
                                        in_=PB(LB + (ti % 2) * 1024 + dc * 128, 128),
                                        identity=sb["id"][:, :])
                    tick(i)
                op("tensor", lnT,
                   reads=(pr(LB + (ti % 2) * 1024, 1024), rr("c_id")),
                   writes=(POP((ti % 2) * 512, 512),), tag=f"lnT{ti}")

                def lnE(e, ti=ti, t=t):
                    sl = (ti % 2) * 1024
                    tick(e.tensor_copy(
                        out=PB(LN, 16384).rearrange("p (dc n) -> p dc n", n=2048)[:, :, t * 128:(t + 1) * 128],
                        in_=pt_op[:, sl:sl + 1024].rearrange("p (dc n) -> p dc n", n=128)))
                op("vector", lnE, reads=(POP((ti % 2) * 512, 512),),
                   writes=(POP((ti % 2) * 512, 512),) + tuple(
                       pr(LN + dc * 2048 + t * 128, 128) for dc in range(8)),
                   tag=f"lnE{ti}")

            NQG, NKG = H, H // 2
            ev = 0
            for li in range(NQG + NKG + 2):
                wsl = WS + (li % 2) * 2048
                if li < NQG:
                    h = li
                    op("sync", is_dma_op=True, emit=lambda e, h=h, wsl=wsl: dma(e.dma_start(
                        out=PB(wsl, 512).rearrange("p (dc n) -> p dc n", n=64),
                        in_=ap["wq_b"][:, h * 64:(h + 1) * 64].rearrange("(dc p) n -> p dc n", p=128))),
                       reads=(rr("wq_b"),), writes=(pr(wsl, 512),), tag=f"wq{b}_{h}")
                    for tc in range(4):
                        sl = ev % 4
                        def mmq(e, h=h, tc=tc, sl=sl, wsl=wsl):
                            for dc in range(8):
                                i = e.matmul(
                                    out=ps_sc[(h % 2) * 64:(h % 2) * 64 + 64, sl * 512:(sl + 1) * 512],
                                    lhsT=PB(wsl + dc * 64, 64),
                                    rhs=PB(LN + dc * 2048 + tc * 512, 512),
                                    start=(dc == 0), stop=(dc == 7),
                                    tile_position=(0, (h % 2) * 64))
                            tick(i)
                        op("tensor", mmq, reads=(pr(wsl, 512), pr(LN, 16384)),
                           writes=(PSC(sl * 512, 512),), tag=f"mmq{b}_{h}_{tc}")
                        def evq(e, h=h, tc=tc, sl=sl):
                            tick(e.tensor_copy(
                                out=PBr((h % 2) * 64, (h % 2) * 64 + 64,
                                        QP + h * 2048 + tc * 512, 512),
                                in_=ps_sc[(h % 2) * 64:(h % 2) * 64 + 64, sl * 512:(sl + 1) * 512]))
                        op("vector", evq,
                           reads=(PSC(sl * 512, 512), pr(QP + h * 2048 + tc * 512, 512)),
                           writes=(PSC(sl * 512, 512), pr(QP + h * 2048 + tc * 512, 512)),
                           tag=f"evq{b}_{h}_{tc}")
                        ev += 1
                elif li < NQG + NKG:
                    hp = li - NQG
                    op("sync", is_dma_op=True, emit=lambda e, hp=hp, wsl=wsl: dma(e.dma_start(
                        out=PB(wsl, 1024).rearrange("p (dc n) -> p dc n", n=128),
                        in_=ap["wkv_b"][:, hp * 128:(hp + 1) * 128].rearrange("(dc p) n -> p dc n", p=128))),
                       reads=(rr("wkv_b"),), writes=(pr(wsl, 1024),), tag=f"wk{b}_{hp}")
                    for tc in range(4):
                        sl = ev % 4
                        def mmk(e, tc=tc, sl=sl, wsl=wsl):
                            for dc in range(8):
                                i = e.matmul(
                                    out=ps_sc[:, sl * 512:(sl + 1) * 512],
                                    lhsT=PB(wsl + dc * 128, 128),
                                    rhs=PB(LN + dc * 2048 + tc * 512, 512),
                                    start=(dc == 0), stop=(dc == 7))
                            tick(i)
                        op("tensor", mmk, reads=(pr(wsl, 1024), pr(LN, 16384)),
                           writes=(PSC(sl * 512, 512),), tag=f"mmk{b}_{hp}_{tc}")
                        def evk(e, hp=hp, tc=tc, sl=sl):
                            tick(e.tensor_copy(
                                out=PB(KF + hp * 2048 + tc * 512, 512),
                                in_=ps_sc[:, sl * 512:(sl + 1) * 512]))
                        op("vector", evk, reads=(PSC(sl * 512, 512),),
                           writes=(PSC(sl * 512, 512), pr(KF + hp * 2048 + tc * 512, 512)),
                           tag=f"evk{b}_{hp}_{tc}")
                        ev += 1
                else:
                    vc = li - NQG - NKG
                    op("sync", is_dma_op=True, emit=lambda e, vc=vc: dma(e.dma_start(
                        out=PB(WS, 4096).rearrange("p (dc n) -> p dc n", n=512),
                        in_=ap["wkv_b"][:, 1024 + vc * 512:1024 + (vc + 1) * 512].rearrange("(dc p) n -> p dc n", p=128))),
                       reads=(rr("wkv_b"),), writes=(pr(WS, 4096),), tag=f"wv{b}_{vc}")
                    for kt in range(NTB):
                        sl = ev % 4
                        def mmv(e, kt=kt, sl=sl):
                            for dc in range(8):
                                i = e.matmul(
                                    out=ps_sc[:, sl * 512:(sl + 1) * 512],
                                    lhsT=PB(LN + dc * 2048 + kt * 128, 128),
                                    rhs=PB(WS + dc * 512, 512),
                                    start=(dc == 0), stop=(dc == 7))
                            tick(i)
                        op("tensor", mmv, reads=(pr(WS, 4096), pr(LN, 16384)),
                           writes=(PSC(sl * 512, 512),), tag=f"mmv{b}_{vc}_{kt}")
                        def evv(e, kt=kt, vc=vc, sl=sl):
                            vv = PB(VE + kt * VEW + vc * 8 * VW, 8 * VW)
                            vv = vv.rearrange("p (hh w) -> p hh w", w=VW)[:, :, 0:64]
                            tick(e.tensor_copy(
                                out=vv,
                                in_=ps_sc[:, sl * 512:(sl + 1) * 512].rearrange("p (hh n) -> p hh n", n=64)))
                        op("vector", evv, reads=(PSC(sl * 512, 512),),
                           writes=(PSC(sl * 512, 512),
                                   pr(VE + kt * VEW + vc * 8 * VW, 8 * VW)),
                           tag=f"evv{b}_{vc}_{kt}")
                        ev += 1

            for h in range(H):
                hp = h // 2
                for qc in range(4):
                    qi = (b * H + h) * 4 + qc
                    cxb = ps_cx if qi % 2 == 0 else ps_dm
                    cxr = PCX() if qi % 2 == 0 else PDM()
                    def mk_qk(p):
                        gs = p % 2
                        def qk(e, hp=hp, h=h, qc=qc, p=p, gs=gs):
                            for j in range(2):
                                kt = 2 * p + j
                                i = e.matmul(
                                    out=ps_sc[:, gs * 1024 + j * 512: gs * 1024 + (j + 1) * 512],
                                    lhsT=PB(KF + hp * 2048 + kt * 128, 128),
                                    rhs=PB(QP + h * 2048 + qc * 512, 512),
                                    start=True, stop=True)
                            tick(i)
                        op("tensor", qk,
                           reads=(pr(KF + hp * 2048, 2048), pr(QP + h * 2048 + qc * 512, 512)),
                           writes=(PSC(gs * 1024, 1024),), tag=f"qk{qi}_{p}")
                        def ex(e, gs=gs):
                            tick(e.activation(out=PB(EB + gs * 1024, 1024),
                                              in_=ps_sc[:, gs * 1024:(gs + 1) * 1024],
                                              func=AF.Exp, scale=0.125,
                                              bias=sb["cbias"][:, 0:1]))
                        op("scalar", ex, reads=(PSC(gs * 1024, 1024), rr("cbias", 0, 1)),
                           writes=(PSC(gs * 1024, 1024), pr(EB + gs * 1024, 1024)),
                           tag=f"ex{qi}_{p}")

                    def mk_av(p):
                        gs = p % 2
                        def av(e, h=h, p=p, gs=gs, cxb=cxb):
                            for j in range(2):
                                kt = 2 * p + j
                                i = e.matmul(
                                    out=cxb[:, 0:512],
                                    lhsT=PB(VE + kt * VEW + h * VW, 128),
                                    rhs=PB(EB + gs * 1024 + j * 512, 512),
                                    start=(p == 0 and j == 0), stop=(p == 7 and j == 1))
                            tick(i)
                        op("tensor", av,
                           reads=(pr(EB + gs * 1024, 1024), pr(VE, NTB * VEW)),
                           writes=(cxr,), tag=f"av{qi}_{p}")

                    mk_qk(0)
                    for p in range(1, 8):
                        mk_qk(p)
                        mk_av(p - 1)
                    mk_av(7)
                    def cxe(e, h=h, qc=qc, qi=qi, cxb=cxb):
                        e.tensor_copy(
                            out=PBr(0, 64, CX + h * 2048 + qc * 512, 512),
                            in_=cxb[0:64, 0:512])
                        tick(e.tensor_copy(
                            out=sb["dstage"][64:65, (qi % 2) * 512:(qi % 2) * 512 + 512],
                            in_=cxb[64:65, 0:512]))
                    op("vector", cxe, reads=(cxr,),
                       writes=(cxr, pr(CX + h * 2048 + qc * 512, 512),
                               rr("dstage", (qi % 2) * 512, (qi % 2) * 512 + 512)),
                       tag=f"cxe{qi}")
                    op("gpsimd", is_dma_op=True, emit=lambda e, h=h, qc=qc, qi=qi: dma(e.dma_start(
                        out=ap["den_hb"][h * 4 + qc: h * 4 + qc + 1, :],
                        in_=sb["dstage"][64:65, (qi % 2) * 512:(qi % 2) * 512 + 512])),
                       reads=(rr("dstage", (qi % 2) * 512, (qi % 2) * 512 + 512),),
                       writes=(rr("den_hb", h * 4 + qc, h * 4 + qc + 1),), tag=f"dh{qi}")

            for s in range(4):
                op("sync", is_dma_op=True, emit=lambda e, s=s: dma(e.dma_start_transpose(
                    out=sb["dcolb"][:, s * 64:(s + 1) * 64],
                    in_=ap["den_hb"][0:64, s * 128:(s + 1) * 128])),
                   reads=(rr("den_hb"),), writes=(rr("dcolb", s * 64, (s + 1) * 64),),
                   tag=f"dt{b}_{s}")

            def recip(e):
                e.reciprocal(out=sb["dcol"][:, :], in_=sb["dcolb"][:, :])
                e.drain()
                tick(e.tensor_copy(out=sb["dcolb"][:, :], in_=sb["dcol"][:, :]))
            op("vector", recip, reads=(rr("dcolb"),),
               writes=(rr("dcolb"), rr("dcol")), tag=f"recip{b}")

            for Bb in range(2):
                op("sync", is_dma_op=True, emit=lambda e, Bb=Bb: dma(e.dma_start_transpose(
                    out=sb["drows"][:, Bb * 128:(Bb + 1) * 128],
                    in_=sb["dcolb"][:, Bb * 128:(Bb + 1) * 128])),
                   reads=(rr("dcolb", Bb * 128, (Bb + 1) * 128),),
                   writes=(rr("drows", Bb * 128, (Bb + 1) * 128),), tag=f"dtr{b}_{Bb}")
            op("gpsimd", is_dma_op=True, emit=lambda e: dma(e.dma_start(
                out=ap["rden_hb"].rearrange("(bb r) q -> r bb q", bb=2),
                in_=sb["drows"][:, :].rearrange("r (bb q) -> r bb q", q=128))),
               reads=(rr("drows"),), writes=(rr("rden_hb"),), tag=f"rdw{b}")

            for h in range(H):
                for qc in range(4):
                    i4 = (b * H + h) * 4 + qc
                    dmb = ps_cx if i4 % 2 == 0 else ps_dm
                    dmr = PCX() if i4 % 2 == 0 else PDM()
                    hq = h * 4 + qc
                    rsl = (i4 % 2) * 512
                    op("sync", is_dma_op=True, emit=lambda e, hq=hq, rsl=rsl: dma(e.dma_start(
                        out=sb["dstage"][0:1, rsl:rsl + 512].rearrange("p (s n) -> p s n", n=128),
                        in_=ap["rden_hb"][hq::64, :].unsqueeze(0))),
                       reads=(rr("rden_hb"),),
                       writes=(rr("dstage", 1024 + rsl, 1024 + rsl + 512),), tag=f"rdg{i4}")
                    def dmm(e, rsl=rsl, dmb=dmb):
                        tick(e.matmul(out=dmb[0:64, 0:512],
                                      lhsT=sb["ones"][0:1, 0:64],
                                      rhs=sb["dstage"][0:1, rsl:rsl + 512],
                                      start=True, stop=True))
                    op("tensor", dmm,
                       reads=(rr("dstage", 1024 + rsl, 1024 + rsl + 512), rr("c_ones")),
                       writes=(dmr,), tag=f"dmm{i4}")
                    def nrm(e, h=h, qc=qc, dmb=dmb):
                        tick(e.tensor_tensor(
                            out=PBr(0, 64, CX + h * 2048 + qc * 512, 512),
                            in0=PBr(0, 64, CX + h * 2048 + qc * 512, 512),
                            in1=dmb[0:64, 0:512], op=ALU.mult))
                    op("vector", nrm,
                       reads=(dmr, pr(CX + h * 2048 + qc * 512, 512)),
                       writes=(dmr, pr(CX + h * 2048 + qc * 512, 512)), tag=f"nrm{i4}")

            for h in range(H):
                op("sync", is_dma_op=True, emit=lambda e, h=h: dma(e.dma_start(
                    out=PBr(0, 64, WO + h * 1024, 1024),
                    in_=ap["wout_b"][h * 64:(h + 1) * 64, :])),
                   reads=(rr("wout_b"),), writes=(pr(WO + h * 1024, 1024),),
                   tag=f"wo{b}_{h}")
            op("sync", is_dma_op=True, emit=lambda e: dma(e.dma_start(out=g.pool[0:1, BO:BO + 1024],
                                                 in_=ap["bout_b"])),
               reads=(rr("bout_b"),), writes=(pr(BO, 1024),), tag=f"bo{b}")
            for t in range(NTB):
                ti = b * NTB + t
                sl2 = (t % 2) * 2048
                op("sync", is_dma_op=True, emit=lambda e, b=b, t=t, sl2=sl2: dma(e.dma_start(
                    out=PF(XD + sl2, 1024),
                    in_=ap["x_in"][b, t * 128:(t + 1) * 128, :])),
                   reads=(rr("x_in"),), writes=(prf(XD + sl2, 1024),), tag=f"xd{ti}")
                def opj(e, t=t):
                    for dd in range(2):
                        for h in range(H):
                            i = e.matmul(out=ps_op[:, dd * 512:(dd + 1) * 512],
                                         lhsT=PBr(0, 64, CX + h * 2048 + t * 128, 128),
                                         rhs=PBr(0, 64, WO + h * 1024 + dd * 512, 512),
                                         start=(h == 0), stop=False)
                        i = e.matmul(out=ps_op[:, dd * 512:(dd + 1) * 512],
                                     lhsT=sb["ones"][0:1, 0:128],
                                     rhs=g.pool[0:1, BO + dd * 512: BO + (dd + 1) * 512],
                                     start=False, stop=True)
                    tick(i)
                op("tensor", opj,
                   reads=(pr(CX, 32768), pr(WO, 16384), pr(BO, 1024), rr("c_ones")),
                   writes=(POP(0, 1024),), tag=f"opj{ti}")
                def resid(e, sl2=sl2):
                    tick(e.scalar_tensor_tensor(out=PF(XD + sl2, 1024),
                                                in0=ps_op[:, 0:1024], scalar=1.0,
                                                in1=PF(XD + sl2, 1024),
                                                op0=ALU.mult, op1=ALU.add))
                op("vector", resid, reads=(POP(0, 1024), prf(XD + sl2, 1024)),
                   writes=(POP(0, 1024), prf(XD + sl2, 1024)), tag=f"res{ti}")
                op("gpsimd", is_dma_op=True, emit=lambda e, b=b, t=t, sl2=sl2: dma(e.dma_start(
                    out=ap["xres"][(b * T + t * 128):(b * T + (t + 1) * 128), :],
                    in_=PF(XD + sl2, 1024))),
                   reads=(prf(XD + sl2, 1024),),
                   writes=(rr("xres", b * T + t * 128, b * T + (t + 1) * 128),),
                   tag=f"xw{ti}")

    l0_attn()

    XF2 = 0
    YJ = 4096
    XQ = 12288
    WL = 45056
    WB2 = 49152
    GU = 53248
    EL = 61440
    WT = 69632
    EL2 = 12288
    OI = 49152
    DS8 = 53248
    NIC = I0 // 128

    def lpbq_prep(wname, bname, ntile, ncol):
        nct = ncol // 1024
        tiles = [(it, ct) for it in range(ntile) for ct in range(nct)]

        def lp_load(ti):
            it, ct = tiles[ti]
            sl = ti % 2
            src = ap[wname][it * 128:(it + 1) * 128, ct * 1024:(ct + 1) * 1024]
            op("sync", is_dma_op=True,
               emit=lambda e, src=src, sl=sl: dma(e.dma_start(out=PF(WL + sl * 2048, 1024), in_=src)),
               reads=(rr(wname),), writes=(prf(WL + sl * 2048, 1024),),
               tag=f"lq{wname}{ti}")

        def lp_step(ti, k):
            it, ct = tiles[ti]
            sl = ti % 2
            wv = PF(WL + sl * 2048, 1024)
            wvr = prf(WL + sl * 2048, 1024)
            R = sb["cols"] if sl == 0 else sb["dcol"]
            Rn = "cols" if sl == 0 else "dcol"
            ev = PF(EL + sl * 2048, 1024)
            evr = prf(EL + sl * 2048, 1024)
            if k == 0:
                op("vector", lambda e, wv=wv, R=R: tick(e.tensor_reduce(
                    out=R[:, 0:32], in_=wv.rearrange("p (b k) -> p b k", k=32),
                    axis=mybir.AxisListType.X, op=ALU.max,
                    apply_absolute_value=True)),
                   reads=(wvr,), writes=(rr(Rn, 0, 32),), tag=f"dqa{wname}{ti}")
            elif k == 1:
                op("vector", lambda e, R=R: tick(e.tensor_scalar(
                    out=R[:, 32:64], in0=R[:, 0:32], scalar1=1.0 / 7.0,
                    scalar2=1e-12, op0=ALU.mult, op1=ALU.max)),
                   reads=(rr(Rn, 0, 32),), writes=(rr(Rn, 32, 64),), tag=f"dqb{wname}{ti}")
            elif k == 2:
                op("vector", lambda e, R=R: tick(e.reciprocal(
                    out=R[:, 0:32], in_=R[:, 32:64])),
                   reads=(rr(Rn, 32, 64),), writes=(rr(Rn, 0, 32),), tag=f"dqc{wname}{ti}")
            elif k == 3:
                op("vector", lambda e, wv=wv, ev=ev, R=R: tick(e.tensor_tensor(
                    out=ev.rearrange("p (b k) -> p b k", k=32),
                    in0=wv.rearrange("p (b k) -> p b k", k=32),
                    in1=R[:, 0:32].unsqueeze(2).broadcast_to([128, 32, 32]),
                    op=ALU.mult)),
                   reads=(wvr, rr(Rn, 0, 32)), writes=(evr,), tag=f"dqd{wname}{ti}")
            elif k == 4:
                op("vector", lambda e, ev=ev: tick(e.tensor_scalar(
                    out=ev, in0=ev, scalar1=MAGIC, scalar2=MAGIC,
                    op0=ALU.add, op1=ALU.subtract)),
                   reads=(evr,), writes=(evr,), tag=f"dqe{wname}{ti}")
            elif k == 5:
                op("vector", lambda e, ev=ev: tick(e.tensor_scalar(
                    out=ev, in0=ev, scalar1=7.0, scalar2=-8.0,
                    op0=ALU.min, op1=ALU.max)),
                   reads=(evr,), writes=(evr,), tag=f"dqf{wname}{ti}")
            else:
                op("vector", lambda e, ev=ev, R=R, sl=sl: tick(e.tensor_tensor(
                    out=PB(WB2 + sl * 1024, 1024).rearrange("p (b k) -> p b k", k=32),
                    in0=ev.rearrange("p (b k) -> p b k", k=32),
                    in1=R[:, 32:64].unsqueeze(2).broadcast_to([128, 32, 32]),
                    op=ALU.mult)),
                   reads=(evr, rr(Rn, 32, 64)),
                   writes=(pr(WB2 + sl * 1024, 1024),), tag=f"dqg{wname}{ti}")

        def lp_store(ti):
            it, ct = tiles[ti]
            sl = ti % 2
            dst = ap[bname][it * 128:(it + 1) * 128, ct * 1024:(ct + 1) * 1024]
            op("gpsimd", is_dma_op=True,
               emit=lambda e, dst=dst, sl=sl: dma(e.dma_start(out=dst, in_=PB(WB2 + sl * 1024, 1024))),
               reads=(pr(WB2 + sl * 1024, 1024),),
               writes=(rr(bname, ti, ti + 1),), tag=f"wqw{wname}{ti}")

        n = len(tiles)
        for p in range(0, n, 2):
            pts = [p] + ([p + 1] if p + 1 < n else [])
            for ti in pts:
                lp_load(ti)
            for k in range(7):
                for ti in pts:
                    lp_step(ti, k)
            for ti in pts:
                lp_store(ti)

    def ground(gi, nsrc, cci, cco):
        def comb(e, nsrc=nsrc, gi=gi):
            e.scalar_tensor_tensor(out=sb["colx"][:, 128:128 + nsrc],
                                   in0=sb["colx"][:, 64:64 + nsrc], scalar=-1.0,
                                   in1=sb["colx"][:, 0:nsrc], op0=ALU.mult, op1=ALU.max)
            e.drain()
            tick(e.tensor_reduce(out=sb["tcc2"][:, gi:gi + 1],
                                 in_=sb["colx"][:, 128:128 + nsrc],
                                 axis=mybir.AxisListType.X, op=ALU.max))
        op("vector", comb, reads=(rr("colx", 0, 64 + nsrc),),
           writes=(rr("colx", 128, 128 + nsrc), rr("tcc2", gi, gi + 1)), tag=f"gc{gi}")
        op("gpsimd", lambda e, gi=gi: tick(e.tensor_reduce(
            out=sb["tcc"][0:1, gi:gi + 1], in_=sb["tcc2"][:, gi:gi + 1],
            axis=mybir.AxisListType.C, op=ALU.max)),
           reads=(rr("tcc2", gi, gi + 1),), writes=(rr("tcc", gi, gi + 1),), tag=f"gp{gi}")
        op("gpsimd", is_dma_op=True,
           emit=lambda e, cci=cci: dma(e.dma_start(out=ap[cci], in_=sb["tcc"][0:1, :])),
           reads=(rr("tcc"),), writes=(rr(cci),), tag=f"gw{gi}")
        op("gpsimd", lambda e, cci=cci, cco=cco: tick(e.collective_compute(
            "AllReduce", ALU.max, replica_groups=[list(range(8))],
            ins=[ap[cci].opt()], outs=[ap[cco].opt()])),
           reads=(rr(cci),), writes=(rr(cco),), tag=f"cc{gi}", is_coll=True)
        op("gpsimd", is_dma_op=True,
           emit=lambda e, cco=cco, gi=gi: dma(e.dma_start(
               out=sb["tcc"][0:1, 8 + gi:8 + gi + 1], in_=ap[cco][0:1, gi:gi + 1])),
           reads=(rr(cco),), writes=(rr("tcc", 8 + gi, 8 + gi + 1),), tag=f"gr{gi}")
        op("tensor", lambda e, gi=gi: tick(e.matmul(
            out=ps_dm[:, 0:1], lhsT=sb["onesf"][0:1, :],
            rhs=sb["tcc"][0:1, 8 + gi:8 + gi + 1], start=True, stop=True)),
           reads=(rr("tcc", 8 + gi, 8 + gi + 1), rr("c_onesf")), writes=(PDM(),),
           tag=f"gb{gi}")
        op("vector", lambda e, gi=gi: tick(e.tensor_copy(
            out=sb["glob"][:, gi:gi + 1], in_=ps_dm[:, 0:1])),
           reads=(PDM(),), writes=(PDM(), rr("glob", gi, gi + 1)), tag=f"ge{gi}")

    def l0_ffn():
        lpbq_prep("wg0", "wg0_b", NIC, D)
        lpbq_prep("wu0", "wu0_b", NIC, D)
        lpbq_prep("wd0", "wd0_b", D // 128, I0)

        for t in range(NT):
            sl = t % 2
            op("sync", is_dma_op=True, emit=lambda e, t=t, sl=sl: dma(e.dma_start(
                out=PF(XF2 + sl * 2048, 1024), in_=ap["xres"][t * 128:(t + 1) * 128, :])),
               reads=(rr("xres", t * 128, (t + 1) * 128),),
               writes=(prf(XF2 + sl * 2048, 1024),), tag=f"f2x{t}")

            def stats(e, t=t, sl=sl):
                e.bn_stats(out=sb["tiny"][:, 0:6], in_=PF(XF2 + sl * 2048, 512))
                e.bn_stats(out=sb["tiny"][:, 6:12], in_=PF(XF2 + sl * 2048 + 1024, 512))
                e.drain()
                e.bn_aggr(out=sb["tiny"][:, 12:14], in_=sb["tiny"][:, 0:12])
                e.drain()
                e.tensor_copy(out=sb["colm"][:, t:t + 1], in_=sb["tiny"][:, 12:13])
                tick(e.tensor_scalar(out=sb["colx"][:, 160 + t:161 + t], in0=sb["tiny"][:, 13:14],
                                     scalar1=EPS, scalar2=None, op0=ALU.add))
            op("vector", stats, reads=(prf(XF2 + sl * 2048, 1024),),
               writes=(rr("tiny", 0, 15), rr("colm", t, t + 1),
                       rr("colx", 160 + t, 161 + t)),
               tag=f"f2s{t}")
            jx = PF(YJ, 1024) if sl == 0 else PF(XQ, 1024)
            jn = PF(YJ + 2048, 1024) if sl == 0 else PF(XQ + 2048, 1024)
            jxr = prf(YJ, 1024) if sl == 0 else prf(XQ, 1024)
            jnr = prf(YJ + 2048, 1024) if sl == 0 else prf(XQ + 2048, 1024)
            op("vector", lambda e, t=t, sl=sl, jx=jx: tick(e.tensor_scalar(
                out=jx, in0=PF(XF2 + sl * 2048, 1024),
                scalar1=1.0, scalar2=-1e30, op0=ALU.mult, op1=ALU.max,
                accum_out=sb["dcol"][:, t:t + 1])),
               reads=(prf(XF2 + sl * 2048, 1024),),
               writes=(jxr, rr("dcol", t, t + 1)), tag=f"f2gx{t}")
            op("vector", lambda e, t=t, sl=sl, jn=jn: tick(e.tensor_scalar(
                out=jn, in0=PF(XF2 + sl * 2048, 1024),
                scalar1=1.0, scalar2=1e30, op0=ALU.mult, op1=ALU.min,
                accum_out=sb["dcol"][:, 32 + t:33 + t])),
               reads=(prf(XF2 + sl * 2048, 1024),),
               writes=(jnr, rr("dcol", 32 + t, 33 + t)), tag=f"f2gn{t}")
        op("scalar", lambda e: (
            e.activation(out=sb["colx"][:, 160:160 + NT], in_=sb["colx"][:, 160:160 + NT], func=AF.Ln),
            e.drain(),
            tick(e.activation(out=sb["colr"][:, 0:NT], in_=sb["colx"][:, 160:160 + NT],
                              func=AF.Exp, scale=-0.5))),
           reads=(rr("colx", 160, 160 + NT),),
           writes=(rr("colx", 160, 160 + NT), rr("colr", 0, NT)), tag="f2rsb")
        def scale_cols(e):
            e.tensor_tensor(out=sb["colx"][:, 0:NT], in0=sb["dcol"][:, 0:NT],
                            in1=sb["colm"][:, 0:NT], op=ALU.subtract)
            e.tensor_tensor(out=sb["colx"][:, 64:64 + NT], in0=sb["dcol"][:, 32:32 + NT],
                            in1=sb["colm"][:, 0:NT], op=ALU.subtract)
            e.drain()
            e.tensor_tensor(out=sb["colx"][:, 0:NT], in0=sb["colx"][:, 0:NT],
                            in1=sb["colr"][:, 0:NT], op=ALU.mult)
            tick(e.tensor_tensor(out=sb["colx"][:, 64:64 + NT], in0=sb["colx"][:, 64:64 + NT],
                                 in1=sb["colr"][:, 0:NT], op=ALU.mult))
        op("vector", scale_cols, reads=(rr("colr", 0, NT), rr("dcol", 0, 64),
                                        rr("colm", 0, NT)),
           writes=(rr("colx", 0, NT), rr("colx", 64, 64 + NT)), tag="f2sc")
        ground(0, NT, "cci1", "cco1")
        def mkscale0(e):
            e.reciprocal(out=sb["cols"][:, 0:1], in_=sb["glob"][:, 0:1])
            e.drain()
            e.tensor_scalar(out=sb["cols"][:, 0:1], in0=sb["cols"][:, 0:1],
                            scalar1=32767.0, scalar2=None, op0=ALU.mult)
            e.tensor_scalar(out=sb["cols"][:, 1:2], in0=sb["glob"][:, 0:1],
                            scalar1=1.0 / 32767.0, scalar2=None, op0=ALU.mult)
            e.drain()
            tick(e.tensor_tensor(out=sb["cols"][:, 8:8 + NT], in0=sb["colr"][:, 0:NT],
                                 in1=sb["cols"][:, 0:1].broadcast_to([128, NT]),
                                 op=ALU.mult))
        op("vector", mkscale0, reads=(rr("glob", 0, 1), rr("colr", 0, NT)),
           writes=(rr("cols", 0, 2), rr("cols", 8, 8 + NT)), tag="f2k0")

        def f2_ldx(t):
            sl = t % 2
            op("sync", is_dma_op=True, emit=lambda e, t=t, sl=sl: dma(e.dma_start(
                out=PF(XF2 + sl * 2048, 1024), in_=ap["xres"][t * 128:(t + 1) * 128, :])),
               reads=(rr("xres", t * 128, (t + 1) * 128),),
               writes=(prf(XF2 + sl * 2048, 1024),), tag=f"f2x2_{t}")

        def f2_tev(t):
            def xqT(e, t=t):
                for dc in range(8):
                    i = e.transpose(out=pt_op[:, (t % 2) * 1024 + dc * 128:(t % 2) * 1024 + (dc + 1) * 128],
                                    in_=PB(WB2 + (t % 2) * 1024 + dc * 128, 128),
                                    identity=sb["id"][:, :])
                tick(i)
            op("tensor", xqT, reads=(pr(WB2 + (t % 2) * 1024, 1024), rr("c_id")),
               writes=(POP((t % 2) * 512, 512),), tag=f"f2t{t}")
            op("vector", lambda e, t=t: tick(e.tensor_copy(
                out=PB(XQ, 32768).rearrange("p (dc n) -> p dc n", n=4096)[:, :, t * 128:(t + 1) * 128],
                in_=pt_op[:, (t % 2) * 1024:(t % 2) * 1024 + 1024].rearrange("p (dc n) -> p dc n", n=128))),
               reads=(POP((t % 2) * 512, 512),),
               writes=(POP((t % 2) * 512, 512),) + tuple(
                   pr(XQ + dc * 4096 + t * 128, 128) for dc in range(8)),
               tag=f"f2e{t}")

        for tp in range(NT // 2):
            t0, t1 = 2 * tp, 2 * tp + 1
            f2_ldx(t0)
            f2_ldx(t1)
            def one(t, k):
                sl = t % 2
                xt = PF(XF2 + sl * 2048, 1024)
                yv = PF(YJ + sl * 2048, 1024)
                if k == 0:
                    op("vector", lambda e, t=t, xt=xt, yv=yv: tick(e.tensor_scalar(
                        out=yv, in0=xt, scalar1=sb["colm"][:, t:t + 1],
                        scalar2=sb["cols"][:, 8 + t:8 + t + 1],
                        op0=ALU.subtract, op1=ALU.mult)),
                       reads=(prf(XF2 + sl * 2048, 1024), rr("colm", t, t + 1),
                              rr("cols", 8 + t, 9 + t)),
                       writes=(prf(YJ + sl * 2048, 1024),), tag=f"f2qa{t}")
                elif k == 1:
                    op("vector", lambda e, yv=yv: tick(e.tensor_scalar(
                        out=yv, in0=yv, scalar1=MAGIC, scalar2=MAGIC,
                        op0=ALU.add, op1=ALU.subtract)),
                       reads=(prf(YJ + sl * 2048, 1024),),
                       writes=(prf(YJ + sl * 2048, 1024),), tag=f"f2qb{t}")
                elif k == 2:
                    op("vector", lambda e, yv=yv: tick(e.tensor_scalar(
                        out=yv, in0=yv, scalar1=32767.0, scalar2=-32768.0,
                        op0=ALU.min, op1=ALU.max)),
                       reads=(prf(YJ + sl * 2048, 1024),),
                       writes=(prf(YJ + sl * 2048, 1024),), tag=f"f2qc{t}")
                else:
                    op("vector", lambda e, t=t, yv=yv: tick(e.tensor_scalar(
                        out=PB(WB2 + (t % 2) * 1024, 1024), in0=yv,
                        scalar1=sb["cols"][:, 1:2], scalar2=None, op0=ALU.mult)),
                       reads=(prf(YJ + sl * 2048, 1024), rr("cols", 1, 2)),
                       writes=(pr(WB2 + (t % 2) * 1024, 1024),), tag=f"f2qd{t}")
            for k in range(4):
                one(t0, k)
                one(t1, k)
            f2_tev(t0)
            f2_tev(t1)

        def wT_loads(bname, tagp):
            for dc in range(8):
                op("sync", is_dma_op=True, emit=lambda e, dc=dc, bname=bname: dma(
                    e.dma_start_transpose(out=PB(WT + dc * 3072, 3072),
                                          in_=ap[bname][:, dc * 128:(dc + 1) * 128])),
                   reads=(rr(bname),), writes=(pr(WT + dc * 3072, 3072),),
                   tag=f"{tagp}{dc}")

        def mm_ug(scname, colbase, tagp):
            for ic in range(NIC):
                gsl = ic % 2
                for tc in range(8):
                    psl = tc % 4
                    def mm(e, ic=ic, tc=tc, psl=psl):
                        for dc in range(8):
                            i = e.matmul(out=ps_sc[:, psl * 512:(psl + 1) * 512],
                                         lhsT=PB(WT + dc * 3072 + ic * 128, 128),
                                         rhs=PB(XQ + dc * 4096 + tc * 512, 512),
                                         start=(dc == 0), stop=(dc == 7))
                        tick(i)
                    op("tensor", mm, reads=(pr(WT, 24576), pr(XQ, 32768)),
                       writes=(PSC(psl * 512, 512),), tag=f"{tagp}m{ic}_{tc}")

                    def ev(e, ic=ic, tc=tc, psl=psl, gsl=gsl):
                        e.tensor_scalar(out=PB(GU + gsl * 4096 + tc * 512, 512),
                                        in0=ps_sc[:, psl * 512:(psl + 1) * 512],
                                        scalar1=1.0, scalar2=-1e30,
                                        op0=ALU.mult, op1=ALU.max,
                                        accum_out=sb["tiny"][:, 16 + tc:17 + tc])
                        tick(e.tensor_scalar(out=PB(YJ + (tc % 2) * 2048, 512),
                                             in0=ps_sc[:, psl * 512:(psl + 1) * 512],
                                             scalar1=1.0, scalar2=1e30,
                                             op0=ALU.mult, op1=ALU.min,
                                             accum_out=sb["tiny"][:, 24 + tc:25 + tc]))
                    op("vector", ev,
                       reads=(PSC(psl * 512, 512),),
                       writes=(PSC(psl * 512, 512), pr(GU + gsl * 4096 + tc * 512, 512),
                               pr(YJ + (tc % 2) * 2048, 512),
                               rr("tiny", 16 + tc, 17 + tc),
                               rr("tiny", 24 + tc, 25 + tc)),
                       tag=f"{tagp}e{ic}_{tc}")
                def fold(e, ic=ic, colbase=colbase):
                    e.tensor_reduce(out=sb["colx"][:, (0 if colbase == 0 else 32) + ic:(0 if colbase == 0 else 32) + ic + 1],
                                    in_=sb["tiny"][:, 16:24], axis=mybir.AxisListType.X, op=ALU.max)
                    e.drain()
                    tick(e.tensor_reduce(out=sb["colx"][:, (64 if colbase == 0 else 96) + ic:(64 if colbase == 0 else 96) + ic + 1],
                                         in_=sb["tiny"][:, 24:32], axis=mybir.AxisListType.X, op=ALU.min))
                op("vector", fold, reads=(rr("tiny", 16, 32),),
                   writes=(rr("colx", (0 if colbase == 0 else 32) + ic, (0 if colbase == 0 else 32) + ic + 1),
                           rr("colx", (64 if colbase == 0 else 96) + ic, (64 if colbase == 0 else 96) + ic + 1)),
                   tag=f"{tagp}f{ic}")
                op("gpsimd", is_dma_op=True, emit=lambda e, ic=ic, gsl=gsl, scname=scname: dma(
                    e.dma_start(out=ap[scname][ic], in_=PB(GU + gsl * 4096, 4096))),
                   reads=(pr(GU + gsl * 4096, 4096),),
                   writes=(rr(scname, ic, ic + 1),), tag=f"{tagp}s{ic}")

        wT_loads("wg0_b", "wgT")
        mm_ug("gate_sc", 0, "g")
        ground(1, NIC, "cci2", "cco2")
        wT_loads("wu0_b", "wuT")
        mm_ug("up_sc", 32, "u")
        def upmix(e):
            e.scalar_tensor_tensor(out=sb["colx"][:, 128:128 + NIC],
                                   in0=sb["colx"][:, 96:96 + NIC], scalar=-1.0,
                                   in1=sb["colx"][:, 32:32 + NIC], op0=ALU.mult, op1=ALU.max)
            e.drain()
            tick(e.tensor_reduce(out=sb["tcc2"][:, 2:3], in_=sb["colx"][:, 128:128 + NIC],
                                 axis=mybir.AxisListType.X, op=ALU.max))
        op("vector", upmix, reads=(rr("colx", 32, 32 + NIC), rr("colx", 96, 96 + NIC)),
           writes=(rr("colx", 128, 128 + NIC), rr("tcc2", 2, 3)), tag="gcup")
        op("gpsimd", lambda e: tick(e.tensor_reduce(
            out=sb["tcc"][0:1, 2:3], in_=sb["tcc2"][:, 2:3],
            axis=mybir.AxisListType.C, op=ALU.max)),
           reads=(rr("tcc2", 2, 3),), writes=(rr("tcc", 2, 3),), tag="gpup")
        op("gpsimd", is_dma_op=True,
           emit=lambda e: dma(e.dma_start(out=ap["cci3"], in_=sb["tcc"][0:1, :])),
           reads=(rr("tcc"),), writes=(rr("cci3"),), tag="gwup")
        op("gpsimd", lambda e: tick(e.collective_compute(
            "AllReduce", ALU.max, replica_groups=[list(range(8))],
            ins=[ap["cci3"].opt()], outs=[ap["cco3"].opt()])),
           reads=(rr("cci3"),), writes=(rr("cco3"),), tag="ccup", is_coll=True)
        op("gpsimd", is_dma_op=True,
           emit=lambda e: dma(e.dma_start(out=sb["tcc"][0:1, 10:11], in_=ap["cco3"][0:1, 2:3])),
           reads=(rr("cco3"),), writes=(rr("tcc", 10, 11),), tag="grup")
        op("tensor", lambda e: tick(e.matmul(
            out=ps_dm[:, 0:1], lhsT=sb["onesf"][0:1, :],
            rhs=sb["tcc"][0:1, 10:11], start=True, stop=True)),
           reads=(rr("tcc", 10, 11), rr("c_onesf")), writes=(PDM(),), tag="gbup")
        op("vector", lambda e: tick(e.tensor_copy(
            out=sb["glob"][:, 2:3], in_=ps_dm[:, 0:1])),
           reads=(PDM(),), writes=(PDM(), rr("glob", 2, 3)), tag="geup")

        def mkscale12(e):
            e.reciprocal(out=sb["cols"][:, 2:3], in_=sb["glob"][:, 1:2])
            e.reciprocal(out=sb["cols"][:, 4:5], in_=sb["glob"][:, 2:3])
            e.drain()
            e.tensor_scalar(out=sb["cols"][:, 2:3], in0=sb["cols"][:, 2:3],
                            scalar1=32767.0, scalar2=None, op0=ALU.mult)
            e.tensor_scalar(out=sb["cols"][:, 3:4], in0=sb["glob"][:, 1:2],
                            scalar1=1.0 / 32767.0, scalar2=None, op0=ALU.mult)
            e.tensor_scalar(out=sb["cols"][:, 4:5], in0=sb["cols"][:, 4:5],
                            scalar1=32767.0, scalar2=None, op0=ALU.mult)
            e.tensor_scalar(out=sb["cols"][:, 5:6], in0=sb["glob"][:, 2:3],
                            scalar1=1.0 / 32767.0, scalar2=None, op0=ALU.mult)
            e.drain()
            tick(e.tensor_scalar(out=sb["cols"][:, 44:45], in0=sb["cols"][:, 3:4],
                                 scalar1=1.0 / 65536.0, scalar2=None, op0=ALU.mult))
        op("vector", mkscale12, reads=(rr("glob", 1, 3),),
           writes=(rr("cols", 2, 6), rr("cols", 44, 45)), tag="f2k12")

        P2Q = lambda r: XQ + r * 8192
        for icp in range(NIC // 2):
            for ic in (2 * icp, 2 * icp + 1):
                rho = ic % 2
                op("sync", is_dma_op=True, emit=lambda e, ic=ic, rho=rho: dma(e.dma_start(
                    out=PB(GU + rho * 4096, 4096), in_=ap["gate_sc"][ic])),
                   reads=(rr("gate_sc", ic, ic + 1),),
                   writes=(pr(GU + rho * 4096, 4096),), tag=f"tld{ic}")

            def p2s(ic, k):
                rho = ic % 2
                q, qr = PF(P2Q(rho), 4096), prf(P2Q(rho), 4096)
                sg, sgr = PF(P2Q(2 + rho), 4096), prf(P2Q(2 + rho), 4096)
                tv, tvr = PF(WT + rho * 8192, 4096), prf(WT + rho * 8192, 4096)
                ob, obr = PB(WL + rho * 4096, 4096), pr(WL + rho * 4096, 4096)
                if k == 0:
                    op("scalar", lambda e, rho=rho, q=q: tick(e.activation(
                        out=q, in_=PB(GU + rho * 4096, 4096), func=AF.Copy,
                        scale=sb["cols"][:, 2:3], bias=MAGIC)),
                       reads=(pr(GU + rho * 4096, 4096), rr("cols", 2, 3)),
                       writes=(qr,), tag=f"tqa{ic}")
                elif k == 1:
                    op("scalar", lambda e, q=q: tick(e.activation(
                        out=q, in_=q, func=AF.Copy, bias=-MAGIC)),
                       reads=(qr,), writes=(qr,), tag=f"tqa2{ic}")
                elif k == 2:
                    op("vector", lambda e, q=q: tick(e.tensor_scalar(
                        out=q, in0=q, scalar1=32767.0, scalar2=-32768.0,
                        op0=ALU.min, op1=ALU.max)),
                       reads=(qr,), writes=(qr,), tag=f"tqb{ic}")
                elif k == 3:
                    op("scalar", lambda e, q=q, sg=sg: tick(e.activation(
                        out=sg, in_=q, func=AF.Sigmoid, scale=sb["cols"][:, 3:4])),
                       reads=(qr, rr("cols", 3, 4)), writes=(sgr,), tag=f"tsg{ic}")
                elif k == 4:
                    op("scalar", lambda e, sg=sg: tick(e.activation(
                        out=sg, in_=sg, func=AF.Copy, scale=65536.0,
                        bias=MAGIC)),
                       reads=(sgr,), writes=(sgr,), tag=f"tfa{ic}")
                elif k == 5:
                    op("scalar", lambda e, sg=sg: tick(e.activation(
                        out=sg, in_=sg, func=AF.Copy, bias=-MAGIC)),
                       reads=(sgr,), writes=(sgr,), tag=f"tfa2{ic}")
                elif k == 6:
                    op("vector", lambda e, sg=sg: tick(e.tensor_scalar(
                        out=sg, in0=sg, scalar1=65535.0, scalar2=0.0,
                        op0=ALU.min, op1=ALU.max)),
                       reads=(sgr,), writes=(sgr,), tag=f"tfb{ic}")
                elif k == 9:
                    op("vector", lambda e, q=q, sg=sg, tv=tv: tick(e.tensor_tensor(
                        out=tv, in0=sg, in1=q, op=ALU.mult)),
                       reads=(sgr, qr), writes=(tvr,), tag=f"tfc{ic}")
                elif k == 7:
                    op("vector", lambda e, ic=ic, tv=tv, ob=ob: tick(e.tensor_scalar(
                        out=ob, in0=tv, scalar1=sb["cols"][:, 44:45], scalar2=-1e30,
                        op0=ALU.mult, op1=ALU.max,
                        accum_out=sb["colx"][:, ic:ic + 1])),
                       reads=(tvr, rr("cols", 44, 45)),
                       writes=(obr, rr("colx", ic, ic + 1)), tag=f"tfd{ic}")
                else:
                    op("vector", lambda e, ic=ic, tv=tv, q=q: tick(e.tensor_scalar(
                        out=q, in0=tv, scalar1=sb["cols"][:, 44:45], scalar2=1e30,
                        op0=ALU.mult, op1=ALU.min,
                        accum_out=sb["colx"][:, 64 + ic:64 + ic + 1])),
                       reads=(tvr, rr("cols", 44, 45)),
                       writes=(qr, rr("colx", 64 + ic, 64 + ic + 1)), tag=f"tfe{ic}")

            for k in (0, 1, 2, 3, 4, 5, 6, 9, 7, 8):
                p2s(2 * icp, k)
                p2s(2 * icp + 1, k)
            for ic in (2 * icp, 2 * icp + 1):
                rho = ic % 2
                op("gpsimd", is_dma_op=True, emit=lambda e, ic=ic, rho=rho: dma(e.dma_start(
                    out=ap["gate_sc"][ic], in_=PB(WL + rho * 4096, 4096))),
                   reads=(pr(WL + rho * 4096, 4096),),
                   writes=(rr("gate_sc", ic, ic + 1),), tag=f"tst{ic}")
        ground(3, NIC, "cci4", "cco4")
        def mkscale3(e):
            e.reciprocal(out=sb["cols"][:, 6:7], in_=sb["glob"][:, 3:4])
            e.drain()
            e.tensor_scalar(out=sb["cols"][:, 6:7], in0=sb["cols"][:, 6:7],
                            scalar1=32767.0, scalar2=None, op0=ALU.mult)
            tick(e.tensor_scalar(out=sb["cols"][:, 7:8], in0=sb["glob"][:, 3:4],
                                 scalar1=1.0 / 32767.0, scalar2=None, op0=ALU.mult))
        op("vector", mkscale3, reads=(rr("glob", 3, 4),), writes=(rr("cols", 6, 8),), tag="f2k3")

        for icp in range(NIC // 2):
            for ic in (2 * icp, 2 * icp + 1):
                rho = ic % 2
                op("sync", is_dma_op=True, emit=lambda e, ic=ic, rho=rho: dma(e.dma_start(
                    out=PB(GU + rho * 4096, 4096), in_=ap["gate_sc"][ic])),
                   reads=(rr("gate_sc", ic, ic + 1),),
                   writes=(pr(GU + rho * 4096, 4096),), tag=f"old{ic}")
                op("scalar", is_dma_op=True, emit=lambda e, ic=ic, rho=rho: dma(e.dma_start(
                    out=PB(EL + rho * 4096, 4096), in_=ap["up_sc"][ic])),
                   reads=(rr("up_sc", ic, ic + 1),),
                   writes=(pr(EL + rho * 4096, 4096),), tag=f"old2{ic}")

            def p3s(ic, k):
                rho = ic % 2
                g2, g2r = PF(P2Q(rho), 4096), prf(P2Q(rho), 4096)
                uq, uqr = PF(P2Q(2 + rho), 4096), prf(P2Q(2 + rho), 4096)
                tv, tvr = PF(WT + rho * 8192, 4096), prf(WT + rho * 8192, 4096)
                ob, obr = PB(WL + rho * 4096, 4096), pr(WL + rho * 4096, 4096)
                if k == 0:
                    op("scalar", lambda e, rho=rho, g2=g2: tick(e.activation(
                        out=g2, in_=PB(GU + rho * 4096, 4096), func=AF.Copy,
                        scale=sb["cols"][:, 6:7], bias=MAGIC)),
                       reads=(pr(GU + rho * 4096, 4096), rr("cols", 6, 7)),
                       writes=(g2r,), tag=f"opa{ic}")
                elif k == 1:
                    op("scalar", lambda e, g2=g2: tick(e.activation(
                        out=g2, in_=g2, func=AF.Copy, bias=-MAGIC)),
                       reads=(g2r,), writes=(g2r,), tag=f"opa2{ic}")
                elif k == 2:
                    op("vector", lambda e, g2=g2: tick(e.tensor_scalar(
                        out=g2, in0=g2, scalar1=32767.0, scalar2=-32768.0,
                        op0=ALU.min, op1=ALU.max)),
                       reads=(g2r,), writes=(g2r,), tag=f"opb{ic}")
                elif k == 3:
                    op("scalar", lambda e, rho=rho, uq=uq: tick(e.activation(
                        out=uq, in_=PB(EL + rho * 4096, 4096), func=AF.Copy,
                        scale=sb["cols"][:, 4:5], bias=MAGIC)),
                       reads=(pr(EL + rho * 4096, 4096), rr("cols", 4, 5)),
                       writes=(uqr,), tag=f"opd{ic}")
                elif k == 4:
                    op("scalar", lambda e, uq=uq: tick(e.activation(
                        out=uq, in_=uq, func=AF.Copy, bias=-MAGIC)),
                       reads=(uqr,), writes=(uqr,), tag=f"opd2{ic}")
                elif k == 5:
                    op("vector", lambda e, uq=uq: tick(e.tensor_scalar(
                        out=uq, in0=uq, scalar1=32767.0, scalar2=-32768.0,
                        op0=ALU.min, op1=ALU.max)),
                       reads=(uqr,), writes=(uqr,), tag=f"ope{ic}")
                elif k == 6:
                    op("vector", lambda e, g2=g2, uq=uq, tv=tv: tick(e.scalar_tensor_tensor(
                        out=tv, in0=g2, scalar=sb["cols"][:, 7:8], in1=uq,
                        op0=ALU.mult, op1=ALU.mult)),
                       reads=(g2r, uqr, rr("cols", 7, 8)), writes=(tvr,), tag=f"opg{ic}")
                elif k == 7:
                    op("vector", lambda e, ic=ic, tv=tv, ob=ob: tick(e.tensor_scalar(
                        out=ob, in0=tv, scalar1=sb["cols"][:, 5:6], scalar2=-1e30,
                        op0=ALU.mult, op1=ALU.max,
                        accum_out=sb["colx"][:, 32 + ic:32 + ic + 1])),
                       reads=(tvr, rr("cols", 5, 6)),
                       writes=(obr, rr("colx", 32 + ic, 32 + ic + 1)), tag=f"oph{ic}")
                else:
                    op("vector", lambda e, ic=ic, tv=tv, g2=g2: tick(e.tensor_scalar(
                        out=g2, in0=tv, scalar1=sb["cols"][:, 5:6], scalar2=1e30,
                        op0=ALU.mult, op1=ALU.min,
                        accum_out=sb["colx"][:, 96 + ic:96 + ic + 1])),
                       reads=(tvr, rr("cols", 5, 6)),
                       writes=(g2r, rr("colx", 96 + ic, 96 + ic + 1)), tag=f"opi{ic}")

            for k in range(9):
                p3s(2 * icp, k)
                p3s(2 * icp + 1, k)
            for ic in (2 * icp, 2 * icp + 1):
                rho = ic % 2
                op("gpsimd", is_dma_op=True, emit=lambda e, ic=ic, rho=rho: dma(e.dma_start(
                    out=ap["up_sc"][ic], in_=PB(WL + rho * 4096, 4096))),
                   reads=(pr(WL + rho * 4096, 4096),),
                   writes=(rr("up_sc", ic, ic + 1),), tag=f"ost{ic}")
        def omix(e):
            e.scalar_tensor_tensor(out=sb["colx"][:, 128:128 + NIC],
                                   in0=sb["colx"][:, 96:96 + NIC], scalar=-1.0,
                                   in1=sb["colx"][:, 32:32 + NIC], op0=ALU.mult, op1=ALU.max)
            e.drain()
            tick(e.tensor_reduce(out=sb["tcc2"][:, 4:5], in_=sb["colx"][:, 128:128 + NIC],
                                 axis=mybir.AxisListType.X, op=ALU.max))
        op("vector", omix, reads=(rr("colx", 32, 32 + NIC), rr("colx", 96, 96 + NIC)),
           writes=(rr("colx", 128, 128 + NIC), rr("tcc2", 4, 5)), tag="gco")
        op("gpsimd", lambda e: tick(e.tensor_reduce(
            out=sb["tcc"][0:1, 4:5], in_=sb["tcc2"][:, 4:5],
            axis=mybir.AxisListType.C, op=ALU.max)),
           reads=(rr("tcc2", 4, 5),), writes=(rr("tcc", 4, 5),), tag="gpo")
        op("gpsimd", is_dma_op=True,
           emit=lambda e: dma(e.dma_start(out=ap["cci5"], in_=sb["tcc"][0:1, :])),
           reads=(rr("tcc"),), writes=(rr("cci5"),), tag="gwo")
        op("gpsimd", lambda e: tick(e.collective_compute(
            "AllReduce", ALU.max, replica_groups=[list(range(8))],
            ins=[ap["cci5"].opt()], outs=[ap["cco5"].opt()])),
           reads=(rr("cci5"),), writes=(rr("cco5"),), tag="cco", is_coll=True)
        op("gpsimd", is_dma_op=True,
           emit=lambda e: dma(e.dma_start(out=sb["tcc"][0:1, 12:13], in_=ap["cco5"][0:1, 4:5])),
           reads=(rr("cco5"),), writes=(rr("tcc", 12, 13),), tag="gro")
        op("tensor", lambda e: tick(e.matmul(
            out=ps_dm[:, 0:1], lhsT=sb["onesf"][0:1, :],
            rhs=sb["tcc"][0:1, 12:13], start=True, stop=True)),
           reads=(rr("tcc", 12, 13), rr("c_onesf")), writes=(PDM(),), tag="gbo")
        op("vector", lambda e: (
            e.tensor_copy(out=sb["glob"][:, 4:5], in_=ps_dm[:, 0:1]),
            e.drain(),
            e.reciprocal(out=sb["cols"][:, 12:13], in_=sb["glob"][:, 4:5]),
            e.drain(),
            e.tensor_scalar(out=sb["cols"][:, 12:13], in0=sb["cols"][:, 12:13],
                            scalar1=32767.0, scalar2=None, op0=ALU.mult),
            tick(e.tensor_scalar(out=sb["cols"][:, 13:14], in0=sb["glob"][:, 4:5],
                                 scalar1=1.0 / 32767.0, scalar2=None, op0=ALU.mult))),
           reads=(PDM(),), writes=(PDM(), rr("glob", 4, 5), rr("cols", 12, 14)), tag="geo")

        for ic in range(NIC):
            op("sync", is_dma_op=True, emit=lambda e, ic=ic: dma(
                e.dma_start_transpose(out=PB(WT + ic * 1024, 1024),
                                      in_=ap["wd0_b"][:, ic * 128:(ic + 1) * 128])),
               reads=(rr("wd0_b"),), writes=(pr(WT + ic * 1024, 1024),), tag=f"wdT{ic}")
        for tg in range(8):
            for ic in range(NIC):
                osl = ic % 4
                op("sync", is_dma_op=True, emit=lambda e, tg=tg, ic=ic: dma(e.dma_start(
                    out=PB(EL + (ic % 2) * 1024, 512),
                    in_=ap["up_sc"][ic, :, tg * 512:(tg + 1) * 512])),
                   reads=(rr("up_sc", ic, ic + 1),),
                   writes=(pr(EL + (ic % 2) * 1024, 512),), tag=f"dol{tg}_{ic}")

                def oq(e, tg=tg, ic=ic, osl=osl):
                    ov = PF(EL2, 512)
                    e.tensor_scalar(out=ov, in0=PB(EL + (ic % 2) * 1024, 512),
                                    scalar1=sb["cols"][:, 12:13], scalar2=MAGIC,
                                    op0=ALU.mult, op1=ALU.add)
                    e.drain()
                    e.tensor_scalar(out=ov, in0=ov, scalar1=MAGIC, scalar2=32767.0,
                                    op0=ALU.subtract, op1=ALU.min)
                    e.drain()
                    tick(e.tensor_scalar(out=PB(OI + osl * 512, 512), in0=ov,
                                         scalar1=-32768.0, scalar2=sb["cols"][:, 13:14],
                                         op0=ALU.max, op1=ALU.mult))
                op("vector", oq, reads=(pr(EL + (ic % 2) * 1024, 512), rr("cols", 12, 14)),
                   writes=(prf(EL2, 512), pr(OI + osl * 512, 512)), tag=f"doq{tg}_{ic}")

                def dmm(e, tg=tg, ic=ic, osl=osl):
                    for hc in range(8):
                        if hc < 4:
                            out = ps_sc[:, hc * 512:(hc + 1) * 512]
                        elif hc == 4:
                            out = ps_cx[:, 0:512]
                        elif hc == 5:
                            out = ps_dm[:, 0:512]
                        else:
                            out = ps_op[:, (hc - 6) * 512:(hc - 5) * 512]
                        i = e.matmul(out=out,
                                     lhsT=PB(WT + ic * 1024 + hc * 128, 128),
                                     rhs=PB(OI + osl * 512, 512),
                                     start=(ic == 0), stop=(ic == NIC - 1))
                    tick(i)
                op("tensor", dmm, reads=(pr(WT, 24576), pr(OI + osl * 512, 512)),
                   writes=(PSC(0, 2048), PCX(), PDM(), POP(0, 1024)), tag=f"dmm{tg}_{ic}")
            def dev(e, tg=tg):
                for hc in range(8):
                    if hc < 4:
                        src = ps_sc[:, hc * 512:(hc + 1) * 512]
                    elif hc == 4:
                        src = ps_cx[:, 0:512]
                    elif hc == 5:
                        src = ps_dm[:, 0:512]
                    else:
                        src = ps_op[:, (hc - 6) * 512:(hc - 5) * 512]
                    i = e.tensor_copy(out=PF(DS8 + hc * 1024, 512), in_=src)
                tick(i)
            op("vector", dev,
               reads=(PSC(0, 2048), PCX(), PDM(), POP(0, 1024)),
               writes=(PSC(0, 2048), PCX(), PDM(), POP(0, 1024), prf(DS8, 4096)),
               tag=f"dev{tg}")

            def dtr2(e, tg=tg):
                for hc in range(8):
                    for tc2 in range(4):
                        if tc2 < 2:
                            out = ps_sc[:, tc2 * 1024 + hc * 128: tc2 * 1024 + (hc + 1) * 128]
                        elif tc2 == 2:
                            out = (ps_cx if hc < 4 else ps_dm)[:, (hc % 4) * 128:(hc % 4 + 1) * 128]
                        else:
                            out = ps_op[:, hc * 128:(hc + 1) * 128]
                        i = e.transpose(out=out,
                                        in_=PF(DS8 + hc * 1024 + tc2 * 256, 128),
                                        identity=sb["idf"][:, :])
                tick(i)
            op("tensor", dtr2, reads=(prf(DS8, 4096), rr("c_idf")),
               writes=(PSC(0, 2048), PCX(), PDM(), POP(0, 1024)), tag=f"dtr{tg}")
            for tc2 in range(4):
                t = tg * 4 + tc2
                op("sync", is_dma_op=True, emit=lambda e, t=t, tc2=tc2: dma(e.dma_start(
                    out=PF(XF2 + (tc2 % 2) * 2048, 1024),
                    in_=ap["xres"][t * 128:(t + 1) * 128, :])),
                   reads=(rr("xres", t * 128, (t + 1) * 128),),
                   writes=(prf(XF2 + (tc2 % 2) * 2048, 1024),), tag=f"dx{t}")

                def dres(e, tc2=tc2):
                    xv = PF(XF2 + (tc2 % 2) * 2048, 1024)
                    if tc2 < 2:
                        i = e.scalar_tensor_tensor(out=xv, in0=ps_sc[:, tc2 * 1024:(tc2 + 1) * 1024],
                                                   scalar=1.0, in1=xv, op0=ALU.mult, op1=ALU.add)
                    elif tc2 == 2:
                        e.scalar_tensor_tensor(out=PF(XF2 + (tc2 % 2) * 2048, 512), in0=ps_cx[:, 0:512],
                                               scalar=1.0, in1=PF(XF2 + (tc2 % 2) * 2048, 512),
                                               op0=ALU.mult, op1=ALU.add)
                        e.drain()
                        i = e.scalar_tensor_tensor(out=PF(XF2 + (tc2 % 2) * 2048 + 1024, 512),
                                                   in0=ps_dm[:, 0:512], scalar=1.0,
                                                   in1=PF(XF2 + (tc2 % 2) * 2048 + 1024, 512),
                                                   op0=ALU.mult, op1=ALU.add)
                    else:
                        i = e.scalar_tensor_tensor(out=xv, in0=ps_op[:, 0:1024],
                                                   scalar=1.0, in1=xv, op0=ALU.mult, op1=ALU.add)
                    tick(i)
                rdsets = {0: (PSC(0, 1024),), 1: (PSC(1024, 1024),),
                          2: (PCX(), PDM()), 3: (POP(0, 1024),)}[tc2]
                op("vector", dres,
                   reads=rdsets + (prf(XF2 + (tc2 % 2) * 2048, 1024),),
                   writes=rdsets + (prf(XF2 + (tc2 % 2) * 2048, 1024),), tag=f"dr{t}")
                op("gpsimd", is_dma_op=True, emit=lambda e, t=t, tc2=tc2: dma(e.dma_start(
                    out=ap["xres"][t * 128:(t + 1) * 128, :],
                    in_=PF(XF2 + (tc2 % 2) * 2048, 1024))),
                   reads=(prf(XF2 + (tc2 % 2) * 2048, 1024),),
                   writes=(rr("xres", t * 128, (t + 1) * 128),), tag=f"dw{t}")

    if n_phases >= 2:
        l0_ffn()

    XF3 = 0
    LNT = 4096
    LNF = 20480
    W1T = 36864
    HF = 45056
    YB = 61440
    CSB = 65536
    SR = 69632
    ER = 73728
    WC = 77824

    def l1_attn():
        op("sync", is_dma_op=True, emit=lambda e: dma(e.dma_start(
            out=sb["cols"][:, 16:24],
            in_=ap["b1"][0:1, :].rearrange("o (c p) -> (o p) c", p=128),
            allow_slow_non_contiguous=True)),
           reads=(rr("b1"),), writes=(rr("cols", 16, 24),), tag="b1l")
        op("sync", is_dma_op=True, emit=lambda e: dma(e.dma_start(
            out=sb["cols"][:, 24:32],
            in_=ap["w2"][0:1, :].rearrange("o (c p) -> (o p) c", p=128),
            allow_slow_non_contiguous=True)),
           reads=(rr("w2"),), writes=(rr("cols", 24, 32),), tag="w2l")
        op("vector", lambda e: tick(e.tensor_copy(out=PB(WC, 8), in_=sb["cols"][:, 24:32])),
           reads=(rr("cols", 24, 32),), writes=(pr(WC, 8),), tag="w2c")
        op("sync", is_dma_op=True, emit=lambda e: dma(e.dma_start(
            out=sb["b2sb"][0:1, 0:1], in_=ap["b2"])),
           reads=(rr("b2"),), writes=(rr("b2sb", 0, 1),), tag="b2l")
        for dc in range(8):
            op("sync", is_dma_op=True, emit=lambda e, dc=dc: dma(
                e.dma_start_transpose(out=PB(W1T + dc * 1024, 1024),
                                      in_=ap["w1_b"][:, dc * 128:(dc + 1) * 128])),
               reads=(rr("w1_b"),), writes=(pr(W1T + dc * 1024, 1024),), tag=f"w1T{dc}")

        for b in range(B_PC):
            for t in range(NTB):
                ti = b * NTB + t
                sl2 = (t % 2) * 2048
                op("sync", is_dma_op=True, emit=lambda e, b=b, t=t, sl2=sl2: dma(e.dma_start(
                    out=PF(XF3 + sl2, 1024),
                    in_=ap["xres"][(b * T + t * 128):(b * T + (t + 1) * 128), :])),
                   reads=(rr("xres", b * T + t * 128, b * T + (t + 1) * 128),),
                   writes=(prf(XF3 + sl2, 1024),), tag=f"a3x{ti}")

                def stats(e, t=t, ti=ti, sl2=sl2):
                    e.bn_stats(out=sb["tiny"][:, 0:6], in_=PF(XF3 + sl2, 512))
                    e.bn_stats(out=sb["tiny"][:, 6:12], in_=PF(XF3 + sl2 + 1024, 512))
                    e.drain()
                    e.bn_aggr(out=sb["tiny"][:, 12:14], in_=sb["tiny"][:, 0:12])
                    e.drain()
                    e.tensor_copy(out=sb["colm"][:, ti:ti + 1], in_=sb["tiny"][:, 12:13])
                    tick(e.tensor_scalar(out=sb["colx"][:, 160 + t:161 + t],
                                         in0=sb["tiny"][:, 13:14],
                                         scalar1=EPS, scalar2=None, op0=ALU.add))
                op("vector", stats, reads=(prf(XF3 + sl2, 1024),),
                   writes=(rr("tiny", 0, 15), rr("colm", ti, ti + 1),
                           rr("colx", 160 + t, 161 + t)), tag=f"a3s{ti}")
            op("scalar", lambda e, b=b: (
                e.activation(out=sb["colx"][:, 160:160 + NTB],
                             in_=sb["colx"][:, 160:160 + NTB], func=AF.Ln),
                e.drain(),
                tick(e.activation(out=sb["colr"][:, b * NTB:(b + 1) * NTB],
                                  in_=sb["colx"][:, 160:160 + NTB],
                                  func=AF.Exp, scale=-0.5))),
               reads=(rr("colx", 160, 160 + NTB),),
               writes=(rr("colx", 160, 160 + NTB),
                       rr("colr", b * NTB, (b + 1) * NTB)), tag=f"a3rb{b}")
            for t in range(NTB):
                ti = b * NTB + t
                sl2 = (t % 2) * 2048
                op("sync", is_dma_op=True, emit=lambda e, b=b, t=t, sl2=sl2: dma(e.dma_start(
                    out=PF(XF3 + sl2, 1024),
                    in_=ap["xres"][(b * T + t * 128):(b * T + (t + 1) * 128), :])),
                   reads=(rr("xres", b * T + t * 128, b * T + (t + 1) * 128),),
                   writes=(prf(XF3 + sl2, 1024),), tag=f"a3xb{ti}")
                op("vector", lambda e, t=t, ti=ti, sl2=sl2: tick(e.tensor_scalar(
                    out=PB(LNT + t * 1024, 1024), in0=PF(XF3 + sl2, 1024),
                    scalar1=sb["colm"][:, ti:ti + 1], scalar2=sb["colr"][:, ti:ti + 1],
                    op0=ALU.subtract, op1=ALU.mult)),
                   reads=(prf(XF3 + sl2, 1024), rr("colm", ti, ti + 1), rr("colr", ti, ti + 1)),
                   writes=(pr(LNT + t * 1024, 1024),), tag=f"a3l{ti}")
                def lnT(e, t=t):
                    for dc in range(8):
                        i = e.transpose(out=pt_op[:, (t % 2) * 1024 + dc * 128:(t % 2) * 1024 + (dc + 1) * 128],
                                        in_=PB(LNT + t * 1024 + dc * 128, 128),
                                        identity=sb["id"][:, :])
                    tick(i)
                op("tensor", lnT, reads=(pr(LNT + t * 1024, 1024), rr("c_id")),
                   writes=(POP((t % 2) * 512, 512),), tag=f"a3T{ti}")
                op("vector", lambda e, t=t: tick(e.tensor_copy(
                    out=PB(LNF, 16384).rearrange("p (dc n) -> p dc n", n=2048)[:, :, t * 128:(t + 1) * 128],
                    in_=pt_op[:, (t % 2) * 1024:(t % 2) * 1024 + 1024].rearrange("p (dc n) -> p dc n", n=128))),
                   reads=(POP((t % 2) * 512, 512),),
                   writes=(POP((t % 2) * 512, 512),) + tuple(
                       pr(LNF + dc * 2048 + t * 128, 128) for dc in range(8)),
                   tag=f"a3e{ti}")

            for ec in range(8):
                for tc in range(4):
                    psl = (ec * 4 + tc) % 4
                    def hmm(e, ec=ec, tc=tc, psl=psl):
                        for dc in range(8):
                            i = e.matmul(out=ps_sc[:, psl * 512:(psl + 1) * 512],
                                         lhsT=PB(W1T + dc * 1024 + ec * 128, 128),
                                         rhs=PB(LNF + dc * 2048 + tc * 512, 512),
                                         start=(dc == 0), stop=(dc == 7))
                        tick(i)
                    op("tensor", hmm, reads=(pr(W1T, 8192), pr(LNF, 16384)),
                       writes=(PSC(psl * 512, 512),), tag=f"hm{b}_{ec}_{tc}")
                    op("scalar", lambda e, ec=ec, tc=tc, psl=psl: tick(e.activation(
                        out=PB(HF + ec * 2048 + tc * 512, 512),
                        in_=ps_sc[:, psl * 512:(psl + 1) * 512], func=AF.Tanh,
                        bias=sb["cols"][:, 16 + ec:17 + ec])),
                       reads=(PSC(psl * 512, 512), rr("cols", 16, 24)),
                       writes=(PSC(psl * 512, 512), pr(HF + ec * 2048 + tc * 512, 512)),
                       tag=f"he{b}_{ec}_{tc}")

            for tc in range(4):
                def smm(e, tc=tc):
                    for ec in range(8):
                        i = e.matmul(out=ps_dm[0:1, 0:512],
                                     lhsT=g.pool[:, WC + ec:WC + ec + 1],
                                     rhs=PB(HF + ec * 2048 + tc * 512, 512),
                                     start=(ec == 0), stop=(ec == 7))
                    tick(i)
                op("tensor", smm, reads=(pr(WC, 8), pr(HF, 16384)),
                   writes=(PDM(),), tag=f"sm{b}_{tc}")
                op("vector", lambda e, tc=tc: tick(e.tensor_scalar(
                    out=PF(SR + tc * 1024, 512)[0:1, :], in0=ps_dm[0:1, 0:512],
                    scalar1=sb["b2sb"][0:1, 0:1], scalar2=None, op0=ALU.add)),
                   reads=(PDM(), rr("b2sb", 0, 1)),
                   writes=(PDM(), prf(SR + tc * 1024, 512)), tag=f"se{b}_{tc}")
            def mrow(e):
                tick(e.tensor_reduce(out=sb["b2sb"][0:1, 1:2], in_=PF(SR, 2048)[0:1, :],
                                     axis=mybir.AxisListType.X, op=ALU.max, negate=True))
            op("vector", mrow, reads=(prf(SR, 2048),),
               writes=(rr("b2sb", 1, 2), rr("tcc", 15, 16)), tag=f"mr{b}")
            op("scalar", lambda e: tick(e.activation(
                out=PF(ER, 2048)[0:1, :], in_=PF(SR, 2048)[0:1, :], func=AF.Exp,
                bias=sb["b2sb"][0:1, 1:2])),
               reads=(prf(SR, 2048), rr("b2sb", 1, 2)),
               writes=(prf(ER, 2048),), tag=f"ex3{b}")
            def ecT(e):
                for t in range(NTB):
                    i = e.transpose(out=ps_cx[:, t:t + 1], in_=PF(ER + t * 256, 128)[0:1, :],
                                    identity=sb["idf"][0:1, 0:1])
                tick(i)
            op("tensor", ecT, reads=(prf(ER, 2048), rr("c_idf")), writes=(PCX(),),
               tag=f"ecT{b}")
            op("vector", lambda e: tick(e.tensor_copy(out=sb["glob"][:, 0:16],
                                                      in_=ps_cx[:, 0:16])),
               reads=(PCX(),), writes=(PCX(), rr("glob", 0, 16)), tag=f"ecE{b}")

            for m in range(NTB):
                ysl = (m % 2) * 2048
                def ymk(e, m=m, ysl=ysl):
                    e.tensor_scalar(out=PB(YB + ysl, 1024), in0=PB(LNT + m * 1024, 1024),
                                    scalar1=sb["glob"][:, m:m + 1], scalar2=None,
                                    op0=ALU.mult)
                    e.drain()
                    tick(e.tensor_copy(out=PB(YB + ysl + 1024, 1), in_=sb["glob"][:, m:m + 1]))
                op("vector", ymk, reads=(pr(LNT + m * 1024, 1024), rr("glob", 0, 16)),
                   writes=(pr(YB + ysl, 1025),), tag=f"y{b}_{m}")
                def csm(e, m=m, ysl=ysl):
                    e.matmul(out=ps_sc[0:16, 0:512], lhsT=sb["ind"][:, m * 16:(m + 1) * 16],
                             rhs=PB(YB + ysl, 512), start=(m == 0), stop=(m == NTB - 1))
                    e.matmul(out=ps_sc[0:16, 512:1024], lhsT=sb["ind"][:, m * 16:(m + 1) * 16],
                             rhs=PB(YB + ysl + 512, 512), start=(m == 0), stop=(m == NTB - 1))
                    tick(e.matmul(out=ps_sc[0:16, 1024:1025], lhsT=sb["ind"][:, m * 16:(m + 1) * 16],
                                  rhs=PB(YB + ysl + 1024, 1), start=(m == 0), stop=(m == NTB - 1)))
                op("tensor", csm, reads=(pr(YB + ysl, 1025), rr("c_ind")),
                   writes=(PSC(0, 1536),), tag=f"cs{b}_{m}")
            op("vector", lambda e: tick(e.tensor_copy(
                out=g.pool[0:16, CSB:CSB + 1024], in_=ps_sc[0:16, 0:1024])),
               reads=(PSC(0, 1536),), writes=(PSC(0, 1536), pr(CSB, 1024)), tag=f"cse{b}")
            op("vector", lambda e: tick(e.tensor_copy(
                out=g.pool[0:16, CSB + 1024:CSB + 1025], in_=ps_sc[0:16, 1024:1025])),
               reads=(PSC(0, 1536),), writes=(pr(CSB + 1024, 1),), tag=f"cs2{b}")
            def spre(e):
                e.matmul(out=ps_cx[0:16, 0:512], lhsT=sb["t16"][:, :],
                         rhs=g.pool[0:16, CSB:CSB + 512], start=True, stop=True)
                e.matmul(out=ps_dm[0:16, 0:512], lhsT=sb["t16"][:, :],
                         rhs=g.pool[0:16, CSB + 512:CSB + 1024], start=True, stop=True)
                tick(e.matmul(out=ps_op[0:16, 0:1], lhsT=sb["t16"][:, :],
                              rhs=g.pool[0:16, CSB + 1024:CSB + 1025], start=True, stop=True))
            op("tensor", spre, reads=(pr(CSB, 1025), rr("c_t16")),
               writes=(PCX(), PDM(), POP(0, 1)), tag=f"sp{b}")
            def sev(e):
                e.tensor_copy(out=g.pool[0:16, CSB + 2048:CSB + 2048 + 512], in_=ps_cx[0:16, 0:512])
                e.tensor_copy(out=g.pool[0:16, CSB + 2560:CSB + 2560 + 512], in_=ps_dm[0:16, 0:512])
                tick(e.tensor_copy(out=g.pool[0:16, CSB + 3072:CSB + 3072 + 1], in_=ps_op[0:16, 0:1]))
            op("vector", sev, reads=(PCX(), PDM(), POP(0, 1)),
               writes=(PCX(), PDM(), POP(0, 1), pr(CSB + 2048, 1025)), tag=f"sev{b}")

            for m in range(NTB):
                ysl = (m % 2) * 2048
                def ymk2(e, m=m, ysl=ysl):
                    e.tensor_scalar(out=PB(YB + ysl, 1024), in0=PB(LNT + m * 1024, 1024),
                                    scalar1=sb["glob"][:, m:m + 1], scalar2=None,
                                    op0=ALU.mult)
                    e.drain()
                    tick(e.tensor_copy(out=PB(YB + ysl + 1024, 1), in_=sb["glob"][:, m:m + 1]))
                op("vector", ymk2, reads=(pr(LNT + m * 1024, 1024), rr("glob", 0, 16)),
                   writes=(pr(YB + ysl, 1025),), tag=f"y2{b}_{m}")
                def numm(e, m=m, ysl=ysl):
                    e.matmul(out=ps_sc[:, 0:512], lhsT=sb["ut"][:, :],
                             rhs=PB(YB + ysl, 512), start=True, stop=False)
                    e.matmul(out=ps_sc[:, 512:1024], lhsT=sb["ut"][:, :],
                             rhs=PB(YB + ysl + 512, 512), start=True, stop=False)
                    e.matmul(out=ps_sc[:, 1024:1025], lhsT=sb["ut"][:, :],
                             rhs=PB(YB + ysl + 1024, 1), start=True, stop=False)
                    e.matmul(out=ps_sc[:, 0:512], lhsT=sb["selm"][:, m * 128:(m + 1) * 128],
                             rhs=g.pool[0:16, CSB + 2048:CSB + 2048 + 512],
                             start=False, stop=True)
                    e.matmul(out=ps_sc[:, 512:1024], lhsT=sb["selm"][:, m * 128:(m + 1) * 128],
                             rhs=g.pool[0:16, CSB + 2560:CSB + 2560 + 512],
                             start=False, stop=True)
                    tick(e.matmul(out=ps_sc[:, 1024:1025], lhsT=sb["selm"][:, m * 128:(m + 1) * 128],
                                  rhs=g.pool[0:16, CSB + 3072:CSB + 3072 + 1],
                                  start=False, stop=True))
                op("tensor", numm,
                   reads=(pr(YB + ysl, 1025), pr(CSB + 2048, 1025), rr("c_ut"), rr("c_selm")),
                   writes=(PSC(0, 1536),), tag=f"nm{b}_{m}")
                t = m
                sl2 = (t % 2) * 2048
                op("sync", is_dma_op=True, emit=lambda e, b=b, t=t, sl2=sl2: dma(e.dma_start(
                    out=PF(XF3 + sl2, 1024),
                    in_=ap["xres"][(b * T + t * 128):(b * T + (t + 1) * 128), :])),
                   reads=(rr("xres", b * T + t * 128, b * T + (t + 1) * 128),),
                   writes=(prf(XF3 + sl2, 1024),), tag=f"a3x2_{b}_{t}")
                def resid(e, sl2=sl2):
                    e.reciprocal(out=sb["tiny"][:, 16:17], in_=ps_sc[:, 1024:1025])
                    e.drain()
                    tick(e.scalar_tensor_tensor(out=PF(XF3 + sl2, 1024),
                                                in0=ps_sc[:, 0:1024],
                                                scalar=sb["tiny"][:, 16:17],
                                                in1=PF(XF3 + sl2, 1024),
                                                op0=ALU.mult, op1=ALU.add))
                op("vector", resid,
                   reads=(PSC(0, 1536), prf(XF3 + sl2, 1024)),
                   writes=(PSC(0, 1536), prf(XF3 + sl2, 1024), rr("tiny", 16, 17)),
                   tag=f"a3d{b}_{m}")
                op("gpsimd", is_dma_op=True, emit=lambda e, b=b, t=t, sl2=sl2: dma(e.dma_start(
                    out=ap["xres"][(b * T + t * 128):(b * T + (t + 1) * 128), :],
                    in_=PF(XF3 + sl2, 1024))),
                   reads=(prf(XF3 + sl2, 1024),),
                   writes=(rr("xres", b * T + t * 128, b * T + (t + 1) * 128),),
                   tag=f"a3w{b}_{t}")

    if n_phases >= 3:
        l1_attn()

    XF4 = 0
    YS4 = 4096
    XQ8 = 6144
    W4A = 38912
    W4B = 55296
    GT4 = 71680
    UP4 = 73728
    GUF = 75776
    GUB = 79872
    WL4 = 83968
    WB4 = 88064
    W4D = 38912
    OB4 = 71680
    Q8S = 79872
    Q8T = 88064

    def l1_ffn():
        for wi, (wname, bname) in enumerate([("wg1", "wg1_b"), ("wu1", "wu1_b"),
                                             ("wd1", "wd1_b")]):
            nrow = ap[wname].shape[0]
            ncol = ap[wname].shape[1]
            ntile = nrow // 128
            nct = ncol // 1024
            ci = 0
            for it in range(ntile):
                for ct in range(nct):
                    sl = ci % 2
                    op("sync", is_dma_op=True, emit=lambda e, wname=wname, it=it, ct=ct, sl=sl: dma(
                        e.dma_start(out=PF(WL4 + sl * 2048, 1024),
                                    in_=ap[wname][it * 128:(it + 1) * 128, ct * 1024:(ct + 1) * 1024])),
                       reads=(rr(wname),), writes=(prf(WL4 + sl * 2048, 1024),),
                       tag=f"m4l{wi}_{ci}")
                    op("scalar", lambda e, sl=sl, ci=ci, wi=wi: tick(e.activation(
                        out=PF(YS4, 1024), in_=PF(WL4 + sl * 2048, 1024), func=AF.Abs,
                        accum_out=sb["colx"][:, 160 + (ci % 32):161 + (ci % 32)])),
                       reads=(prf(WL4 + sl * 2048, 1024),),
                       writes=(prf(YS4, 1024), rr("colx", 160 + (ci % 32), 161 + (ci % 32))),
                       tag=f"m4a{wi}_{ci}")
                    if ci % 32 == 31:
                        op("vector", lambda e, wi=wi, first=(ci == 31): (
                            e.tensor_reduce(out=sb["tiny"][:, 18:19], in_=sb["colx"][:, 160:192],
                                            axis=mybir.AxisListType.X, op=ALU.add),
                            e.drain(),
                            tick(e.tensor_tensor(out=sb["tiny"][:, 19:20],
                                                 in0=sb["tiny"][:, 18:19],
                                                 in1=sb["tiny"][:, 19:20], op=ALU.add)
                                 if not first else
                                 e.tensor_copy(out=sb["tiny"][:, 19:20], in_=sb["tiny"][:, 18:19]))),
                           reads=(rr("colx", 160, 192), rr("tiny", 19, 20)) if ci != 31 else (rr("colx", 160, 192),),
                           writes=(rr("tiny", 18, 20),), tag=f"m4c{wi}_{ci}")
                    ci += 1
            op("gpsimd", lambda e, wi=wi: tick(e.tensor_reduce(
                out=sb["tcc"][0:1, 5 + wi:6 + wi], in_=sb["tiny"][:, 19:20],
                axis=mybir.AxisListType.C, op=ALU.add)),
               reads=(rr("tiny", 19, 20),), writes=(rr("tcc", 5 + wi, 6 + wi),),
               tag=f"m4g{wi}")
            nelem = float(nrow * ncol)
            op("tensor", lambda e, wi=wi: tick(e.matmul(
                out=ps_dm[:, 0:1], lhsT=sb["onesf"][0:1, :],
                rhs=sb["tcc"][0:1, 5 + wi:6 + wi], start=True, stop=True)),
               reads=(rr("tcc", 5 + wi, 6 + wi), rr("c_onesf")), writes=(PDM(),),
               tag=f"m4b{wi}")
            def mkw(e, wi=wi, nelem=nelem):
                e.tensor_scalar(out=sb["glob"][:, 5 + wi:6 + wi], in0=ps_dm[:, 0:1],
                                scalar1=1.0 / nelem, scalar2=1e-5,
                                op0=ALU.mult, op1=ALU.max)
                e.drain()
                tick(e.reciprocal(out=sb["cols"][:, 40 + wi:41 + wi],
                                  in_=sb["glob"][:, 5 + wi:6 + wi]))
            op("vector", mkw, reads=(PDM(),),
               writes=(PDM(), rr("glob", 5 + wi, 6 + wi), rr("cols", 40 + wi, 41 + wi)),
               tag=f"m4k{wi}")
            ci = 0
            for it in range(ntile):
                for ct in range(nct):
                    sl = ci % 2
                    op("sync", is_dma_op=True, emit=lambda e, wname=wname, it=it, ct=ct, sl=sl: dma(
                        e.dma_start(out=PF(WL4 + sl * 2048, 1024),
                                    in_=ap[wname][it * 128:(it + 1) * 128, ct * 1024:(ct + 1) * 1024])),
                       reads=(rr(wname),), writes=(prf(WL4 + sl * 2048, 1024),),
                       tag=f"t4l{wi}_{ci}")
                    wv = PF(WL4 + sl * 2048, 1024)
                    wvr = prf(WL4 + sl * 2048, 1024)
                    op("vector", lambda e, sl=sl, wi=wi, wv=wv: tick(e.tensor_scalar(
                        out=wv, in0=wv, scalar1=sb["cols"][:, 40 + wi:41 + wi],
                        scalar2=MAGIC, op0=ALU.mult, op1=ALU.add)),
                       reads=(wvr, rr("cols", 40 + wi, 41 + wi)),
                       writes=(wvr,), tag=f"t4ta{wi}_{ci}")
                    op("vector", lambda e, wv=wv: tick(e.tensor_scalar(
                        out=wv, in0=wv, scalar1=MAGIC, scalar2=1.0,
                        op0=ALU.subtract, op1=ALU.min)),
                       reads=(wvr,), writes=(wvr,), tag=f"t4tb{wi}_{ci}")
                    op("vector", lambda e, sl=sl, wv=wv: tick(e.tensor_scalar(
                        out=PB(WB4 + sl * 1024, 1024), in0=wv,
                        scalar1=-1.0, scalar2=None, op0=ALU.max)),
                       reads=(wvr,), writes=(pr(WB4 + sl * 1024, 1024),),
                       tag=f"t4tc{wi}_{ci}")
                    op("gpsimd", is_dma_op=True, emit=lambda e, bname=bname, it=it, ct=ct, sl=sl: dma(
                        e.dma_start(out=ap[bname][it * 128:(it + 1) * 128, ct * 1024:(ct + 1) * 1024],
                                    in_=PB(WB4 + sl * 1024, 1024))),
                       reads=(pr(WB4 + sl * 1024, 1024),),
                       writes=(rr(bname, it * nct + ct, it * nct + ct + 1),),
                       tag=f"t4w{wi}_{ci}")
                    ci += 1

        for t in range(NT):
            sl2 = (t % 2) * 2048
            op("sync", is_dma_op=True, emit=lambda e, t=t, sl2=sl2: dma(e.dma_start(
                out=PF(XF4 + sl2, 1024), in_=ap["xres"][t * 128:(t + 1) * 128, :])),
               reads=(rr("xres", t * 128, (t + 1) * 128),),
               writes=(prf(XF4 + sl2, 1024),), tag=f"q8x{t}")

            def stats(e, t=t, sl2=sl2):
                e.bn_stats(out=sb["tiny"][:, 0:6], in_=PF(XF4 + sl2, 512))
                e.bn_stats(out=sb["tiny"][:, 6:12], in_=PF(XF4 + sl2 + 1024, 512))
                e.drain()
                e.bn_aggr(out=sb["tiny"][:, 12:14], in_=sb["tiny"][:, 0:12])
                e.drain()
                e.tensor_copy(out=sb["colm"][:, t:t + 1], in_=sb["tiny"][:, 12:13])
                tick(e.tensor_scalar(out=sb["colx"][:, 160 + t:161 + t],
                                     in0=sb["tiny"][:, 13:14],
                                     scalar1=EPS, scalar2=None, op0=ALU.add))
            op("vector", stats, reads=(prf(XF4 + sl2, 1024),),
               writes=(rr("tiny", 0, 15), rr("colm", t, t + 1),
                       rr("colx", 160 + t, 161 + t)), tag=f"q8s{t}")
            jx = PF(YS4, 1024) if t % 2 == 0 else PF(UP4, 1024)
            jn = PF(GT4, 1024) if t % 2 == 0 else PF(GUF, 1024)
            jxr = prf(YS4, 1024) if t % 2 == 0 else prf(UP4, 1024)
            jnr = prf(GT4, 1024) if t % 2 == 0 else prf(GUF, 1024)
            op("vector", lambda e, t=t, sl2=sl2, jx=jx: tick(e.tensor_scalar(
                out=jx, in0=PF(XF4 + sl2, 1024),
                scalar1=1.0, scalar2=-1e30, op0=ALU.mult, op1=ALU.max,
                accum_out=sb["dcol"][:, t:t + 1])),
               reads=(prf(XF4 + sl2, 1024),),
               writes=(jxr, rr("dcol", t, t + 1)), tag=f"q8gx{t}")
            op("vector", lambda e, t=t, sl2=sl2, jn=jn: tick(e.tensor_scalar(
                out=jn, in0=PF(XF4 + sl2, 1024),
                scalar1=1.0, scalar2=1e30, op0=ALU.mult, op1=ALU.min,
                accum_out=sb["dcol"][:, 32 + t:33 + t])),
               reads=(prf(XF4 + sl2, 1024),),
               writes=(jnr, rr("dcol", 32 + t, 33 + t)), tag=f"q8gn{t}")
        op("scalar", lambda e: (
            e.activation(out=sb["colx"][:, 160:160 + NT],
                         in_=sb["colx"][:, 160:160 + NT], func=AF.Ln),
            e.drain(),
            tick(e.activation(out=sb["colr"][:, 0:NT], in_=sb["colx"][:, 160:160 + NT],
                              func=AF.Exp, scale=-0.5))),
           reads=(rr("colx", 160, 160 + NT),),
           writes=(rr("colx", 160, 160 + NT), rr("colr", 0, NT)), tag="q8rb")
        def q8cols(e):
            e.tensor_tensor(out=sb["dcol"][:, 64:96], in0=sb["dcol"][:, 0:32],
                            in1=sb["colm"][:, 0:NT], op=ALU.subtract)
            e.scalar_tensor_tensor(out=sb["dcol"][:, 96:128], in0=sb["dcol"][:, 32:64],
                                   scalar=-1.0, in1=sb["colm"][:, 0:NT],
                                   op0=ALU.mult, op1=ALU.add)
            e.drain()
            e.tensor_tensor(out=sb["dcol"][:, 64:96], in0=sb["dcol"][:, 64:96],
                            in1=sb["dcol"][:, 96:128], op=ALU.max)
            e.drain()
            e.tensor_tensor(out=sb["dcol"][:, 64:96], in0=sb["dcol"][:, 64:96],
                            in1=sb["colr"][:, 0:NT], op=ALU.mult)
            e.drain()
            e.tensor_scalar(out=sb["colx"][:, 0:32], in0=sb["dcol"][:, 64:96],
                            scalar1=1e-5, scalar2=None, op0=ALU.max)
            e.drain()
            e.reciprocal(out=sb["dcol"][:, 96:128], in_=sb["colx"][:, 0:32])
            e.drain()
            tick(e.scalar_tensor_tensor(out=sb["dcol"][:, 96:128],
                                        in0=sb["dcol"][:, 96:128], scalar=127.0,
                                        in1=sb["colr"][:, 0:NT],
                                        op0=ALU.mult, op1=ALU.mult))
        op("vector", q8cols, reads=(rr("dcol", 0, 64), rr("colm", 0, NT),
                                    rr("colr", 0, NT)),
           writes=(rr("dcol", 64, 128), rr("colx", 0, 32)), tag="q8cols")

        def q8_ldx(t):
            sl2 = (t % 2) * 2048
            op("sync", is_dma_op=True, emit=lambda e, t=t, sl2=sl2: dma(e.dma_start(
                out=PF(XF4 + sl2, 1024), in_=ap["xres"][t * 128:(t + 1) * 128, :])),
               reads=(rr("xres", t * 128, (t + 1) * 128),),
               writes=(prf(XF4 + sl2, 1024),), tag=f"q8xb{t}")

        def q8_step(t, k):
            sl2 = (t % 2) * 2048
            xt = PF(XF4 + sl2, 1024)
            yv = PF(YS4, 1024) if t % 2 == 0 else PF(GT4, 1024)
            ybase = ("pool", YS4, YS4 + 2048) if t % 2 == 0 else ("pool", GT4, GT4 + 2048)
            if k == 0:
                op("vector", lambda e, t=t, xt=xt, yv=yv: tick(e.scalar_tensor_tensor(
                    out=yv, in0=xt, scalar=sb["colm"][:, t:t + 1],
                    in1=sb["dcol"][:, 96 + t:97 + t].broadcast_to([128, 1024]),
                    op0=ALU.subtract, op1=ALU.mult)),
                   reads=(prf(XF4 + sl2, 1024), rr("colm", t, t + 1),
                          rr("dcol", 96 + t, 97 + t)),
                   writes=(ybase,), tag=f"q8qa{t}")
            elif k == 1:
                op("vector", lambda e, yv=yv: tick(e.tensor_scalar(
                    out=yv, in0=yv, scalar1=MAGIC, scalar2=MAGIC,
                    op0=ALU.add, op1=ALU.subtract)),
                   reads=(ybase,), writes=(ybase,), tag=f"q8qb{t}")
            else:
                op("vector", lambda e, t=t, yv=yv: tick(e.tensor_scalar(
                    out=PB(WB4 + (t % 2) * 1024, 1024), in0=yv,
                    scalar1=127.0, scalar2=-128.0, op0=ALU.min, op1=ALU.max)),
                   reads=(ybase,), writes=(pr(WB4 + (t % 2) * 1024, 1024),),
                   tag=f"q8qc{t}")

        def q8_tev(t):
            def xqT(e, t=t):
                for dc in range(8):
                    i = e.transpose(out=pt_op[:, (t % 2) * 1024 + dc * 128:(t % 2) * 1024 + (dc + 1) * 128],
                                    in_=PB(WB4 + (t % 2) * 1024 + dc * 128, 128),
                                    identity=sb["id"][:, :])
                tick(i)
            op("tensor", xqT, reads=(pr(WB4 + (t % 2) * 1024, 1024), rr("c_id")),
               writes=(POP((t % 2) * 512, 512),), tag=f"q8T{t}")
            op("vector", lambda e, t=t: tick(e.tensor_copy(
                out=PB(XQ8, 32768).rearrange("p (dc n) -> p dc n", n=4096)[:, :, t * 128:(t + 1) * 128],
                in_=pt_op[:, (t % 2) * 1024:(t % 2) * 1024 + 1024].rearrange("p (dc n) -> p dc n", n=128))),
               reads=(POP((t % 2) * 512, 512),),
               writes=(POP((t % 2) * 512, 512),) + tuple(
                   pr(XQ8 + dc * 4096 + t * 128, 128) for dc in range(8)),
               tag=f"q8e{t}")

        for tp in range(NT // 2):
            t0, t1 = 2 * tp, 2 * tp + 1
            q8_ldx(t0)
            q8_ldx(t1)
            for k in range(3):
                q8_step(t0, k)
                q8_step(t1, k)
            q8_tev(t0)
            q8_tev(t1)

        def mkoscale(e):
            e.scalar_tensor_tensor(out=sb["colx"][:, 32:64], in0=sb["colx"][:, 0:32],
                                   scalar=1.0 / 127.0,
                                   in1=sb["glob"][:, 5:6].broadcast_to([128, 32]),
                                   op0=ALU.mult, op1=ALU.mult)
            tick(e.scalar_tensor_tensor(out=sb["colx"][:, 64:96], in0=sb["colx"][:, 0:32],
                                        scalar=1.0 / 127.0,
                                        in1=sb["glob"][:, 6:7].broadcast_to([128, 32]),
                                        op0=ALU.mult, op1=ALU.mult))
        op("vector", mkoscale, reads=(rr("colx", 0, 32), rr("glob", 5, 7)),
           writes=(rr("colx", 32, 96),), tag="q8os")

        for hf in range(2):
            for dc in range(8):
                op("sync", is_dma_op=True, emit=lambda e, dc=dc, hf=hf: dma(
                    e.dma_start_transpose(out=PB(W4A + dc * 2048, 2048),
                                          in_=ap["wg1_b"][hf * 2048:(hf + 1) * 2048, dc * 128:(dc + 1) * 128])),
                   reads=(rr("wg1_b"),), writes=(pr(W4A + dc * 2048, 2048),),
                   tag=f"wgT4{hf}_{dc}")
                op("sync", is_dma_op=True, emit=lambda e, dc=dc, hf=hf: dma(
                    e.dma_start_transpose(out=PB(W4B + dc * 2048, 2048),
                                          in_=ap["wu1_b"][hf * 2048:(hf + 1) * 2048, dc * 128:(dc + 1) * 128])),
                   reads=(rr("wu1_b"),), writes=(pr(W4B + dc * 2048, 2048),),
                   tag=f"wuT4{hf}_{dc}")
            for t in range(NT):
                for ic8 in range(4):
                    psl = ic8 % 2
                    def gmm(e, t=t, ic8=ic8, psl=psl):
                        for dc in range(8):
                            i = e.matmul(out=ps_sc[:, psl * 512:(psl + 1) * 512],
                                         lhsT=PB(XQ8 + dc * 4096 + t * 128, 128),
                                         rhs=PB(W4A + dc * 2048 + ic8 * 512, 512),
                                         start=(dc == 0), stop=(dc == 7))
                        tick(i)
                    op("tensor", gmm, reads=(pr(XQ8, 32768), pr(W4A, 16384)),
                       writes=(PSC(psl * 512, 512),), tag=f"gm4{hf}_{t}_{ic8}")
                    op("scalar", lambda e, t=t, ic8=ic8, psl=psl: tick(e.activation(
                        out=PB(GT4 + ic8 * 512, 512),
                        in_=ps_sc[:, psl * 512:(psl + 1) * 512], func=AF.Silu,
                        scale=sb["colx"][:, 32 + t:33 + t])),
                       reads=(PSC(psl * 512, 512), rr("colx", 32, 64)),
                       writes=(PSC(psl * 512, 512), pr(GT4 + ic8 * 512, 512)),
                       tag=f"ge4{hf}_{t}_{ic8}")
                    def umm(e, t=t, ic8=ic8, psl=psl):
                        for dc in range(8):
                            i = e.matmul(out=ps_sc[:, 1024 + psl * 512:1024 + (psl + 1) * 512],
                                         lhsT=PB(XQ8 + dc * 4096 + t * 128, 128),
                                         rhs=PB(W4B + dc * 2048 + ic8 * 512, 512),
                                         start=(dc == 0), stop=(dc == 7))
                        tick(i)
                    op("tensor", umm, reads=(pr(XQ8, 32768), pr(W4B, 16384)),
                       writes=(PSC(1024 + psl * 512, 512),), tag=f"um4{hf}_{t}_{ic8}")
                    op("vector", lambda e, t=t, ic8=ic8, psl=psl: tick(e.tensor_scalar(
                        out=PB(UP4 + ic8 * 512, 512),
                        in0=ps_sc[:, 1024 + psl * 512:1024 + (psl + 1) * 512],
                        scalar1=sb["colx"][:, 64 + t:65 + t], scalar2=None,
                        op0=ALU.mult)),
                       reads=(PSC(1024 + psl * 512, 512), rr("colx", 64, 96)),
                       writes=(PSC(1024 + psl * 512, 512), pr(UP4 + ic8 * 512, 512)),
                       tag=f"ue4{hf}_{t}_{ic8}")
                gsl = t % 2
                op("vector", lambda e, t=t: tick(e.tensor_tensor(
                    out=PF(GUF, 2048), in0=PB(GT4, 2048),
                    in1=PB(UP4, 2048), op=ALU.mult)),
                   reads=(pr(GT4, 2048), pr(UP4, 2048)),
                   writes=(prf(GUF, 2048),), tag=f"gu4{hf}_{t}")
                mxdst = sb["colx"][:, 96 + t:97 + t] if hf == 0 else sb["dcol"][:, t:t + 1]
                mndst = sb["colx"][:, 128 + t:129 + t] if hf == 0 else sb["dcol"][:, 32 + t:33 + t]
                mxr = rr("colx", 96 + t, 97 + t) if hf == 0 else rr("dcol", t, t + 1)
                mnr = rr("colx", 128 + t, 129 + t) if hf == 0 else rr("dcol", 32 + t, 33 + t)
                op("vector", lambda e, t=t, gsl=gsl, mxdst=mxdst: tick(e.tensor_scalar(
                    out=PB(GUB + gsl * 2048, 2048), in0=PF(GUF, 2048),
                    scalar1=1.0, scalar2=-1e30, op0=ALU.mult, op1=ALU.max,
                    accum_out=mxdst)),
                   reads=(prf(GUF, 2048),),
                   writes=(pr(GUB + gsl * 2048, 2048), mxr), tag=f"gux{hf}_{t}")
                op("vector", lambda e, t=t, mndst=mndst: tick(e.tensor_scalar(
                    out=PB(YS4, 2048), in0=PF(GUF, 2048),
                    scalar1=1.0, scalar2=1e30, op0=ALU.mult, op1=ALU.min,
                    accum_out=mndst)),
                   reads=(prf(GUF, 2048),),
                   writes=(pr(YS4, 2048), mnr), tag=f"gun{hf}_{t}")
                op("gpsimd", is_dma_op=True, emit=lambda e, t=t, hf=hf, gsl=gsl: dma(
                    e.dma_start(out=ap["gu_sc"][t, :, hf * 2048:(hf + 1) * 2048],
                                in_=PB(GUB + gsl * 2048, 2048))),
                   reads=(pr(GUB + gsl * 2048, 2048),),
                   writes=(rr("gu_sc", t * 2 + hf, t * 2 + hf + 1),), tag=f"gs4{hf}_{t}")

        def bfold(e):
            e.tensor_tensor(out=sb["colx"][:, 96:128], in0=sb["colx"][:, 96:128],
                            in1=sb["dcol"][:, 0:32], op=ALU.max)
            e.tensor_tensor(out=sb["colx"][:, 128:160], in0=sb["colx"][:, 128:160],
                            in1=sb["dcol"][:, 32:64], op=ALU.min)
            e.drain()
            e.scalar_tensor_tensor(out=sb["colx"][:, 96:128], in0=sb["colx"][:, 128:160],
                                   scalar=-1.0, in1=sb["colx"][:, 96:128],
                                   op0=ALU.mult, op1=ALU.max)
            e.drain()
            e.tensor_scalar(out=sb["colx"][:, 96:128], in0=sb["colx"][:, 96:128],
                            scalar1=1e-5, scalar2=None, op0=ALU.max)
            e.drain()
            e.reciprocal(out=sb["dcol"][:, 64:96], in_=sb["colx"][:, 96:128])
            e.scalar_tensor_tensor(out=sb["dcol"][:, 96:128], in0=sb["colx"][:, 96:128],
                                   scalar=1.0 / 127.0,
                                   in1=sb["glob"][:, 7:8].broadcast_to([128, 32]),
                                   op0=ALU.mult, op1=ALU.mult)
            e.drain()
            tick(e.tensor_scalar(out=sb["dcol"][:, 64:96], in0=sb["dcol"][:, 64:96],
                                 scalar1=127.0, scalar2=None, op0=ALU.mult))
        op("vector", bfold, reads=(rr("colx", 96, 160), rr("dcol", 0, 64),
                                   rr("glob", 7, 8)),
           writes=(rr("colx", 96, 160), rr("dcol", 64, 128)), tag="bfold")

        for ic in range(I1 // 128):
            op("sync", is_dma_op=True, emit=lambda e, ic=ic: dma(
                e.dma_start_transpose(out=PB(W4D + ic * 1024, 1024),
                                      in_=ap["wd1_b"][:, ic * 128:(ic + 1) * 128])),
               reads=(rr("wd1_b"),), writes=(pr(W4D + ic * 1024, 1024),), tag=f"wdT4{ic}")
        for t in range(NT):
            gsl = t % 2
            op("sync", is_dma_op=True, emit=lambda e, t=t, gsl=gsl: dma(e.dma_start(
                out=PB(OB4 + gsl * 4096, 4096), in_=ap["gu_sc"][t])),
               reads=(rr("gu_sc", t * 2, t * 2 + 2),),
               writes=(pr(OB4 + gsl * 4096, 4096),), tag=f"gl4{t}")

            for k in range(3):
                for hh in range(2):
                    qv = PF(XQ8 + hh * 4096, 2048)
                    qvr = prf(XQ8 + hh * 4096, 2048)
                    if k == 0:
                        op("vector", lambda e, t=t, gsl=gsl, hh=hh, qv=qv: tick(
                            e.tensor_scalar(
                                out=qv, in0=PB(OB4 + gsl * 4096 + hh * 2048, 2048),
                                scalar1=sb["dcol"][:, 64 + t:65 + t], scalar2=MAGIC,
                                op0=ALU.mult, op1=ALU.add)),
                           reads=(pr(OB4 + gsl * 4096 + hh * 2048, 2048),
                                  rr("dcol", 64 + t, 65 + t)),
                           writes=(qvr,), tag=f"q24a{t}_{hh}")
                    elif k == 1:
                        op("vector", lambda e, qv=qv: tick(e.tensor_scalar(
                            out=qv, in0=qv, scalar1=MAGIC, scalar2=127.0,
                            op0=ALU.subtract, op1=ALU.min)),
                           reads=(qvr,), writes=(qvr,), tag=f"q24b{t}_{hh}")
                    else:
                        op("vector", lambda e, hh=hh, qv=qv: tick(e.tensor_scalar(
                            out=PB(Q8T + hh * 2048, 2048), in0=qv,
                            scalar1=-128.0, scalar2=None, op0=ALU.max)),
                           reads=(qvr,), writes=(pr(Q8T + hh * 2048, 2048),),
                           tag=f"q24c{t}_{hh}")
            for half in range(2):
                def trh(e, t=t, half=half):
                    for c in range(16):
                        i = e.transpose(out=pt_op[:, c * 128:(c + 1) * 128],
                                        in_=PB(Q8T + (half * 16 + c) * 128, 128),
                                        identity=sb["id"][:, :])
                    tick(i)
                op("tensor", trh, reads=(pr(Q8T, 4096), rr("c_id")),
                   writes=(POP(0, 1024),), tag=f"q2T{t}_{half}")
                op("vector", lambda e, t=t, half=half, gsl=gsl: tick(e.tensor_copy(
                    out=PB(Q8S + gsl * 4096 + half * 2048, 2048),
                    in_=pt_op[:, 0:2048])),
                   reads=(POP(0, 1024),),
                   writes=(POP(0, 1024), pr(Q8S + gsl * 4096 + half * 2048, 2048)),
                   tag=f"q2E{t}_{half}")

            def dmm4(e, t=t, gsl=gsl):
                for dd in range(2):
                    for ic in range(32):
                        i = e.matmul(out=ps_sc[:, dd * 512:(dd + 1) * 512],
                                     lhsT=PB(Q8S + gsl * 4096 + ic * 128, 128),
                                     rhs=PB(W4D + ic * 1024 + dd * 512, 512),
                                     start=(ic == 0), stop=(ic == 31))
                    tick(i)
            op("tensor", dmm4, reads=(pr(Q8S + gsl * 4096, 4096), pr(W4D, 32768)),
               writes=(PSC(0, 1024),), tag=f"dm4{t}")
            sl2 = (t % 2) * 2048
            op("sync", is_dma_op=True, emit=lambda e, t=t, sl2=sl2: dma(e.dma_start(
                out=PF(XF4 + sl2, 1024), in_=ap["xres"][t * 128:(t + 1) * 128, :])),
               reads=(rr("xres", t * 128, (t + 1) * 128), prf(XF4, 4096)),
               writes=(prf(XF4 + sl2, 1024),), tag=f"q8x2_{t}")
            op("vector", lambda e, t=t, sl2=sl2: tick(e.scalar_tensor_tensor(
                out=PF(XF4 + sl2, 1024), in0=ps_sc[:, 0:1024],
                scalar=sb["dcol"][:, 96 + t:97 + t], in1=PF(XF4 + sl2, 1024),
                op0=ALU.mult, op1=ALU.add)),
               reads=(PSC(0, 1024), prf(XF4 + sl2, 1024), rr("dcol", 96 + t, 97 + t)),
               writes=(PSC(0, 1024), prf(XF4 + sl2, 1024)),
               tag=f"dr4{t}")
            op("gpsimd", is_dma_op=True, emit=lambda e, t=t, sl2=sl2: dma(e.dma_start(
                out=ap["out"].rearrange("b tt d -> (b tt) d")[t * 128:(t + 1) * 128, :],
                in_=PF(XF4 + sl2, 1024))),
               reads=(prf(XF4 + sl2, 1024),),
               writes=(rr("OUT", t * 128, (t + 1) * 128),), tag=f"ow4{t}")

    if n_phases >= 4:
        l1_ffn()

    if n_phases < 4:
        op("sync", is_dma_op=True, emit=lambda e: dma(e.dma_start(
            out=ap["out"].rearrange("b t d -> (b t) d"), in_=ap["xres"])),
           reads=(rr("xres"),), writes=(rr("OUT"),), tag="out")
    op("sync", emit=lambda e: None, reads=(rr("OUT"),), writes=(), tag="fin")


def _consts():
    idb = np.eye(128).astype(BF)
    idf = np.eye(128).astype(np.float32)
    ones = np.ones((128, 128), BF)
    ut = np.triu(np.ones((128, 128))).astype(BF)
    t16 = (np.arange(16)[:, None] < np.arange(16)[None, :]).astype(BF)
    ind = np.zeros((128, 256), BF)
    for m in range(16):
        ind[:, m * 16 + m] = 1
    selm = np.zeros((16, 2048), BF)
    for m in range(16):
        selm[m, m * 128:(m + 1) * 128] = 1
    return dict(id_bf=idb, id_f=idf, ones_bf=ones, ut128=ut, t16c=t16, ind16=ind,
                ones_f=np.ones((1, 128), np.float32), selm16=selm)


def kernel(L0_attn_x, L0_attn_w_q, L0_attn_w_kv, L0_attn_w_out, L0_attn_b_out,
           L0_ffn_w_gate, L0_ffn_w_up, L0_ffn_w_down, L1_attn_W1, L1_attn_b1_,
           L1_attn_W2, L1_attn_b2_, L1_ffn_w_gate, L1_ffn_w_up, L1_ffn_w_down):
    x = np.ascontiguousarray(L0_attn_x)
    shared = dict(
        wq=np.ascontiguousarray(L0_attn_w_q), wkv=np.ascontiguousarray(L0_attn_w_kv),
        wout=np.ascontiguousarray(L0_attn_w_out),
        bout=np.ascontiguousarray(L0_attn_b_out).reshape(1, D),
        wg0=np.ascontiguousarray(L0_ffn_w_gate), wu0=np.ascontiguousarray(L0_ffn_w_up),
        wd0=np.ascontiguousarray(L0_ffn_w_down), w1=np.ascontiguousarray(L1_attn_W1),
        b1=np.ascontiguousarray(L1_attn_b1_).reshape(1, D),
        w2=np.ascontiguousarray(L1_attn_W2).reshape(1, D),
        b2=np.ascontiguousarray(L1_attn_b2_).reshape(1, 1),
        wg1=np.ascontiguousarray(L1_ffn_w_gate), wu1=np.ascontiguousarray(L1_ffn_w_up),
        wd1=np.ascontiguousarray(L1_ffn_w_down), **_consts())
    nc = build_nc(N_PHASES)
    in_maps = []
    for ci in range(8):
        m = dict(shared)
        m["x_in"] = x[2 * ci:2 * ci + 2]
        in_maps.append(m)
    res = run_bass_kernel_spmd(nc, in_maps, core_ids=list(range(8)))
    return np.concatenate([r["out"] for r in res.results], axis=0)
